# version 03
# baseline (speedup 1.0000x reference)
import math
from contextlib import ExitStack

import numpy as np
import ml_dtypes

import concourse.bass as bass
import concourse.mybir as mybir
from concourse.bass_utils import run_bass_kernel_spmd

F32 = mybir.dt.float32
BF16 = mybir.dt.bfloat16
I32 = mybir.dt.int32
AF = mybir.ActivationFunctionType
ALU = mybir.AluOpType
AX = mybir.AxisListType
BF = ml_dtypes.bfloat16

T = 4096
SEQ = 2048
D = 1024
NEG = -3.4028234663852886e+38
NT = T // 128

_CONST_CACHE = None

def _host_consts():
    global _CONST_CACHE
    if _CONST_CACHE is not None:
        return _CONST_CACHE
    c = {}
    dh, ws, jlen, SCALE_BASE = 64, 128, 384, 64
    inv_freq = 1.0 / 10000.0 ** (np.arange(0, dh, 2, dtype=np.float64) / dh)
    t = np.arange(jlen, dtype=np.float64)
    half = t[:, None] * inv_freq[None, :]
    freqs = np.concatenate([half, half], -1)
    power = (t - jlen // 2) / SCALE_BASE
    sbase = (np.arange(0, dh, 2, dtype=np.float64) + 0.4 * dh) / (1.4 * dh)
    scale = sbase[None, :] ** power[:, None]
    scale = np.concatenate([scale, scale], -1)
    qf, qs = freqs[-ws:], scale[-ws:]
    sgn = np.where(np.arange(dh) < dh // 2, -1.0, 1.0)
    s05 = dh ** -0.5
    CQ = (np.cos(qf) * qs * s05).T
    SQ = (np.sin(qf) * qs * s05).T * sgn[:, None]
    inv_s = 1.0 / scale
    CK = (np.cos(freqs) * inv_s).T
    SK = (np.sin(freqs) * inv_s).T * sgn[:, None]
    c["l0_cq"] = np.concatenate([CQ, CQ], 0).astype(np.float32)
    c["l0_sq"] = np.concatenate([SQ, SQ], 0).astype(np.float32)
    c["l0_ck"] = np.concatenate([CK, CK], 0).astype(np.float32)
    c["l0_sk"] = np.concatenate([SK, SK], 0).astype(np.float32)
    i = np.arange(128)[:, None]
    j = np.arange(384)[None, :]
    band = (j >= i) & (j <= i + 256)
    m = []
    for typ in range(3):
        ok = band.copy()
        if typ == 0:
            ok &= j >= 128
        if typ == 2:
            ok &= j < 256
        m.append(np.where(ok, 0.0, NEG).astype(np.float32))
    c["l0_mask"] = np.concatenate(m, axis=1)
    try:
        import jax
        import jax.numpy as jnp
        with jax.default_device(jax.devices("cpu")[0]):
            pos = jnp.arange(SEQ, dtype=jnp.float32)
            invf = 10000.0 ** (jnp.arange(64, dtype=jnp.float32) / 64)
            sinu = pos[:, None] * invf[None, :]
            sin1 = np.asarray(jnp.sin(sinu), dtype=np.float32)
            cos1 = np.asarray(jnp.cos(sinu), dtype=np.float32)
    except Exception:
        pos = np.arange(SEQ, dtype=np.float32)
        invf = (np.float32(10000.0) ** (np.arange(64, dtype=np.float32) / np.float32(64))).astype(np.float32)
        sinu = (pos[:, None] * invf[None, :]).astype(np.float32)
        sin1 = np.sin(sinu).astype(np.float32)
        cos1 = np.cos(sinu).astype(np.float32)
    sinT = sin1.T
    cosT = cos1.T
    c["l1_sin"] = np.ascontiguousarray(np.concatenate([-sinT, sinT], 0))
    c["l1_cos"] = np.ascontiguousarray(np.concatenate([cosT, cosT], 0))
    c["bits"] = ((np.arange(128)[:, None] >> np.arange(7)[None, :]) & 1).astype(np.int32)
    c["ident"] = np.eye(128, dtype=np.float32)
    _CONST_CACHE = c
    return c

class Sem:
    def __init__(self, ctx, name):
        self.h = ctx.es.enter_context(ctx.nc.semaphore(name))
        self.n = 0

    def inc(self, instr, k=1):
        instr.then_inc(self.h, k)
        self.n += k
        return self.n

    def wait(self, eng, val=None):
        eng.wait_ge(self.h, self.n if val is None else val)


class Ctx:
    def __init__(self, nc, es, dbg=False):
        self.nc = nc
        self.es = es
        self.dbg = dbg
        self.h = {}
        self.sb = {}
        self.s_ld = Sem(self, "s_ld")
        self.s_cast = Sem(self, "s_cast")
        self.s_st = Sem(self, "s_st")
        self.s_mm = Sem(self, "s_mm")
        self.s_v = Sem(self, "s_v")
        self.s_sc = Sem(self, "s_sc")
        self.s_g = Sem(self, "s_g")
        self.s_ldr = [Sem(self, "s_ldr0"), Sem(self, "s_ldr1")]

    def sbuf(self, name, shape, dtype):
        return self.es.enter_context(self.nc.sbuf_tensor(name, shape, dtype))

    def psum(self, name, shape, dtype=F32):
        return self.es.enter_context(self.nc.psum_tensor(name, shape, dtype))

    def barrier(self):
        self.nc.all_engine_barrier()


def _dram(nc, name, shape, dtype, kind=None):
    if kind is None:
        return nc.dram_tensor(name, shape, dtype)
    return nc.dram_tensor(name, shape, dtype, kind=kind)


STAGE_SPECS = [
    ("st_wqkv", "L0_attn_w_qkv", (1024, 1024)),
    ("st_wout", "L0_attn_w_out", (1024, 512)),
    ("st_uvw", "L1_attn_uv_w", (4224, 1024)),
    ("st_ow", "L1_attn_o_w", (1024, 2048)),
    ("st_gw", "L1_ffn_gate_w", (32, 1024)),
    ("st_eg", "L1_ffn_expert_gate", (32 * 512, 1024)),
    ("st_eu", "L1_ffn_expert_up", (32 * 512, 1024)),
    ("st_ed", "L1_ffn_expert_down", (32 * 1024, 512)),
    ("st_shg", "L1_ffn_shared_gate", (1024, 1024)),
    ("st_shu", "L1_ffn_shared_up", (1024, 1024)),
    ("st_shd", "L1_ffn_shared_down", (1024, 1024)),
]


def phase_stage(ctx):
    nc = ctx.nc
    for dst, src, shape in STAGE_SPECS:
        d = ctx.h[dst]
        s = ctx.h[src]
        rows, cols = shape
        rchunk = max(1, min(rows, (2 << 20) // cols))
        r = 0
        while r < rows:
            n = min(rchunk, rows - r)
            ctx.s_cast.inc(nc.gpsimd.dma_start(
                out=bass.AP(d, r * cols, [[cols, n], [1, cols]]),
                in_=bass.AP(s, r * cols, [[cols, n], [1, cols]])), 16)
            r += n
    d = ctx.h["st_wqkv2"]
    s = ctx.h["L0_attn_w_qkv"]
    ctx.s_cast.inc(nc.gpsimd.dma_start(
        out=bass.AP(d, 0, [[64 * 1024, 8], [1024, 32], [1, 1024]]),
        in_=bass.AP(s, 32 * 1024, [[64 * 1024, 8], [1024, 32], [1, 1024]])), 16)
    ctx.s_cast.inc(nc.gpsimd.dma_start(
        out=bass.AP(d, 32 * 1024, [[64 * 1024, 8], [1024, 32], [1, 1024]]),
        in_=bass.AP(s, 0, [[64 * 1024, 8], [1024, 32], [1, 1024]])), 16)
    d2 = ctx.h["st_uvw2"]
    s2 = ctx.h["L1_attn_uv_w"]
    ctx.s_cast.inc(nc.gpsimd.dma_start(
        out=bass.AP(d2, 0, [[1024, 64], [1, 1024]]),
        in_=bass.AP(s2, (4096 + 64) * 1024, [[1024, 64], [1, 1024]])), 16)
    ctx.s_cast.inc(nc.gpsimd.dma_start(
        out=bass.AP(d2, 64 * 1024, [[1024, 64], [1, 1024]]),
        in_=bass.AP(s2, 4096 * 1024, [[1024, 64], [1, 1024]])), 16)
    nc.gpsimd.wait_ge(ctx.s_cast.h, ctx.s_cast.n)


def phase_ln(ctx, src_ap, dst_bf, dst_f32=None, inner_bf=None, tag=""):
    nc = ctx.nc
    v, sc, g, sy = nc.vector, nc.scalar, nc.gpsimd, nc.sync
    sx = [ctx.sbuf(f"ln_x{tag}{i}", [128, D], F32) for i in range(2)]
    xn32 = ctx.sbuf(f"ln_xn32{tag}", [128, D], F32)
    xnbf = ctx.sbuf(f"ln_xnbf{tag}", [128, D], BF16)
    xabf = ctx.sbuf(f"ln_xabf{tag}", [128, D], BF16)
    scr = ctx.sbuf(f"ln_scr{tag}", [128, D], F32)
    st = ctx.sbuf(f"ln_st{tag}", [128, 8], F32)
    v.memset(st[:, 7:8], 1e-5)
    def norm_tile(buf, out_tile):
        v.reduce_sum(st[:, 0:1], buf, AX.X, negate=True)
        ctx.s_v.inc(v.memset(st[:, 1:2], 0.0))
        ctx.s_v.wait(sc)
        sc.activation(scr[:], buf, AF.Square, accum_out=st[:, 1:2])
        ctx.s_sc.inc(sc.mul(st[:, 2:3], st[:, 0:1], 1.0 / D))
        ctx.s_sc.wait(sc)
        ctx.s_sc.inc(sc.activation(st[:, 3:4], st[:, 2:3], AF.Square))
        ctx.s_sc.wait(v)
        ctx.s_v.inc(v.tensor_scalar(st[:, 4:5], st[:, 1:2], 1.0 / D, st[:, 3:4],
                                    ALU.mult, ALU.subtract))
        ctx.s_v.wait(sc)
        ctx.s_sc.inc(sc.activation(st[:, 5:6], st[:, 4:5], AF.Sqrt, bias=st[:, 7:8]))
        ctx.s_sc.wait(v)
        ctx.s_v.inc(v.reciprocal(st[:, 6:7], st[:, 5:6]))
        ctx.s_v.wait(v)
        ctx.s_v.inc(v.scalar_tensor_tensor(out_tile, buf, st[:, 2:3],
                                           st[:, 6:7].broadcast_to([128, D]),
                                           ALU.add, ALU.mult))

    ld_done = {}
    for t in range(NT):
        buf = sx[t % 2]
        if t == 0:
            ld_done[0] = ctx.s_ldr[0].inc(sy.dma_start(out=sx[0][:], in_=src_ap[0:128, :]), 16)
        if t + 1 < NT:
            sl = (t + 1) % 2
            ld_done[t + 1] = ctx.s_ldr[sl].inc(
                sy.dma_start(out=sx[sl][:], in_=src_ap[(t + 1) * 128:(t + 2) * 128, :]), 16)
        v.wait_ge(ctx.s_ldr[t % 2].h, ld_done[t])
        norm_tile(buf[:], xn32[:])
        ctx.s_v.inc(v.tensor_copy(xnbf[:], xn32[:]))
        ctx.s_v.wait(g)
        ctx.s_st.inc(g.dma_start(out=dst_bf.ap()[t * 128:(t + 1) * 128, :], in_=xnbf[:]), 16)
        if dst_f32 is not None:
            ctx.s_st.inc(g.dma_start(out=dst_f32.ap()[t * 128:(t + 1) * 128, :], in_=xn32[:]), 16)
        if inner_bf is not None:
            norm_tile(xn32[:], xabf[:])
            ctx.s_v.wait(g)
            ctx.s_st.inc(g.dma_start(out=inner_bf.ap()[t * 128:(t + 1) * 128, :], in_=xabf[:]), 16)
        ctx.s_st.wait(sy)
    nc.gpsimd.wait_ge(ctx.s_st.h, ctx.s_st.n)


WEIGHT_NAMES = [
    "L0_attn_ln_g", "L0_attn_ln_b", "L0_attn_w_qkv", "L0_attn_w_out",
    "L0_ffn_router_w", "L0_ffn_router_b", "L0_ffn_sw1", "L0_ffn_sb1",
    "L0_ffn_sw2", "L0_ffn_sb2", "L0_ffn_rw1", "L0_ffn_rb1", "L0_ffn_rw2", "L0_ffn_rb2",
    "L1_attn_ln_w", "L1_attn_ln_b", "L1_attn_uv_w", "L1_attn_uv_b",
    "L1_attn_gamma", "L1_attn_beta", "L1_attn_w", "L1_attn_o_w", "L1_attn_o_b",
    "L1_ffn_gate_w", "L1_ffn_gate_bias", "L1_ffn_expert_gate", "L1_ffn_expert_up",
    "L1_ffn_expert_down", "L1_ffn_shared_gate", "L1_ffn_shared_up", "L1_ffn_shared_down",
]
INPUT_SHAPES = {
    "L0_attn_ln_g": (D,), "L0_attn_ln_b": (D,),
    "L0_attn_w_qkv": (1024, 1024), "L0_attn_w_out": (1024, 512),
    "L0_ffn_router_w": (1024, 8), "L0_ffn_router_b": (8,),
    "L0_ffn_sw1": (1024, 2048), "L0_ffn_sb1": (2048,),
    "L0_ffn_sw2": (2048, 1024), "L0_ffn_sb2": (1024,),
    "L0_ffn_rw1": (8, 1024, 2048), "L0_ffn_rb1": (8, 2048),
    "L0_ffn_rw2": (8, 2048, 1024), "L0_ffn_rb2": (8, 1024),
    "L1_attn_ln_w": (D,), "L1_attn_ln_b": (D,),
    "L1_attn_uv_w": (4224, 1024), "L1_attn_uv_b": (4224,),
    "L1_attn_gamma": (2, 128), "L1_attn_beta": (2, 128),
    "L1_attn_w": (1023,), "L1_attn_o_w": (1024, 2048), "L1_attn_o_b": (1024,),
    "L1_ffn_gate_w": (32, 1024), "L1_ffn_gate_bias": (32,),
    "L1_ffn_expert_gate": (32, 512, 1024), "L1_ffn_expert_up": (32, 512, 1024),
    "L1_ffn_expert_down": (32, 1024, 512),
    "L1_ffn_shared_gate": (1024, 1024), "L1_ffn_shared_up": (1024, 1024),
    "L1_ffn_shared_down": (1024, 1024),
}
CONST_SHAPES = {
    "l0_cq": (128, 128), "l0_sq": (128, 128), "l0_ck": (128, 384), "l0_sk": (128, 384),
    "l0_mask": (128, 3 * 384), "l1_sin": (128, SEQ), "l1_cos": (128, SEQ),
    "ident": (128, 128),
}


def declare_tensors(nc, ctx, dbg_taps=()):
    h = ctx.h
    h["x"] = nc.dram_tensor("x", [T, D], F32, kind="ExternalInput")
    h["out"] = nc.dram_tensor("out", [T, D], F32, kind="ExternalOutput")
    for nm in WEIGHT_NAMES:
        shp = list(INPUT_SHAPES[nm])
        if len(shp) == 3:
            shp = [shp[0] * shp[1], shp[2]]
        if len(shp) == 1:
            shp = [1, shp[0]]
        h[nm] = nc.dram_tensor(nm, shp, F32, kind="ExternalInput")
    h["w_rev"] = nc.dram_tensor("w_rev", [1, 1023], F32, kind="ExternalInput")
    for nm, shp in CONST_SHAPES.items():
        h[nm] = nc.dram_tensor(nm, list(shp), F32, kind="ExternalInput")
    h["bits"] = nc.dram_tensor("bits", [128, 7], I32, kind="ExternalInput")

    def scratch(name, shape, dtype):
        kind = "ExternalOutput" if name in dbg_taps else None
        h[name] = _dram(nc, name, list(shape), dtype, kind)

    for nm, src, shape in STAGE_SPECS:
        scratch(nm, shape, BF16)
    scratch("st_wqkv2", (512, 1024), BF16)
    scratch("st_uvw2", (128, 1024), BF16)
    scratch("xn0_bf", (T, D), BF16)
    scratch("xa_bf", (T, D), BF16)
    scratch("x1", (T, D), F32)
    scratch("xn1_bf", (T, D), BF16)
    scratch("xn1_tm", (T, D), F32)
    scratch("xb_bf", (T, D), BF16)
    scratch("uT", (2048, T), BF16)
    scratch("v_tm", (T, 2048), BF16)
    scratch("wext_rev", (1, 4352), F32)
    scratch("dbg0", (16, 128, 512), F32)
    return h


def phase_l0_attn(ctx):
    nc = ctx.nc
    v, sc, g, sy, te = nc.vector, nc.scalar, nc.gpsimd, nc.sync, nc.tensor
    h = ctx.h
    GW = 4352
    qkT = [ctx.sbuf(f"qkT{c}", [128, GW], BF16) for c in range(4)]
    qk2T = [ctx.sbuf(f"qk2T{c}", [128, GW], BF16) for c in range(4)]
    vtm = ctx.sbuf("vtm", [128, 32 * 512], BF16)
    cq = ctx.sbuf("cq", [128, 128], F32)
    sq = ctx.sbuf("sq", [128, 128], F32)
    ck = ctx.sbuf("ck", [128, 384], F32)
    sk = ctx.sbuf("sk", [128, 384], F32)
    msk = ctx.sbuf("msk", [128, 3 * 384], F32)
    identb = ctx.sbuf("identb", [128, 128], BF16)
    gcol = ctx.sbuf("gcol", [128, 8], F32)
    bcol = ctx.sbuf("bcol", [128, 8], F32)
    pan = [ctx.sbuf(f"l0pan{i}", [128, 512], BF16) for i in range(8)]
    wq = ctx.sbuf("wq", [128, 8 * 512], BF16)
    wq2 = ctx.sbuf("wq2", [128, 8 * 512], BF16)
    wv = ctx.sbuf("wv", [128, 8 * 512], BF16)
    wo = ctx.sbuf("wo", [128, 8 * 512], BF16)
    x0t = [ctx.sbuf(f"x0t{i}", [128, D], F32) for i in range(2)]
    x1t = ctx.sbuf("x1t", [128, D], F32)
    tm = ctx.sbuf("tm", [128, 384], F32)
    ssc = ctx.sbuf("ssc", [128, 4], F32)
    attn_bf = ctx.sbuf("attn_bf", [128, 384], BF16)
    e32 = ctx.sbuf("e32", [128, 384], F32)
    attnT_bf = ctx.sbuf("attnT_bf", [128, 384], BF16)
    bq = [ctx.sbuf(f"bq{i}", [128, 128], BF16) for i in range(4)]
    bk = [ctx.sbuf(f"bk{i}", [128, 384], BF16) for i in range(4)]
    rt1 = ctx.sbuf("rt1", [128, 384], F32)
    rt2 = ctx.sbuf("rt2", [128, 384], F32)
    outT_bf = ctx.sbuf("outT_bf", [128, 512], BF16)
    p_sim = [ctx.psum(f"p_sim{i}", [128, 384]) for i in range(2)]
    p_at = ctx.es.enter_context(nc.psum_tensor("p_at", [128, 384], BF16))
    p_outT = ctx.psum("p_outT", [128, 512])
    p_pr = [ctx.psum(f"p_pr{i}", [128, 512]) for i in range(2)]
    dbgt = ctx.sbuf("dbgt", [128, 512], F32)

    def dump(idx, src_ap, width=512):
        if not (ctx.dbg and "attn0" in ctx.dbg):
            return
        v.memset(dbgt[:], 0.0)
        ctx.s_v.inc(v.tensor_copy(dbgt[:, 0:width], src_ap))
        ctx.s_v.wait(g)
        ctx.s_st.inc(g.dma_start(out=h["dbg0"].ap()[idx, :, :], in_=dbgt[:]), 16)
        ctx.s_st.wait(v)

    for dst, src in [(cq, "l0_cq"), (sq, "l0_sq"), (ck, "l0_ck"), (sk, "l0_sk"), (msk, "l0_mask")]:
        ctx.s_ld.inc(sy.dma_start(out=dst[:], in_=h[src].ap()), 16)
    ctx.s_cast.inc(g.dma_start(out=identb[:], in_=h["ident"].ap()), 16)
    for dc in range(8):
        ctx.s_ld.inc(sy.dma_start(out=gcol[:, dc:dc + 1],
                                  in_=bass.AP(h["L0_attn_ln_g"], dc * 128, [[1, 128], [1, 1]])), 16)
        ctx.s_ld.inc(sy.dma_start(out=bcol[:, dc:dc + 1],
                                  in_=bass.AP(h["L0_attn_ln_b"], dc * 128, [[1, 128], [1, 1]])), 16)
    for ic in range(4):
        for dh2 in range(2):
            ctx.s_ld.inc(sy.dma_start_transpose(
                wo[:, (ic * 2 + dh2) * 512:(ic * 2 + dh2 + 1) * 512],
                h["st_wout"].ap()[dh2 * 512:(dh2 + 1) * 512, ic * 128:(ic + 1) * 128]), 16)
    for c in range(4):
        v.memset(qkT[c][:, 0:128], 0.0)
        v.memset(qkT[c][:, GW - 128:GW], 0.0)
        v.memset(qk2T[c][:, 0:128], 0.0)
        v.memset(qk2T[c][:, GW - 128:GW], 0.0)
    ctx.s_v.inc(v.memset(tm[:, 0:1], 0.0))
    ctx.barrier()

    for p in range(8):
        tok0 = p * 512
        ldv = []
        for dc in range(8):
            ldv.append(ctx.s_ld.inc(sy.dma_start_transpose(
                pan[dc][:], h["xa_bf"].ap()[tok0:tok0 + 512, dc * 128:(dc + 1) * 128]), 16))
            ldv.append(ctx.s_ld.inc(sy.dma_start_transpose(
                wq[:, dc * 512:(dc + 1) * 512],
                h["st_wqkv"].ap()[0:512, dc * 128:(dc + 1) * 128]), 16))
            ldv.append(ctx.s_ld.inc(sy.dma_start_transpose(
                wq2[:, dc * 512:(dc + 1) * 512],
                h["st_wqkv2"].ap()[0:512, dc * 128:(dc + 1) * 128]), 16))
            ldv.append(ctx.s_ld.inc(sy.dma_start_transpose(
                wv[:, dc * 512:(dc + 1) * 512],
                h["st_wqkv"].ap()[512:1024, dc * 128:(dc + 1) * 128]), 16))
        v.wait_ge(ctx.s_ld.h, ldv[-1])
        for dc in range(8):
            last = v.tensor_scalar(pan[dc][:], pan[dc][:], gcol[:, dc:dc + 1], bcol[:, dc:dc + 1],
                                   ALU.mult, ALU.add)
        ctx.s_v.inc(last)
        ctx.s_v.wait(te)
        for c in range(4):
            ps = p_pr[c % 2]
            for dc in range(8):
                mm = te.matmul(ps[:], wq[:, dc * 512 + c * 128:dc * 512 + (c + 1) * 128],
                               pan[dc][:], start=(dc == 0), stop=(dc == 7))
            ctx.s_mm.inc(mm)
            ctx.s_mm.wait(v)
            ctx.s_v.inc(v.tensor_copy(qkT[c][:, 128 + tok0:128 + tok0 + 512], ps[:, 0:512]))
            if c >= 1:
                ctx.s_v.wait(te)
        for c in range(4):
            ps = p_pr[c % 2]
            ctx.s_v.wait(te)
            for dc in range(8):
                mm = te.matmul(ps[:], wq2[:, dc * 512 + c * 128:dc * 512 + (c + 1) * 128],
                               pan[dc][:], start=(dc == 0), stop=(dc == 7))
            ctx.s_mm.inc(mm)
            ctx.s_mm.wait(v)
            ctx.s_v.inc(v.tensor_copy(qk2T[c][:, 128 + tok0:128 + tok0 + 512], ps[:, 0:512]))
        for tt in range(4):
            ps = p_pr[tt % 2]
            ctx.s_v.wait(te)
            for dc in range(8):
                mm = te.matmul(ps[:], pan[dc][:, tt * 128:(tt + 1) * 128], wv[:, dc * 512:(dc + 1) * 512],
                               start=(dc == 0), stop=(dc == 7))
            ctx.s_mm.inc(mm)
            ctx.s_mm.wait(v)
            gt = (tok0 // 128) + tt
            ctx.s_v.inc(v.tensor_copy(vtm[:, gt * 512:(gt + 1) * 512], ps[:, 0:512]))
        ctx.s_v.wait(te)
        ctx.s_v.wait(sy)
    ctx.barrier()
    dump(0, qkT[0][:, 128:640])
    dump(1, qk2T[0][:, 128:640])
    dump(2, vtm[:, 0:512])
    ctx.barrier()

    x0ld = []
    for t in range(NT):
        b, w = t // 16, t % 16
        typ = 0 if w == 0 else (2 if w == 15 else 1)
        if t == 0:
            x0ld.append(ctx.s_ldr[0].inc(sy.dma_start(out=x0t[0][:], in_=h["x"].ap()[0:128, :]), 16))
        if t + 1 < NT:
            if t >= 1:
                ctx.s_v.wait(sy)
            x0ld.append(ctx.s_ldr[(t + 1) % 2].inc(
                sy.dma_start(out=x0t[(t + 1) % 2][:], in_=h["x"].ap()[(t + 1) * 128:(t + 2) * 128, :]), 16))
        for hp in range(4):
            qsl = qkT[hp][:, 128 + t * 128:128 + (t + 1) * 128]
            q2sl = qk2T[hp][:, 128 + t * 128:128 + (t + 1) * 128]
            v.tensor_tensor(rt1[:, 0:128], qsl, cq[:], ALU.mult)
            v.tensor_tensor(rt2[:, 0:128], q2sl, sq[:], ALU.mult)
            v.tensor_add(bq[hp][:], rt1[:, 0:128], rt2[:, 0:128])
            ksl = qkT[hp][:, t * 128:t * 128 + 384]
            k2sl = qk2T[hp][:, t * 128:t * 128 + 384]
            v.tensor_tensor(rt1[:], ksl, ck[:], ALU.mult)
            v.tensor_tensor(rt2[:], k2sl, sk[:], ALU.mult)
            last = v.tensor_add(bk[hp][:], rt1[:], rt2[:])
        ctx.s_v.inc(last)
        if t == 0:
            dump(3, bq[0][:], 128)
            dump(4, bk[0][:], 384)
        ctx.s_v.wait(te)
        for hd in range(8):
            hp, half = hd // 2, hd % 2
            ps = p_sim[hd % 2]
            r0 = 64 * half
            ctx.s_mm.inc(te.matmul(ps[:], bq[hp][r0:r0 + 64, :], bk[hp][r0:r0 + 64, :],
                                   start=True, stop=True))
            ctx.s_mm.wait(v)
            if t == 0 and hd == 0:
                dump(5, ps[:, 0:384], 384)
            v.tensor_tensor(tm[:], ps[:], msk[:, typ * 384:(typ + 1) * 384], ALU.add)
            ctx.s_v.inc(v.reduce_max(ssc[:, 0:1], tm[:], AX.X, negate=True))
            ctx.s_v.wait(sc)
            ctx.s_sc.inc(sc.activation(e32[:], tm[:], AF.Exp, bias=ssc[:, 0:1]))
            ctx.s_sc.wait(v)
            ctx.s_v.inc(v.reduce_sum(ssc[:, 1:2], e32[:], AX.X))
            ctx.s_v.wait(v)
            ctx.s_v.inc(v.reciprocal(ssc[:, 2:3], ssc[:, 1:2]))
            ctx.s_v.wait(v)
            ctx.s_v.inc(v.tensor_tensor(attn_bf[:], e32[:],
                                        ssc[:, 2:3].broadcast_to([128, 384]), ALU.mult))
            if t == 0 and hd == 0:
                dump(6, ssc[:, 0:4], 4)
                dump(7, attn_bf[:], 384)
                dump(8, e32[:], 384)
            ctx.s_v.wait(te)
            for kb in range(3):
                tr = te.transpose(p_at[:, kb * 128:(kb + 1) * 128], attn_bf[:, kb * 128:(kb + 1) * 128],
                                  identb[:])
            ctx.s_mm.inc(tr)
            ctx.s_mm.wait(v)
            ctx.s_v.inc(v.tensor_copy(attnT_bf[:], p_at[:]))
            ctx.s_v.wait(te)
            first = True
            for kb in range(3):
                kw = w - 1 + kb
                if kw < 0 or kw >= 16:
                    continue
                kt = b * 16 + kw
                mm = te.matmul(p_outT[r0:r0 + 64, hp * 128:(hp + 1) * 128],
                               vtm[:, kt * 512 + hd * 64:kt * 512 + (hd + 1) * 64],
                               attnT_bf[:, kb * 128:(kb + 1) * 128],
                               start=first, stop=(kb == 2 or (w == 15 and kb == 1)))
                first = False
            ctx.s_mm.inc(mm)
        ctx.s_mm.wait(v)
        ctx.s_v.inc(v.tensor_copy(outT_bf[:], p_outT[:]))
        if t == 0:
            dump(9, outT_bf[:])
        ctx.s_v.wait(te)
        for dh2 in range(2):
            for ic in range(4):
                mm = te.matmul(p_pr[dh2][:], outT_bf[:, ic * 128:(ic + 1) * 128],
                               wo[:, (ic * 2 + dh2) * 512:(ic * 2 + dh2 + 1) * 512],
                               start=(ic == 0), stop=(ic == 3))
        ctx.s_mm.inc(mm)
        v.wait_ge(ctx.s_ldr[t % 2].h, x0ld[t])
        ctx.s_mm.wait(v)
        if t == 0:
            dump(10, p_pr[0][:])
            dump(11, p_pr[1][:])
        v.tensor_add(x1t[:, 0:512], p_pr[0][:], x0t[t % 2][:, 0:512])
        ctx.s_v.inc(v.tensor_add(x1t[:, 512:1024], p_pr[1][:], x0t[t % 2][:, 512:1024]))
        ctx.s_v.wait(g)
        ctx.s_st.inc(g.dma_start(out=h["x1"].ap()[t * 128:(t + 1) * 128, :], in_=x1t[:]), 16)
        ctx.s_st.wait(v)
        ctx.s_v.wait(te)
    nc.gpsimd.wait_ge(ctx.s_st.h, ctx.s_st.n)


def phase_l0_ffn(ctx):
    nc = ctx.nc
    v, sc, g, sy, te = nc.vector, nc.scalar, nc.gpsimd, nc.sync, nc.tensor
    h = ctx.h
    identf = ctx.sbuf("identf", [128, 128], F32)
    pan = [ctx.sbuf(f"f0pan{i}", [128, 512], BF16) for i in range(8)]
    rwbf = ctx.sbuf("rwbf", [128, 8 * 8], BF16)
    rbb = ctx.sbuf("rbb", [128, 8], F32)
    w8 = ctx.sbuf("w8", [128, NT * 8], F32)
    w8T8 = ctx.sbuf("w8T8", [8, T], BF16)
    oh8 = ctx.sbuf("oh8", [8, 8 * 128], BF16)
    lg = ctx.sbuf("lg", [128, 8], F32)
    pe_ = ctx.sbuf("pe_", [128, 8], F32)
    mske = ctx.sbuf("mske", [128, 8], F32)
    mst = ctx.sbuf("mst", [128, 8], F32)
    stt = ctx.sbuf("stt", [128, 4], F32)
    p_r = ctx.psum("p_r", [128, 8])
    p_t8 = ctx.psum("p_t8", [8, 128])
    p_wb = ctx.psum("p_wb", [128, 512])
    p_h = [ctx.psum(f"p_h{i}", [128, 512]) for i in range(2)]
    p_o = [ctx.psum(f"p_o{i}", [128, 512]) for i in range(2)]
    rw1sb = [ctx.sbuf(f"rw1sb{i}", [128, 8 * 2048], BF16) for i in range(2)]
    rw2sb = [ctx.sbuf(f"rw2sb{i}", [128, 16 * 1024], BF16) for i in range(2)]
    h1sb = [ctx.sbuf(f"h1sb{i}", [128, 16 * 512], BF16) for i in range(2)]
    wbsb = [ctx.sbuf(f"wbsb{i}", [128, 512], F32) for i in range(2)]
    g1 = ctx.sbuf("g1", [128, 512], F32)
    rbcol = ctx.sbuf("rbcol", [128, 2 * 16], F32)
    sbcol = ctx.sbuf("sbcol", [128, 16], F32)
    rb2bf = ctx.sbuf("rb2bf", [8, 1024], BF16)
    sb2bf = ctx.sbuf("sb2bf", [1, 1024], BF16)
    onesb = ctx.sbuf("onesb", [1, 128], BF16)
    acc = ctx.sbuf("acc", [128, D], F32)
    dbgt = ctx.sbuf("dbgtf", [128, 512], F32)

    def dump(idx, src_ap, width=512, parts=128):
        if not (ctx.dbg and "ffn0" in ctx.dbg):
            return
        v.memset(dbgt[:], 0.0)
        ctx.s_v.inc(v.tensor_copy(dbgt[0:parts, 0:width], src_ap))
        ctx.s_v.wait(g)
        ctx.s_st.inc(g.dma_start(out=h["dbg0"].ap()[idx, :, :], in_=dbgt[:]), 16)
        ctx.s_st.wait(v)

    ctx.s_ld.inc(sy.dma_start(out=identf[:], in_=h["ident"].ap()), 16)
    ctx.s_ld.inc(sy.dma_start(out=rbb[:], in_=bass.AP(h["L0_ffn_router_b"], 0, [[0, 128], [1, 8]])), 16)
    for dc in range(8):
        ctx.s_cast.inc(g.dma_start(out=rwbf[:, dc * 8:(dc + 1) * 8],
                                   in_=bass.AP(h["L0_ffn_router_w"], dc * 128 * 8, [[8, 128], [1, 8]])), 16)
    ctx.s_cast.inc(g.dma_start(out=rb2bf[:], in_=h["L0_ffn_rb2"].ap()), 16)
    ctx.s_cast.inc(g.dma_start(out=sb2bf[:], in_=h["L0_ffn_sb2"].ap()), 16)
    v.memset(onesb[:], 1.0)
    ioh = ctx.sbuf("ioh", [8, 8 * 128], I32)
    nc.gpsimd.iota(ioh[:], [[1, 8], [0, 128]], base=0, channel_multiplier=-1)
    ctx.barrier()
    v.tensor_single_scalar(oh8[:], ioh[:], 0, ALU.is_equal)
    ctx.s_v.inc(v.memset(stt[:, 0:1], 0.0))
    ctx.barrier()

    for ch in range(8):
        tok0 = ch * 512
        ctx.s_mm.wait(sy)
        ldv = None
        for dc in range(8):
            ldv = ctx.s_ld.inc(sy.dma_start_transpose(
                pan[dc][:], h["xn0_bf"].ap()[tok0:tok0 + 512, dc * 128:(dc + 1) * 128]), 16)
        te.wait_ge(ctx.s_ld.h, ldv)
        for tt in range(4):
            t = ch * 4 + tt
            ctx.s_v.wait(te)
            for dc in range(8):
                mm = te.matmul(p_r[:], pan[dc][:, tt * 128:(tt + 1) * 128], rwbf[:, dc * 8:(dc + 1) * 8],
                               start=(dc == 0), stop=(dc == 7))
            ctx.s_mm.inc(mm)
            ctx.s_mm.wait(v)

            def vs(instr):
                ctx.s_v.inc(instr)
                ctx.s_v.wait(v)

            vs(v.tensor_tensor(lg[:], p_r[:], rbb[:], ALU.add))
            vs(v.reduce_max(stt[:, 0:1], lg[:], AX.X))
            ctx.s_v.inc(v.reduce_max(stt[:, 1:2], lg[:], AX.X, negate=True))
            ctx.s_v.wait(sc)
            ctx.s_sc.inc(sc.activation(pe_[:], lg[:], AF.Exp, bias=stt[:, 1:2]))
            ctx.s_sc.wait(v)
            if t == 1:
                dump(9, lg[:], 8)
            vs(v.reduce_sum(stt[:, 2:3], pe_[:], AX.X))
            vs(v.reciprocal(stt[:, 2:3], stt[:, 2:3]))
            vs(v.tensor_tensor(mst[:], lg[:], stt[:, 0:1].broadcast_to([128, 8]), ALU.is_ge))
            if t == 1:
                dump(10, stt[:, 0:4], 4)
                dump(12, mst[:], 8)
            vs(v.scalar_tensor_tensor(mske[:], mst[:], -1e30, lg[:], ALU.mult, ALU.add))
            if t == 1:
                dump(14, mske[:], 8)
            vs(v.reduce_max(stt[:, 3:4], mske[:], AX.X))
            vs(v.tensor_tensor(mske[:], lg[:], stt[:, 3:4].broadcast_to([128, 8]), ALU.is_ge))
            vs(v.tensor_mul(mst[:], pe_[:], mske[:]))
            ctx.s_v.inc(v.tensor_tensor(w8[:, t * 8:(t + 1) * 8], mst[:],
                                        stt[:, 2:3].broadcast_to([128, 8]), ALU.mult))
            ctx.s_v.wait(te)
            ctx.s_mm.inc(te.transpose(p_t8[:], w8[:, t * 8:(t + 1) * 8], identf[:]))
            ctx.s_mm.wait(v)
            ctx.s_v.inc(v.tensor_copy(w8T8[:, t * 128:(t + 1) * 128], p_t8[:]))
            if t == 1:
                dump(6, lg[:], 8)
                dump(7, pe_[:], 8)
                dump(8, mske[:], 8)
                dump(11, mst[:], 8)
    ctx.barrier()
    dump(0, w8[:, 0:32], 32)
    dump(1, w8T8[0:8, 0:512], 512, 8)
    ctx.barrier()

    def ffn_pass(pair_idx, experts, hdim, w1_src, w1_rows, b1_src, w2_src, w2_rows, b2=False, wb=True):
        nht = hdim // 128
        for ei, (slot, _e) in enumerate(experts):
            for dc in range(8):
                ctx.s_cast.inc(g.dma_start(
                    out=rw1sb[slot][:, dc * hdim:(dc + 1) * hdim],
                    in_=bass.AP(w1_src, (w1_rows[ei] + dc * 128) * hdim, [[hdim, 128], [1, hdim]])), 16)
            for hc in range(nht):
                ctx.s_cast.inc(g.dma_start(
                    out=rw2sb[slot][:, hc * 1024:(hc + 1) * 1024],
                    in_=bass.AP(w2_src, (w2_rows[ei] + hc * 128) * 1024, [[1024, 128], [1, 1024]])), 16)
            for hc in range(nht):
                if b1_src is not None:
                    boff = (0 if _e is None else _e) * hdim + hc * 128
                    ctx.s_ld.inc(sy.dma_start(
                        out=rbcol[:, ei * 16 + hc:ei * 16 + hc + 1],
                        in_=bass.AP(b1_src, boff, [[1, 128], [1, 1]])), 16)
        g.wait_ge(ctx.s_cast.h, ctx.s_cast.n)
        sy.wait_ge(ctx.s_ld.h, ctx.s_ld.n)
        ctx.barrier()
        for ch in range(8):
            tok0 = ch * 512
            ctx.s_mm.wait(sy)
            ldv = None
            for dc in range(8):
                ldv = ctx.s_ld.inc(sy.dma_start_transpose(
                    pan[dc][:], h["xn0_bf"].ap()[tok0:tok0 + 512, dc * 128:(dc + 1) * 128]), 16)
            te.wait_ge(ctx.s_ld.h, ldv)
            v.wait_ge(ctx.s_ld.h, ldv)
            ctx.s_mm.wait(v)
            if wb:
                for ei, (slot, e) in enumerate(experts):
                    ctx.s_v.wait(te)
                    ctx.s_mm.inc(te.matmul(p_wb[:], oh8[:, e * 128:(e + 1) * 128],
                                           w8T8[:, tok0:tok0 + 512], start=True, stop=True))
                    ctx.s_mm.wait(v)
                    ctx.s_v.inc(v.tensor_copy(wbsb[slot][:], p_wb[:]))
                    if ch == 0 and pair_idx == 0 and ei == 0:
                        dump(2, wbsb[0][:])
            for ei, (slot, e) in enumerate(experts):
                for ht in range(nht):
                    ps = p_h[ht % 2]
                    ctx.s_sc.wait(te)
                    for dc in range(8):
                        mm = te.matmul(ps[:], rw1sb[slot][:, dc * hdim + ht * 128:dc * hdim + (ht + 1) * 128],
                                       pan[dc][:], start=(dc == 0), stop=(dc == 7))
                    ctx.s_mm.inc(mm)
                    ctx.s_mm.wait(sc)
                    ctx.s_sc.inc(sc.activation(g1[:], ps[:], AF.Gelu, bias=rbcol[:, ei * 16 + ht:ei * 16 + ht + 1]))
                    ctx.s_sc.wait(v)
                    if ch == 0 and pair_idx == 0 and ei == 0 and ht == 0:
                        dump(3, g1[:])
                    if wb:
                        ctx.s_v.inc(v.tensor_tensor(h1sb[slot][:, ht * 512:(ht + 1) * 512], g1[:],
                                                    wbsb[slot][:], ALU.mult))
                    else:
                        ctx.s_v.inc(v.tensor_copy(h1sb[slot][:, ht * 512:(ht + 1) * 512], g1[:]))
                    if ch == 0 and pair_idx == 0 and ei == 0 and ht == 0:
                        dump(4, h1sb[0][:, 0:512])
            ctx.s_v.wait(te)
            for tt in range(4):
                po0, po1 = p_o[0], p_o[1]
                if tt >= 1:
                    ctx.s_st.wait(te)
                mms = []
                for ei, (slot, e) in enumerate(experts):
                    for hc in range(nht):
                        lh = h1sb[slot][:, hc * 512 + tt * 128:hc * 512 + (tt + 1) * 128]
                        mms.append((0, lh, rw2sb[slot][:, hc * 1024:hc * 1024 + 512]))
                        mms.append((1, lh, rw2sb[slot][:, hc * 1024 + 512:(hc + 1) * 1024]))
                if b2:
                    tcols = slice(tok0 + tt * 128, tok0 + (tt + 1) * 128)
                    mms.append((0, w8T8[:, tcols], rb2bf[:, 0:512]))
                    mms.append((1, w8T8[:, tcols], rb2bf[:, 512:1024]))
                    mms.append((0, onesb[:, 0:128], sb2bf[:, 0:512]))
                    mms.append((1, onesb[:, 0:128], sb2bf[:, 512:1024]))
                lastof = {0: max(i for i, m in enumerate(mms) if m[0] == 0),
                          1: max(i for i, m in enumerate(mms) if m[0] == 1)}
                seen = set()
                for i, (pi, lh, rh) in enumerate(mms):
                    ps = po0 if pi == 0 else po1
                    mm = te.matmul(ps[:], lh, rh, start=(pi not in seen), stop=(i == lastof[pi]))
                    seen.add(pi)
                ctx.s_mm.inc(mm)
                ctx.s_mm.wait(v)
                v.tensor_copy(acc[:, 0:512], po0[:])
                ctx.s_v.inc(v.tensor_copy(acc[:, 512:1024], po1[:]))
                if ch == 0 and tt == 0 and pair_idx == 0:
                    dump(5, acc[:, 0:512])
                ctx.s_v.wait(g)
                t = ch * 4 + tt
                ctx.s_st.inc(g.dma_start(out=h["x1"].ap()[t * 128:(t + 1) * 128, :], in_=acc[:],
                                         accum_op=ALU.add), 16)
                ctx.s_st.wait(v)
            ctx.s_v.wait(sy)
        g.wait_ge(ctx.s_st.h, ctx.s_st.n)
        ctx.barrier()

    for pr in range(4):
        ffn_pass(pr, [(0, 2 * pr), (1, 2 * pr + 1)], 2048,
                 h["L0_ffn_rw1"], [2 * pr * 1024, (2 * pr + 1) * 1024], h["L0_ffn_rb1"],
                 h["L0_ffn_rw2"], [2 * pr * 2048, (2 * pr + 1) * 2048], b2=False, wb=True)
    ffn_pass(4, [(0, None)], 2048, h["L0_ffn_sw1"], [0], h["L0_ffn_sb1"],
             h["L0_ffn_sw2"], [0], b2=True, wb=False)
    nc.gpsimd.wait_ge(ctx.s_st.h, ctx.s_st.n)


def phase_l1_attn(ctx, stage="all"):
    nc = ctx.nc
    v, sc, g, sy, te = nc.vector, nc.scalar, nc.gpsimd, nc.sync, nc.tensor
    h = ctx.h
    SQRT_S = float(np.sqrt(128.0))
    S2 = ctx.sbuf("S2", [128, 4224], F32)
    import contextlib
    with ExitStack() as ses:
        old_es = ctx.es
        ctx.es = ses
        wr = ctx.sbuf("wr", [1, 1023], F32)
        we = ctx.sbuf("we", [1, 4352], F32)
        st0 = ctx.sbuf("stair0", [128, 4352], F32)
        st1 = ctx.sbuf("stair1", [128, 4352], F32)
        bits = ctx.sbuf("bits_sb", [128, 7], I32)
        ctx.s_ld.inc(sy.dma_start(out=wr[:], in_=h["w_rev"].ap()), 16)
        ctx.s_ld.inc(sy.dma_start(out=bits[:], in_=h["bits"].ap()), 16)
        sy.wait_ge(ctx.s_ld.h, ctx.s_ld.n)
        ctx.barrier()
        v.tensor_copy(we[0:1, 0:1536], wr[0:1, 0:1].broadcast_to([1, 1536]))
        v.tensor_copy(we[0:1, 1536:2559], wr[0:1, :])
        v.tensor_copy(we[0:1, 2559:3582], wr[0:1, :])
        ctx.s_v.inc(v.tensor_copy(we[0:1, 3582:4352], wr[0:1, 1022:1023].broadcast_to([1, 770])))
        ctx.s_v.wait(g)
        ctx.s_st.inc(g.dma_start(out=h["wext_rev"].ap(), in_=we[0:1, :]), 16)
        g.wait_ge(ctx.s_st.h, ctx.s_st.n)
        ctx.s_cast.inc(g.dma_start(out=st0[:], in_=bass.AP(h["wext_rev"], 0, [[0, 128], [1, 4352]])), 16)
        g.wait_ge(ctx.s_cast.h, ctx.s_cast.n)
        ctx.barrier()
        N = 4352
        for k in range(7):
            sh = 1 << k
            tin, tout = (st0, st1) if k % 2 == 0 else (st1, st0)
            v.select(tout[:, sh:N], bits[:, k:k + 1].broadcast_to([128, N - sh]),
                     tin[:, 0:N - sh], tin[:, sh:N])
        v.tensor_copy(S2[:], st1[:, 0:4224])
        ctx.barrier()
        ctx.es = old_es

    qT = ctx.sbuf("qTl1", [128, T], BF16)
    kT = ctx.sbuf("kTl1", [128, T], BF16)
    onesb = ctx.sbuf("onesl1", [1, 128], BF16)
    p_a = [ctx.psum(f"p_l1a{i}", [128, 512]) for i in range(2)]
    sub_es = ExitStack()
    old_es2 = ctx.es
    ctx.es = sub_es
    pan8 = [ctx.sbuf(f"xbpan{i}", [128, T], BF16) for i in range(8)]
    cos1 = ctx.sbuf("cos1", [128, SEQ], F32)
    sin1 = ctx.sbuf("sin1", [128, SEQ], F32)
    lwcol = ctx.sbuf("lwcol", [128, 8], F32)
    lbcol = ctx.sbuf("lbcol", [128, 8], F32)
    gbcol = ctx.sbuf("gbcol", [128, 4], F32)
    gbcol2 = ctx.sbuf("gbcol2", [128, 4], F32)
    uvbcol2 = ctx.sbuf("uvbcol2", [128, 1], F32)
    uvbcol = ctx.sbuf("uvbcol", [128, 33], F32)
    wt8 = ctx.sbuf("wt8", [128, 8 * 512], BF16)
    bsil = ctx.sbuf("bsil", [128, 512], F32)
    qpre = ctx.sbuf("qpre", [128, 512], F32)
    rt1 = ctx.sbuf("rt1l1", [128, 512], F32)
    rt2 = ctx.sbuf("rt2l1", [128, 512], F32)
    uvbrow = ctx.sbuf("uvbrow", [1, 2048], BF16)
    evbf = ctx.sbuf("evbf", [128, 512], BF16)

    v.memset(onesb[:], 1.0)
    ctx.s_ld.inc(sy.dma_start(out=cos1[:], in_=h["l1_cos"].ap()), 16)
    ctx.s_ld.inc(sy.dma_start(out=sin1[:], in_=h["l1_sin"].ap()), 16)
    for dc in range(8):
        ctx.s_ld.inc(sy.dma_start(out=lwcol[:, dc:dc + 1],
                                  in_=bass.AP(h["L1_attn_ln_w"], dc * 128, [[1, 128], [1, 1]])), 16)
        ctx.s_ld.inc(sy.dma_start(out=lbcol[:, dc:dc + 1],
                                  in_=bass.AP(h["L1_attn_ln_b"], dc * 128, [[1, 128], [1, 1]])), 16)
    for i in range(2):
        ctx.s_ld.inc(sy.dma_start(out=gbcol[:, 2 * i:2 * i + 1],
                                  in_=bass.AP(h["L1_attn_gamma"], i * 128, [[1, 128], [1, 1]])), 16)
        ctx.s_ld.inc(sy.dma_start(out=gbcol[:, 2 * i + 1:2 * i + 2],
                                  in_=bass.AP(h["L1_attn_beta"], i * 128, [[1, 128], [1, 1]])), 16)
        for half in range(2):
            ctx.s_ld.inc(sy.dma_start(out=gbcol2[half * 64:(half + 1) * 64, 2 * i:2 * i + 1],
                                      in_=bass.AP(h["L1_attn_gamma"], i * 128 + (1 - half) * 64,
                                                  [[1, 64], [1, 1]])), 16)
            ctx.s_ld.inc(sy.dma_start(out=gbcol2[half * 64:(half + 1) * 64, 2 * i + 1:2 * i + 2],
                                      in_=bass.AP(h["L1_attn_beta"], i * 128 + (1 - half) * 64,
                                                  [[1, 64], [1, 1]])), 16)
    for half in range(2):
        ctx.s_ld.inc(sy.dma_start(out=uvbcol2[half * 64:(half + 1) * 64, 0:1],
                                  in_=bass.AP(h["L1_attn_uv_b"], 4096 + (1 - half) * 64,
                                              [[1, 64], [1, 1]])), 16)
    for ft in range(33):
        ctx.s_ld.inc(sy.dma_start(out=uvbcol[:, ft:ft + 1],
                                  in_=bass.AP(h["L1_attn_uv_b"], ft * 128, [[1, 128], [1, 1]])), 16)
    pldv = None
    for dc in range(8):
        for p in range(8):
            pldv = ctx.s_ld.inc(sy.dma_start_transpose(
                pan8[dc][:, p * 512:(p + 1) * 512],
                h["xb_bf"].ap()[p * 512:(p + 1) * 512, dc * 128:(dc + 1) * 128]), 16)
    v.wait_ge(ctx.s_ld.h, ctx.s_ld.n)
    for dc in range(8):
        last = v.tensor_scalar(pan8[dc][:], pan8[dc][:], lwcol[:, dc:dc + 1], lbcol[:, dc:dc + 1],
                               ALU.mult, ALU.add)
    ctx.s_v.inc(last)
    ctx.barrier()

    wtb = ctx.sbuf("wtb", [128, 8 * 128], BF16)
    wtb2 = ctx.sbuf("wtb2", [128, 8 * 128], BF16)
    bsil2 = ctx.sbuf("bsil2", [128, 512], F32)
    qpre2 = ctx.sbuf("qpre2", [128, 512], F32)
    for dc in range(8):
        ctx.s_ld.inc(sy.dma_start_transpose(
            wtb[:, dc * 128:(dc + 1) * 128],
            h["st_uvw"].ap()[4096:4224, dc * 128:(dc + 1) * 128]), 16)
        ctx.s_ld.inc(sy.dma_start_transpose(
            wtb2[:, dc * 128:(dc + 1) * 128],
            h["st_uvw2"].ap()[0:128, dc * 128:(dc + 1) * 128]), 16)
    te.wait_ge(ctx.s_ld.h, ctx.s_ld.n)
    for s in range(2):
        for p in range(4):
            gp = s * 4 + p
            ps, ps2 = p_a[0], p_a[1]
            cs = cos1[:, p * 512:(p + 1) * 512]
            sn = sin1[:, p * 512:(p + 1) * 512]
            ctx.s_v.wait(te)
            for dc in range(8):
                te.matmul(ps[:], wtb[:, dc * 128:(dc + 1) * 128],
                          pan8[dc][:, gp * 512:(gp + 1) * 512], start=(dc == 0), stop=(dc == 7))
            for dc in range(8):
                mm = te.matmul(ps2[:], wtb2[:, dc * 128:(dc + 1) * 128],
                               pan8[dc][:, gp * 512:(gp + 1) * 512], start=(dc == 0), stop=(dc == 7))
            ctx.s_mm.inc(mm)
            ctx.s_mm.wait(sc)
            sc.activation(bsil[:], ps[:], AF.Silu, bias=uvbcol[:, 32:33])
            ctx.s_sc.inc(sc.activation(bsil2[:], ps2[:], AF.Silu, bias=uvbcol2[:, 0:1]))
            ctx.s_sc.wait(v)
            cols = slice(gp * 512, (gp + 1) * 512)
            for qi in range(2):
                v.tensor_scalar(qpre[:], bsil[:], gbcol[:, 2 * qi:2 * qi + 1],
                                gbcol[:, 2 * qi + 1:2 * qi + 2], ALU.mult, ALU.add)
                v.tensor_scalar(qpre2[:], bsil2[:], gbcol2[:, 2 * qi:2 * qi + 1],
                                gbcol2[:, 2 * qi + 1:2 * qi + 2], ALU.mult, ALU.add)
                dst = qT if qi == 0 else kT
                v.tensor_tensor(rt1[:], qpre[:], cs, ALU.mult)
                v.tensor_tensor(rt2[:], qpre2[:], sn, ALU.mult)
                last = v.tensor_add(dst[:, cols], rt1[:], rt2[:])
            ctx.s_v.inc(last)
    ctx.barrier()

    for fg in range(4):
        ctx.s_mm.wait(sy)
        ldv = None
        for dc in range(8):
            ldv = ctx.s_ld.inc(sy.dma_start_transpose(
                wt8[:, dc * 512:(dc + 1) * 512],
                h["st_uvw"].ap()[fg * 512:(fg + 1) * 512, dc * 128:(dc + 1) * 128]), 16)
        te.wait_ge(ctx.s_ld.h, ldv)
        for ft in range(4):
            for gp in range(8):
                ps = p_a[gp % 2]
                ctx.s_sc.wait(te)
                for dc in range(8):
                    mm = te.matmul(ps[:], wt8[:, dc * 512 + ft * 128:dc * 512 + (ft + 1) * 128],
                                   pan8[dc][:, gp * 512:(gp + 1) * 512], start=(dc == 0), stop=(dc == 7))
                ctx.s_mm.inc(mm)
                ctx.s_mm.wait(sc)
                ctx.s_sc.inc(sc.activation(evbf[:], ps[:], AF.Silu, bias=uvbcol[:, fg * 4 + ft:fg * 4 + ft + 1]))
                ctx.s_sc.wait(g)
                ctx.s_st.inc(g.dma_start(
                    out=h["uT"].ap()[(fg * 4 + ft) * 128:(fg * 4 + ft + 1) * 128, gp * 512:(gp + 1) * 512],
                    in_=evbf[:]), 16)
                ctx.s_st.wait(sc)
    for vb in range(4):
        ctx.s_mm.wait(sy)
        ldv = None
        for dc in range(8):
            ldv = ctx.s_ld.inc(sy.dma_start_transpose(
                wt8[:, dc * 512:(dc + 1) * 512],
                h["st_uvw"].ap()[2048 + vb * 512:2048 + (vb + 1) * 512, dc * 128:(dc + 1) * 128]), 16)
        ctx.s_cast.inc(g.dma_start(out=uvbrow[0:1, 0:512],
                                   in_=bass.AP(h["L1_attn_uv_b"], 2048 + vb * 512, [[1, 1], [1, 512]])), 16)
        te.wait_ge(ctx.s_ld.h, ldv)
        te.wait_ge(ctx.s_cast.h, ctx.s_cast.n)
        for tt in range(NT):
            ps = p_a[tt % 2]
            ctx.s_sc.wait(te)
            for dc in range(8):
                te.matmul(ps[:], pan8[dc][:, tt * 128:(tt + 1) * 128], wt8[:, dc * 512:(dc + 1) * 512],
                          start=(dc == 0), stop=False)
            mm = te.matmul(ps[:], onesb[:, 0:128], uvbrow[0:1, 0:512], start=False, stop=True)
            ctx.s_mm.inc(mm)
            ctx.s_mm.wait(sc)
            ctx.s_sc.inc(sc.activation(evbf[:], ps[:], AF.Silu))
            ctx.s_sc.wait(g)
            ctx.s_st.inc(g.dma_start(
                out=h["v_tm"].ap()[tt * 128:(tt + 1) * 128, vb * 512:(vb + 1) * 512], in_=evbf[:]), 16)
            ctx.s_st.wait(sc)
    nc.gpsimd.wait_ge(ctx.s_st.h, ctx.s_st.n)
    ctx.barrier()
    sub_es.close()
    ctx.es = old_es2
    if stage == "s2":
        sc.copy(dbg_s2a := None, None) if False else None
        kfa = ctx.sbuf("kfa", [128, 512], F32)
        for blk in range(6):
            ctx.s_v.wait(g)
            ctx.s_v.inc(v.tensor_copy(kfa[:], S2[:, 1024 + blk * 512:1024 + (blk + 1) * 512]))
            ctx.s_v.wait(g)
            ctx.s_st.inc(g.dma_start(out=h["out"].ap()[blk * 128:(blk + 1) * 128, 0:512], in_=kfa[:]), 16)
            ctx.s_st.wait(v)
        nc.gpsimd.wait_ge(ctx.s_st.h, ctx.s_st.n)
        return
    if stage == "uv":
        return

    vtm_s = ctx.sbuf("vtm_s", [128, 16 * 2048], BF16)
    owT = ctx.sbuf("owT", [128, 16 * 1024], BF16)
    obrow = ctx.sbuf("obrow", [1, 1024], BF16)
    uts = ctx.sbuf("uts", [128, 16 * 512], BF16)
    kernT = ctx.sbuf("kernT", [128, 16 * 512], BF16)
    out2T = ctx.sbuf("out2T", [128, 16 * 512], BF16)
    kf = ctx.sbuf("kf", [128, 512], F32)
    kf2 = ctx.sbuf("kf2", [128, 512], F32)
    x1t2 = [ctx.sbuf(f"x1t2_{i}", [128, D], F32) for i in range(2)]
    xn1t2 = [ctx.sbuf(f"xn1t2_{i}", [128, D], F32) for i in range(2)]
    accv = ctx.sbuf("accv", [128, D], F32)
    p_pr = [ctx.psum(f"p_l1pr{i}", [128, 512]) for i in range(2)]

    for es in range(16):
        for dh2 in range(2):
            ctx.s_ld.inc(sy.dma_start_transpose(
                owT[:, es * 1024 + dh2 * 512:es * 1024 + (dh2 + 1) * 512],
                h["st_ow"].ap()[dh2 * 512:(dh2 + 1) * 512, es * 128:(es + 1) * 128]), 16)
    ctx.s_cast.inc(g.dma_start(out=obrow[:], in_=h["L1_attn_o_b"].ap()), 16)

    nseq = 1 if stage in ("sim", "att") else 2
    for s in range(nseq):
        ctx.s_mm.wait(sy)
        ldvt = None
        for kt in range(16):
            ldvt = ctx.s_ld.inc(sy.dma_start(
                out=vtm_s[:, kt * 2048:(kt + 1) * 2048],
                in_=h["v_tm"].ap()[s * 2048 + kt * 128:s * 2048 + (kt + 1) * 128, :]), 16)
        for qp in range(1 if stage in ("sim", "att") else 4):
            ldu = None
            for es in range(16):
                ldu = ctx.s_ld.inc(sy.dma_start(
                    out=uts[:, es * 512:(es + 1) * 512],
                    in_=h["uT"].ap()[es * 128:(es + 1) * 128, s * 2048 + qp * 512:s * 2048 + (qp + 1) * 512]), 16)
            te.wait_ge(ctx.s_ld.h, ldvt)
            te.wait_ge(ctx.s_ld.h, ldu)
            for kt in range(16):
                ps = p_a[kt % 2]
                ctx.s_sc.wait(te)
                ctx.s_mm.inc(te.matmul(ps[:], kT[:, s * 2048 + kt * 128:s * 2048 + (kt + 1) * 128],
                                       qT[:, s * 2048 + qp * 512:s * 2048 + (qp + 1) * 512],
                                       start=True, stop=True))
                ctx.s_mm.wait(v)
                m0 = 2047 - kt * 128 + qp * 512
                v.scalar_tensor_tensor(kf[:], ps[:], 1.0 / SQRT_S, S2[:, m0:m0 + 512],
                                       ALU.mult, ALU.add)
                ctx.s_v.inc(v.tensor_scalar_max(kf2[:], kf[:], 0.0))
                ctx.s_v.wait(sc)
                ctx.s_sc.inc(sc.activation(kernT[:, kt * 512:(kt + 1) * 512], kf2[:], AF.Square))
            ctx.s_sc.wait(te)
            if stage == "sim":
                ctx.s_sc.wait(v)
                ctx.s_v.inc(v.tensor_copy(kf[:], kernT[:, 0:512]))
                ctx.s_v.inc(v.tensor_copy(kf2[:], kernT[:, 512:1024]))
                ctx.s_v.wait(g)
                ctx.s_st.inc(g.dma_start(out=h["out"].ap()[0:128, 0:512], in_=kf[:]), 16)
                ctx.s_st.inc(g.dma_start(out=h["out"].ap()[0:128, 512:1024], in_=kf2[:]), 16)
                nc.gpsimd.wait_ge(ctx.s_st.h, ctx.s_st.n)
                return
            for es in range(16):
                ps = p_a[es % 2]
                ctx.s_v.wait(te)
                for kt in range(16):
                    mm = te.matmul(ps[:], vtm_s[:, kt * 2048 + es * 128:kt * 2048 + (es + 1) * 128],
                                   kernT[:, kt * 512:(kt + 1) * 512], start=(kt == 0), stop=(kt == 15))
                ctx.s_mm.inc(mm)
                ctx.s_mm.wait(v)
                ctx.s_v.inc(v.tensor_tensor(out2T[:, es * 512:(es + 1) * 512], ps[:],
                                            uts[:, es * 512:(es + 1) * 512], ALU.mult))
            ctx.s_v.wait(te)
            if stage == "att":
                ctx.s_v.wait(g)
                for i4 in range(2):
                    ctx.s_st.inc(g.dma_start(out=h["out"].ap()[128 + i4 * 128:256 + i4 * 128, 0:1024],
                                             in_=out2T[:, i4 * 1024:(i4 + 1) * 1024]), 16)
                nc.gpsimd.wait_ge(ctx.s_st.h, ctx.s_st.n)
                return
            for qs in range(4):
                tglob = s * 16 + qp * 4 + qs
                buf = qs % 2
                ctx.s_v.wait(sy)
                l1 = ctx.s_ld.inc(sy.dma_start(out=x1t2[buf][:],
                                               in_=h["x1"].ap()[tglob * 128:(tglob + 1) * 128, :]), 16)
                l2 = ctx.s_ld.inc(sy.dma_start(out=xn1t2[buf][:],
                                               in_=h["xn1_tm"].ap()[tglob * 128:(tglob + 1) * 128, :]), 16)
                mms = []
                for es in range(16):
                    lh = out2T[:, es * 512 + qs * 128:es * 512 + (qs + 1) * 128]
                    mms.append((0, lh, owT[:, es * 1024:es * 1024 + 512]))
                    mms.append((1, lh, owT[:, es * 1024 + 512:(es + 1) * 1024]))
                mms.append((0, onesb[:, 0:128], obrow[0:1, 0:512]))
                mms.append((1, onesb[:, 0:128], obrow[0:1, 512:1024]))
                te.wait_ge(ctx.s_cast.h, ctx.s_cast.n)
                ctx.s_v.wait(te)
                seen = set()
                lastof = {0: max(i for i, m in enumerate(mms) if m[0] == 0),
                          1: max(i for i, m in enumerate(mms) if m[0] == 1)}
                for i, (pi, lh, rh) in enumerate(mms):
                    ps = p_pr[pi]
                    mm = te.matmul(ps[:], lh, rh, start=(pi not in seen), stop=(i == lastof[pi]))
                    seen.add(pi)
                ctx.s_mm.inc(mm)
                v.wait_ge(ctx.s_ld.h, l2)
                ctx.s_mm.wait(v)
                v.tensor_add(accv[:], x1t2[buf][:], xn1t2[buf][:])
                v.tensor_add(accv[:, 0:512], accv[:, 0:512], p_pr[0][:])
                ctx.s_v.inc(v.tensor_add(accv[:, 512:1024], accv[:, 512:1024], p_pr[1][:]))
                ctx.s_v.wait(g)
                ctx.s_st.inc(g.dma_start(out=h["out"].ap()[tglob * 128:(tglob + 1) * 128, :],
                                         in_=accv[:]), 16)
                ctx.s_st.wait(v)
            ctx.s_v.wait(sy)
    nc.gpsimd.wait_ge(ctx.s_st.h, ctx.s_st.n)


def phase_l1_ffn(ctx):
    nc = ctx.nc
    v, sc, g, sy, te = nc.vector, nc.scalar, nc.gpsimd, nc.sync, nc.tensor
    h = ctx.h
    identf = ctx.sbuf("identf1", [128, 128], F32)
    pan = [ctx.sbuf(f"f1pan{i}", [128, 1024], BF16) for i in range(8)]
    gwT = ctx.sbuf("gwT", [128, 8 * 32], BF16)
    gbb = ctx.sbuf("gbb", [128, 32], F32)
    lgch = ctx.sbuf("lgch", [128, 128], F32)
    sc_ = ctx.sbuf("sc_", [128, 128], F32)
    scb = ctx.sbuf("scb", [128, 128], F32)
    mwrk = ctx.sbuf("mwrk", [128, 128], F32)
    tcur = ctx.sbuf("tcur", [128, 128], F32)
    gstat = ctx.sbuf("gstat", [128, 64], F32)
    gstat2 = ctx.sbuf("gstat2", [128, 16], F32)
    mx8 = ctx.sbuf("mx8", [128, 32], F32)
    w32 = ctx.sbuf("w32ch", [128, 128], F32)
    w32T = ctx.sbuf("w32T", [32, T], BF16)
    oh32 = ctx.sbuf("oh32", [32, 32 * 128], BF16)
    onesb = ctx.sbuf("onesf1", [1, 128], BF16)
    p_wb = ctx.psum("p_wbl1", [128, 512])
    p_g = p_wb
    p_t32 = p_wb
    p_h = [ctx.psum(f"p_l1h{i}", [128, 512]) for i in range(2)]
    p_u = [ctx.psum(f"p_l1u{i}", [128, 512]) for i in range(2)]
    p_o = [ctx.psum(f"p_l1o{i}", [128, 512]) for i in range(2)]

    v.memset(onesb[:], 1.0)
    ctx.s_ld.inc(sy.dma_start(out=identf[:], in_=h["ident"].ap()), 16)
    ctx.s_ld.inc(sy.dma_start(out=gbb[:], in_=bass.AP(h["L1_ffn_gate_bias"], 0, [[0, 128], [1, 32]])), 16)
    for dc in range(8):
        ctx.s_ld.inc(sy.dma_start_transpose(
            gwT[:, dc * 32:(dc + 1) * 32],
            h["st_gw"].ap()[0:32, dc * 128:(dc + 1) * 128]), 16)
    with ExitStack() as ioh_es:
        ioh = ioh_es.enter_context(nc.sbuf_tensor("ioh32", [32, 32 * 128], I32))
        nc.gpsimd.iota(ioh[:], [[1, 32], [0, 128]], base=0, channel_multiplier=-1)
        ctx.barrier()
        ctx.s_v.inc(v.tensor_single_scalar(oh32[:], ioh[:], 0, ALU.is_equal))
        ctx.barrier()

    inv28 = 1.0
    BIGN = 1e30
    for ch in range(8):
        tok0 = ch * 512
        ctx.s_mm.wait(sy)
        ldv = None
        for dc in range(8):
            ldv = ctx.s_ld.inc(sy.dma_start_transpose(
                pan[dc][:, 0:512], h["xn1_bf"].ap()[tok0:tok0 + 512, dc * 128:(dc + 1) * 128]), 16)
        te.wait_ge(ctx.s_ld.h, ldv)
        for tt in range(4):
            ctx.s_v.wait(te)
            for dc in range(8):
                mm = te.matmul(p_g[:, 0:32], pan[dc][:, tt * 128:(tt + 1) * 128], gwT[:, dc * 32:(dc + 1) * 32],
                               start=(dc == 0), stop=(dc == 7))
            ctx.s_mm.inc(mm)
            ctx.s_mm.wait(v)
            ctx.s_v.inc(v.tensor_copy(lgch[:, tt * 32:(tt + 1) * 32], p_g[:, 0:32]))
        ctx.s_v.wait(sc)
        ctx.s_sc.inc(sc.activation(sc_[:], lgch[:], AF.Sigmoid))
        ctx.s_sc.wait(v)

        def vs(instr):
            ctx.s_v.inc(instr)
            ctx.s_v.wait(v)

        gbbv = bass.AP(gbb.handle if hasattr(gbb, "handle") else gbb, 0, [[32, 128], [0, 4], [1, 32]])
        vs(v.tensor_tensor(scb.rearrange("p (t e) -> p t e", t=4), sc_.rearrange("p (t e) -> p t e", t=4),
                           gbbv, ALU.add))
        vw4 = scb.rearrange("p (t g e) -> p t g e", t=4, g=4)
        vs(v.reduce_max(gstat[:, 0:16].rearrange("p (t g) -> p t g", t=4), vw4, AX.X))
        m1bc = gstat[:, 0:16].rearrange("p (t g) -> p t g", t=4).unsqueeze(3).broadcast_to([128, 4, 4, 8])
        vs(v.tensor_tensor(tcur.rearrange("p (t g e) -> p t g e", t=4, g=4), vw4, m1bc, ALU.is_ge))
        vs(v.scalar_tensor_tensor(mwrk[:], tcur[:], -BIGN, scb[:], ALU.mult, ALU.add))
        vs(v.reduce_max(gstat[:, 16:32].rearrange("p (t g) -> p t g", t=4),
                        mwrk.rearrange("p (t g e) -> p t g e", t=4, g=4), AX.X))
        vs(v.tensor_add(gstat[:, 32:48], gstat[:, 0:16], gstat[:, 16:32]))
        gsv = gstat[:, 32:48].rearrange("p (t g) -> p t g", t=4)
        vs(v.reduce_max(gstat2[:, 0:4].rearrange("p (t x) -> p t x", t=4), gsv, AX.X))
        m1Gbc = gstat2[:, 0:4].rearrange("p (t x) -> p t x", t=4).broadcast_to([128, 4, 4])
        vs(v.tensor_tensor(gstat[:, 0:16].rearrange("p (t g) -> p t g", t=4), gsv, m1Gbc, ALU.is_ge))
        vs(v.scalar_tensor_tensor(gstat[:, 48:64], gstat[:, 0:16], -BIGN, gstat[:, 32:48],
                                  ALU.mult, ALU.add))
        vs(v.reduce_max(gstat2[:, 4:8].rearrange("p (t x) -> p t x", t=4),
                        gstat[:, 48:64].rearrange("p (t g) -> p t g", t=4), AX.X))
        m2Gbc = gstat2[:, 4:8].rearrange("p (t x) -> p t x", t=4).broadcast_to([128, 4, 4])
        vs(v.tensor_tensor(gstat[:, 0:16].rearrange("p (t g) -> p t g", t=4), gsv, m2Gbc, ALU.is_ge))
        gmaskbc = gstat[:, 0:16].rearrange("p (t g) -> p t g", t=4).unsqueeze(3).broadcast_to([128, 4, 4, 8])
        vs(v.tensor_tensor(mwrk.rearrange("p (t g e) -> p t g e", t=4, g=4), vw4, gmaskbc, ALU.mult))
        vs(v.tensor_copy(tcur[:], mwrk[:]))
        for it in range(8):
            vs(v.reduce_max(mx8[:, it * 4:(it + 1) * 4].rearrange("p (t x) -> p t x", t=4),
                            tcur.rearrange("p (t e) -> p t e", t=4), AX.X))
            if it < 7:
                mxbc = mx8[:, it * 4:(it + 1) * 4].rearrange("p (t x) -> p t x", t=4).broadcast_to([128, 4, 32])
                vs(v.tensor_tensor(w32.rearrange("p (t e) -> p t e", t=4),
                                   tcur.rearrange("p (t e) -> p t e", t=4), mxbc, ALU.is_ge))
                vs(v.scalar_tensor_tensor(scb[:], w32[:], -BIGN, tcur[:], ALU.mult, ALU.add))
                vs(v.tensor_copy(tcur[:], scb[:]))
        t8bc = mx8[:, 28:32].rearrange("p (t x) -> p t x", t=4).broadcast_to([128, 4, 32])
        vs(v.tensor_tensor(w32.rearrange("p (t e) -> p t e", t=4),
                           mwrk.rearrange("p (t e) -> p t e", t=4), t8bc, ALU.is_ge))
        vs(v.tensor_mul(tcur[:], w32[:], sc_[:]))
        vs(v.reduce_sum(gstat2[:, 8:12].rearrange("p (t x) -> p t x", t=4),
                        tcur.rearrange("p (t e) -> p t e", t=4), AX.X))
        vs(v.tensor_scalar_add(gstat2[:, 0:4], gstat2[:, 8:12], 1e-20))
        vs(v.reciprocal(gstat2[:, 12:16], gstat2[:, 0:4]))
        rbc = gstat2[:, 12:16].rearrange("p (t x) -> p t x", t=4).broadcast_to([128, 4, 32])
        ctx.s_v.inc(v.scalar_tensor_tensor(w32.rearrange("p (t e) -> p t e", t=4),
                                           tcur.rearrange("p (t e) -> p t e", t=4), 2.5, rbc,
                                           ALU.mult, ALU.mult))
        for tt in range(4):
            t = ch * 4 + tt
            ctx.s_v.wait(te)
            ctx.s_mm.inc(te.transpose(p_t32[0:32, 0:128], w32[:, tt * 32:(tt + 1) * 32], identf[:]))
            ctx.s_mm.wait(v)
            ctx.s_v.inc(v.tensor_copy(w32T[:, t * 128:(t + 1) * 128], p_t32[0:32, 0:128]))
    ctx.barrier()

    egT = [ctx.sbuf(f"egT{i}", [128, 8 * 1024 if i == 0 else 8 * 512], BF16) for i in range(4)]
    euT = [ctx.sbuf(f"euT{i}", [128, 8 * 1024 if i == 0 else 8 * 512], BF16) for i in range(4)]
    edT = [ctx.sbuf(f"edT{i}", [128, 8 * 1024 if i == 0 else 4 * 1024], BF16) for i in range(4)]
    hsb = [ctx.sbuf(f"l1h{i}", [128, 8 * 1024 if i == 0 else 4 * 1024], BF16) for i in range(4)]
    wbsb = ctx.sbuf("wbsbl1", [128, 512], F32)
    sg1 = ctx.sbuf("sg1", [128, 512], F32)
    ug1 = ctx.sbuf("ug1", [128, 512], F32)
    acc1 = ctx.sbuf("accl1", [128, D], F32)

    def grp_pass(experts, I, w1g, w1u, w2, wb=True):
        nit = I // 128
        for ei, (slot, e) in enumerate(experts):
            r1 = (e or 0) * I
            for dc in range(8):
                ctx.s_ld.inc(sy.dma_start_transpose(
                    egT[slot][:, dc * I:(dc + 1) * I],
                    w1g.ap()[r1:r1 + I, dc * 128:(dc + 1) * 128]), 16)
                ctx.s_ld.inc(sy.dma_start_transpose(
                    euT[slot][:, dc * I:(dc + 1) * I],
                    w1u.ap()[r1:r1 + I, dc * 128:(dc + 1) * 128]), 16)
            r2 = (e or 0) * 1024
            for ic in range(nit):
                for dh2 in range(2):
                    ctx.s_ld.inc(sy.dma_start_transpose(
                        edT[slot][:, ic * 1024 + dh2 * 512:ic * 1024 + (dh2 + 1) * 512],
                        w2.ap()[r2 + dh2 * 512:r2 + (dh2 + 1) * 512, ic * 128:(ic + 1) * 128]), 16)
        sy.wait_ge(ctx.s_ld.h, ctx.s_ld.n)
        ctx.barrier()
        for ch in range(4):
            tok0 = ch * 1024
            ctx.s_mm.wait(sy)
            ldv = None
            for dc in range(8):
                ldv = ctx.s_ld.inc(sy.dma_start_transpose(
                    pan[dc][:], h["xn1_bf"].ap()[tok0:tok0 + 1024, dc * 128:(dc + 1) * 128]), 16)
            te.wait_ge(ctx.s_ld.h, ldv)
            v.wait_ge(ctx.s_ld.h, ldv)
            ctx.s_mm.wait(v)
            for ei, (slot, e) in enumerate(experts):
                for half in range(2):
                    if wb:
                        ctx.s_v.wait(te)
                        ctx.s_mm.inc(te.matmul(p_wb[:], oh32[:, e * 128:(e + 1) * 128],
                                               w32T[:, tok0 + half * 512:tok0 + (half + 1) * 512],
                                               start=True, stop=True))
                        ctx.s_mm.wait(v)
                        ctx.s_v.inc(v.tensor_copy(wbsb[:], p_wb[:]))
                    for it in range(nit):
                        psg, psu = p_h[it % 2], p_u[it % 2]
                        ctx.s_v.wait(te)
                        for dc in range(8):
                            te.matmul(psg[:], egT[slot][:, dc * I + it * 128:dc * I + (it + 1) * 128],
                                      pan[dc][:, half * 512:(half + 1) * 512],
                                      start=(dc == 0), stop=(dc == 7))
                        for dc in range(8):
                            mm = te.matmul(psu[:], euT[slot][:, dc * I + it * 128:dc * I + (it + 1) * 128],
                                           pan[dc][:, half * 512:(half + 1) * 512],
                                           start=(dc == 0), stop=(dc == 7))
                        ctx.s_mm.inc(mm)
                        ctx.s_mm.wait(sc)
                        ctx.s_sc.inc(sc.activation(sg1[:], psg[:], AF.Silu))
                        ctx.s_sc.wait(v)
                        if wb:
                            v.tensor_tensor(ug1[:], psu[:], wbsb[:], ALU.mult)
                        else:
                            v.tensor_copy(ug1[:], psu[:])
                        ctx.s_v.inc(v.tensor_tensor(
                            hsb[slot][:, it * 1024 + half * 512:it * 1024 + (half + 1) * 512],
                            sg1[:], ug1[:], ALU.mult))
            ctx.s_v.wait(te)
            for tt in range(8):
                pslot = tt % 2
                po0, po1 = p_o[pslot], p_h[pslot]
                if tt >= 2:
                    ctx.s_st.wait(te)
                mms = []
                for ei, (slot, e) in enumerate(experts):
                    for ic in range(nit):
                        lh = hsb[slot][:, ic * 1024 + tt * 128:ic * 1024 + (tt + 1) * 128]
                        mms.append((0, lh, edT[slot][:, ic * 1024:ic * 1024 + 512]))
                        mms.append((1, lh, edT[slot][:, ic * 1024 + 512:(ic + 1) * 1024]))
                seen = set()
                lastof = {0: max(i for i, m in enumerate(mms) if m[0] == 0),
                          1: max(i for i, m in enumerate(mms) if m[0] == 1)}
                for i, (pi, lh, rh) in enumerate(mms):
                    ps = po0 if pi == 0 else po1
                    mm = te.matmul(ps[:], lh, rh, start=(pi not in seen), stop=(i == lastof[pi]))
                    seen.add(pi)
                ctx.s_mm.inc(mm)
                ctx.s_mm.wait(v)
                v.tensor_copy(acc1[:, 0:512], po0[:])
                ctx.s_v.inc(v.tensor_copy(acc1[:, 512:1024], po1[:]))
                ctx.s_v.wait(g)
                tg = ch * 8 + tt
                ctx.s_st.inc(g.dma_start(out=h["out"].ap()[tg * 128:(tg + 1) * 128, :], in_=acc1[:],
                                         accum_op=ALU.add), 16)
                ctx.s_st.wait(v)
            ctx.s_v.wait(sy)
        g.wait_ge(ctx.s_st.h, ctx.s_st.n)
        ctx.barrier()

    for gi in range(8):
        grp_pass([(i, gi * 4 + i) for i in range(4)], 512,
                 h["st_eg"], h["st_eu"], h["st_ed"], wb=True)
    grp_pass([(0, None)], 1024, h["st_shg"], h["st_shu"], h["st_shd"], wb=False)
    nc.gpsimd.wait_ge(ctx.s_st.h, ctx.s_st.n)


def build(dbg_taps=(), upto=None, dbg=False):
    nc = bass.Bass("TRN2", target_bir_lowering=False)
    with ExitStack() as es:
        ctx = Ctx(nc, es)
        ctx.dbg = dbg if isinstance(dbg, (set, frozenset)) else ({"attn0", "ffn0"} if dbg else None)
        declare_tensors(nc, ctx, dbg_taps)

        def phase(fn, *a, **k):
            with ExitStack() as pes:
                old = ctx.es
                ctx.es = pes
                fn(ctx, *a, **k)
                ctx.es = old
            ctx.barrier()

        phase(phase_stage)
        phase(phase_ln, ctx.h["x"].ap(), ctx.h["xn0_bf"], None, ctx.h["xa_bf"], tag="0")
        if upto == "ln0":
            return nc
        phase(phase_l0_attn)
        if upto == "l0attn":
            return nc
        phase(phase_l0_ffn)
        if upto == "l0":
            return nc
        phase(phase_ln, ctx.h["x1"].ap(), ctx.h["xn1_bf"], ctx.h["xn1_tm"], ctx.h["xb_bf"], tag="1")
        if upto == "ln1":
            return nc
        if upto == "l1uv":
            phase(phase_l1_attn, stage="uv")
            return nc
        if upto in ("l1sim", "l1att", "l1s2"):
            phase(phase_l1_attn, stage=upto[2:])
            return nc
        phase(phase_l1_attn)
        if upto == "l1attn":
            return nc
        phase(phase_l1_ffn)
    return nc


def make_in_map(core, inputs):
    c = _host_consts()
    m = {}
    x = np.asarray(inputs["L0_attn_x"])
    m["x"] = np.ascontiguousarray(x[2 * core:2 * core + 2].reshape(T, D), dtype=np.float32)
    for nm in WEIGHT_NAMES:
        a = np.asarray(inputs[nm], dtype=np.float32)
        shp = INPUT_SHAPES[nm]
        if len(shp) == 3:
            a = a.reshape(shp[0] * shp[1], shp[2])
        elif len(shp) == 1:
            a = a.reshape(1, shp[0])
        m[nm] = np.ascontiguousarray(a)
    m["w_rev"] = np.ascontiguousarray(np.asarray(inputs["L1_attn_w"], dtype=np.float32)[::-1].reshape(1, 1023))
    for nm in CONST_SHAPES:
        m[nm] = c[nm]
    m["bits"] = c["bits"]
    return m


_NC_CACHE = {}


def kernel(**inputs):
    key = "full"
    if key not in _NC_CACHE:
        _NC_CACHE[key] = build()
    nc = _NC_CACHE[key]
    in_maps = [make_in_map(cc, inputs) for cc in range(8)]
    res = run_bass_kernel_spmd(nc, in_maps, core_ids=list(range(8)))
    out = np.stack([r["out"] for r in res.results])
    return out.reshape(16, SEQ, D).astype(np.float32)


# version 08 (final)
# speedup vs baseline: 1.3402x; 1.3402x over previous
import math
from contextlib import ExitStack

import numpy as np
import ml_dtypes

import concourse.bass as bass
import concourse.mybir as mybir
from concourse.bass_utils import run_bass_kernel_spmd

F32 = mybir.dt.float32
BF16 = mybir.dt.bfloat16
I32 = mybir.dt.int32
AF = mybir.ActivationFunctionType
ALU = mybir.AluOpType
AX = mybir.AxisListType
BF = ml_dtypes.bfloat16

T = 4096
SEQ = 2048
D = 1024
NEG = -3.4028234663852886e+38
NT = T // 128

_CONST_CACHE = None

def _host_consts():
    global _CONST_CACHE
    if _CONST_CACHE is not None:
        return _CONST_CACHE
    c = {}
    dh, ws, jlen, SCALE_BASE = 64, 128, 384, 64
    inv_freq = 1.0 / 10000.0 ** (np.arange(0, dh, 2, dtype=np.float64) / dh)
    t = np.arange(jlen, dtype=np.float64)
    half = t[:, None] * inv_freq[None, :]
    freqs = np.concatenate([half, half], -1)
    power = (t - jlen // 2) / SCALE_BASE
    sbase = (np.arange(0, dh, 2, dtype=np.float64) + 0.4 * dh) / (1.4 * dh)
    scale = sbase[None, :] ** power[:, None]
    scale = np.concatenate([scale, scale], -1)
    qf, qs = freqs[-ws:], scale[-ws:]
    sgn = np.where(np.arange(dh) < dh // 2, -1.0, 1.0)
    s05 = dh ** -0.5
    CQ = (np.cos(qf) * qs * s05).T
    SQ = (np.sin(qf) * qs * s05).T * sgn[:, None]
    inv_s = 1.0 / scale
    CK = (np.cos(freqs) * inv_s).T
    SK = (np.sin(freqs) * inv_s).T * sgn[:, None]
    c["l0_cq"] = np.concatenate([CQ, CQ], 0).astype(np.float32)
    c["l0_sq"] = np.concatenate([SQ, SQ], 0).astype(np.float32)
    c["l0_ck"] = np.concatenate([CK, CK], 0).astype(np.float32)
    c["l0_sk"] = np.concatenate([SK, SK], 0).astype(np.float32)
    i = np.arange(128)[:, None]
    j = np.arange(384)[None, :]
    band = (j >= i) & (j <= i + 256)
    m = []
    for typ in range(3):
        ok = band.copy()
        if typ == 0:
            ok &= j >= 128
        if typ == 2:
            ok &= j < 256
        m.append(np.where(ok, 0.0, NEG).astype(np.float32))
    c["l0_mask"] = np.concatenate(m, axis=1)
    try:
        import jax
        import jax.numpy as jnp
        with jax.default_device(jax.devices("cpu")[0]):
            pos = jnp.arange(SEQ, dtype=jnp.float32)
            invf = 10000.0 ** (jnp.arange(64, dtype=jnp.float32) / 64)
            sinu = pos[:, None] * invf[None, :]
            sin1 = np.asarray(jnp.sin(sinu), dtype=np.float32)
            cos1 = np.asarray(jnp.cos(sinu), dtype=np.float32)
    except Exception:
        pos = np.arange(SEQ, dtype=np.float32)
        invf = (np.float32(10000.0) ** (np.arange(64, dtype=np.float32) / np.float32(64))).astype(np.float32)
        sinu = (pos[:, None] * invf[None, :]).astype(np.float32)
        sin1 = np.sin(sinu).astype(np.float32)
        cos1 = np.cos(sinu).astype(np.float32)
    sinT = sin1.T
    cosT = cos1.T
    c["l1_sin"] = np.ascontiguousarray(np.concatenate([-sinT, sinT], 0))
    c["l1_cos"] = np.ascontiguousarray(np.concatenate([cosT, cosT], 0))
    c["bits"] = ((np.arange(128)[:, None] >> np.arange(7)[None, :]) & 1).astype(np.int32)
    c["ident"] = np.eye(128, dtype=np.float32)
    _CONST_CACHE = c
    return c

class Sem:
    def __init__(self, ctx, name):
        self.h = ctx.es.enter_context(ctx.nc.semaphore(name))
        self.n = 0

    def inc(self, instr, k=1):
        instr.then_inc(self.h, k)
        self.n += k
        return self.n

    def wait(self, eng, val=None):
        eng.wait_ge(self.h, self.n if val is None else val)


class Ctx:
    def __init__(self, nc, es, dbg=False):
        self.nc = nc
        self.es = es
        self.dbg = dbg
        self.h = {}
        self.sb = {}
        self.s_ld = Sem(self, "s_ld")
        self.s_cast = Sem(self, "s_cast")
        self.s_st = Sem(self, "s_st")
        self.s_mm = Sem(self, "s_mm")
        self.s_v = Sem(self, "s_v")
        self.s_sc = Sem(self, "s_sc")
        self.s_g = Sem(self, "s_g")
        self.s_ldr = [Sem(self, "s_ldr0"), Sem(self, "s_ldr1")]
        self.s_accr = [Sem(self, f"s_accr{i}") for i in range(4)]

    def sbuf(self, name, shape, dtype):
        return self.es.enter_context(self.nc.sbuf_tensor(name, shape, dtype))

    def psum(self, name, shape, dtype=F32):
        return self.es.enter_context(self.nc.psum_tensor(name, shape, dtype))

    def barrier(self):
        self.nc.all_engine_barrier()


def _dram(nc, name, shape, dtype, kind=None):
    if kind is None:
        return nc.dram_tensor(name, shape, dtype)
    return nc.dram_tensor(name, shape, dtype, kind=kind)


STAGE_SPECS = [
    ("st_wqkv", "L0_attn_w_qkv", (1024, 1024)),
    ("st_wout", "L0_attn_w_out", (1024, 512)),
    ("st_uvw", "L1_attn_uv_w", (4224, 1024)),
    ("st_ow", "L1_attn_o_w", (1024, 2048)),
    ("st_gw", "L1_ffn_gate_w", (32, 1024)),
    ("st_eg", "L1_ffn_expert_gate", (32 * 512, 1024)),
    ("st_eu", "L1_ffn_expert_up", (32 * 512, 1024)),
    ("st_ed", "L1_ffn_expert_down", (32 * 1024, 512)),
    ("st_shg", "L1_ffn_shared_gate", (1024, 1024)),
    ("st_shu", "L1_ffn_shared_up", (1024, 1024)),
    ("st_shd", "L1_ffn_shared_down", (1024, 1024)),
]


def phase_stage(ctx):
    nc = ctx.nc
    for dst, src, shape in STAGE_SPECS:
        d = ctx.h[dst]
        s = ctx.h[src]
        rows, cols = shape
        rchunk = max(1, min(rows, (2 << 20) // cols))
        r = 0
        while r < rows:
            n = min(rchunk, rows - r)
            ctx.s_cast.inc(nc.gpsimd.dma_start(
                out=bass.AP(d, r * cols, [[cols, n], [1, cols]]),
                in_=bass.AP(s, r * cols, [[cols, n], [1, cols]])), 16)
            r += n
    d = ctx.h["st_wqkv2"]
    s = ctx.h["L0_attn_w_qkv"]
    ctx.s_cast.inc(nc.gpsimd.dma_start(
        out=bass.AP(d, 0, [[64 * 1024, 8], [1024, 32], [1, 1024]]),
        in_=bass.AP(s, 32 * 1024, [[64 * 1024, 8], [1024, 32], [1, 1024]])), 16)
    ctx.s_cast.inc(nc.gpsimd.dma_start(
        out=bass.AP(d, 32 * 1024, [[64 * 1024, 8], [1024, 32], [1, 1024]]),
        in_=bass.AP(s, 0, [[64 * 1024, 8], [1024, 32], [1, 1024]])), 16)
    d2 = ctx.h["st_uvw2"]
    s2 = ctx.h["L1_attn_uv_w"]
    ctx.s_cast.inc(nc.gpsimd.dma_start(
        out=bass.AP(d2, 0, [[1024, 64], [1, 1024]]),
        in_=bass.AP(s2, (4096 + 64) * 1024, [[1024, 64], [1, 1024]])), 16)
    ctx.s_cast.inc(nc.gpsimd.dma_start(
        out=bass.AP(d2, 64 * 1024, [[1024, 64], [1, 1024]]),
        in_=bass.AP(s2, 4096 * 1024, [[1024, 64], [1, 1024]])), 16)
    nc.gpsimd.wait_ge(ctx.s_cast.h, ctx.s_cast.n)


def phase_ln(ctx, src_ap, dst_bf, dst_f32=None, inner_bf=None, tag=""):
    nc = ctx.nc
    v, sc, g, sy = nc.vector, nc.scalar, nc.gpsimd, nc.sync
    sx = [ctx.sbuf(f"ln_x{tag}{i}", [128, D], F32) for i in range(2)]
    xn32 = ctx.sbuf(f"ln_xn32{tag}", [128, D], F32)
    xnbf = ctx.sbuf(f"ln_xnbf{tag}", [128, D], BF16)
    xabf = ctx.sbuf(f"ln_xabf{tag}", [128, D], BF16)
    scr = ctx.sbuf(f"ln_scr{tag}", [128, D], F32)
    st = ctx.sbuf(f"ln_st{tag}", [128, 8], F32)
    v.memset(st[:, 7:8], 1e-5)
    def norm_tile(buf, out_tile):
        v.reduce_sum(st[:, 0:1], buf, AX.X, negate=True)
        ctx.s_v.inc(v.memset(st[:, 1:2], 0.0))
        ctx.s_v.wait(sc)
        sc.activation(scr[:], buf, AF.Square, accum_out=st[:, 1:2])
        ctx.s_sc.inc(sc.mul(st[:, 2:3], st[:, 0:1], 1.0 / D))
        ctx.s_sc.wait(sc)
        ctx.s_sc.inc(sc.activation(st[:, 3:4], st[:, 2:3], AF.Square))
        ctx.s_sc.wait(v)
        ctx.s_v.inc(v.tensor_scalar(st[:, 4:5], st[:, 1:2], 1.0 / D, st[:, 3:4],
                                    ALU.mult, ALU.subtract))
        ctx.s_v.wait(sc)
        ctx.s_sc.inc(sc.activation(st[:, 5:6], st[:, 4:5], AF.Sqrt, bias=st[:, 7:8]))
        ctx.s_sc.wait(v)
        ctx.s_v.inc(v.reciprocal(st[:, 6:7], st[:, 5:6]))
        ctx.s_v.wait(v)
        ctx.s_v.inc(v.scalar_tensor_tensor(out_tile, buf, st[:, 2:3],
                                           st[:, 6:7].broadcast_to([128, D]),
                                           ALU.add, ALU.mult))

    ld_done = {}
    for t in range(NT):
        buf = sx[t % 2]
        if t == 0:
            ld_done[0] = ctx.s_ldr[0].inc(sy.dma_start(out=sx[0][:], in_=src_ap[0:128, :]), 16)
        if t + 1 < NT:
            sl = (t + 1) % 2
            ld_done[t + 1] = ctx.s_ldr[sl].inc(
                sy.dma_start(out=sx[sl][:], in_=src_ap[(t + 1) * 128:(t + 2) * 128, :]), 16)
        v.wait_ge(ctx.s_ldr[t % 2].h, ld_done[t])
        norm_tile(buf[:], xn32[:])
        ctx.s_v.inc(v.tensor_copy(xnbf[:], xn32[:]))
        ctx.s_v.wait(g)
        ctx.s_st.inc(g.dma_start(out=dst_bf.ap()[t * 128:(t + 1) * 128, :], in_=xnbf[:]), 16)
        if dst_f32 is not None:
            ctx.s_st.inc(g.dma_start(out=dst_f32.ap()[t * 128:(t + 1) * 128, :], in_=xn32[:]), 16)
        if inner_bf is not None:
            norm_tile(xn32[:], xabf[:])
            ctx.s_v.wait(g)
            ctx.s_st.inc(g.dma_start(out=inner_bf.ap()[t * 128:(t + 1) * 128, :], in_=xabf[:]), 16)
        ctx.s_st.wait(sy)
    nc.gpsimd.wait_ge(ctx.s_st.h, ctx.s_st.n)


WEIGHT_NAMES = [
    "L0_attn_ln_g", "L0_attn_ln_b", "L0_attn_w_qkv", "L0_attn_w_out",
    "L0_ffn_router_w", "L0_ffn_router_b", "L0_ffn_sw1", "L0_ffn_sb1",
    "L0_ffn_sw2", "L0_ffn_sb2", "L0_ffn_rw1", "L0_ffn_rb1", "L0_ffn_rw2", "L0_ffn_rb2",
    "L1_attn_ln_w", "L1_attn_ln_b", "L1_attn_uv_w", "L1_attn_uv_b",
    "L1_attn_gamma", "L1_attn_beta", "L1_attn_w", "L1_attn_o_w", "L1_attn_o_b",
    "L1_ffn_gate_w", "L1_ffn_gate_bias", "L1_ffn_expert_gate", "L1_ffn_expert_up",
    "L1_ffn_expert_down", "L1_ffn_shared_gate", "L1_ffn_shared_up", "L1_ffn_shared_down",
]
INPUT_SHAPES = {
    "L0_attn_ln_g": (D,), "L0_attn_ln_b": (D,),
    "L0_attn_w_qkv": (1024, 1024), "L0_attn_w_out": (1024, 512),
    "L0_ffn_router_w": (1024, 8), "L0_ffn_router_b": (8,),
    "L0_ffn_sw1": (1024, 2048), "L0_ffn_sb1": (2048,),
    "L0_ffn_sw2": (2048, 1024), "L0_ffn_sb2": (1024,),
    "L0_ffn_rw1": (8, 1024, 2048), "L0_ffn_rb1": (8, 2048),
    "L0_ffn_rw2": (8, 2048, 1024), "L0_ffn_rb2": (8, 1024),
    "L1_attn_ln_w": (D,), "L1_attn_ln_b": (D,),
    "L1_attn_uv_w": (4224, 1024), "L1_attn_uv_b": (4224,),
    "L1_attn_gamma": (2, 128), "L1_attn_beta": (2, 128),
    "L1_attn_w": (1023,), "L1_attn_o_w": (1024, 2048), "L1_attn_o_b": (1024,),
    "L1_ffn_gate_w": (32, 1024), "L1_ffn_gate_bias": (32,),
    "L1_ffn_expert_gate": (32, 512, 1024), "L1_ffn_expert_up": (32, 512, 1024),
    "L1_ffn_expert_down": (32, 1024, 512),
    "L1_ffn_shared_gate": (1024, 1024), "L1_ffn_shared_up": (1024, 1024),
    "L1_ffn_shared_down": (1024, 1024),
}
CONST_SHAPES = {
    "l0_cq": (128, 128), "l0_sq": (128, 128), "l0_ck": (128, 384), "l0_sk": (128, 384),
    "l0_mask": (128, 3 * 384), "l1_sin": (128, SEQ), "l1_cos": (128, SEQ),
    "ident": (128, 128),
}


def declare_tensors(nc, ctx, dbg_taps=()):
    h = ctx.h
    h["x"] = nc.dram_tensor("x", [T, D], F32, kind="ExternalInput")
    h["out"] = nc.dram_tensor("out", [T, D], F32, kind="ExternalOutput")
    for nm in WEIGHT_NAMES:
        shp = list(INPUT_SHAPES[nm])
        if len(shp) == 3:
            shp = [shp[0] * shp[1], shp[2]]
        if len(shp) == 1:
            shp = [1, shp[0]]
        h[nm] = nc.dram_tensor(nm, shp, F32, kind="ExternalInput")
    h["w_rev"] = nc.dram_tensor("w_rev", [1, 1023], F32, kind="ExternalInput")
    for nm, shp in CONST_SHAPES.items():
        h[nm] = nc.dram_tensor(nm, list(shp), F32, kind="ExternalInput")
    h["bits"] = nc.dram_tensor("bits", [128, 7], I32, kind="ExternalInput")

    def scratch(name, shape, dtype):
        kind = "ExternalOutput" if name in dbg_taps else None
        h[name] = _dram(nc, name, list(shape), dtype, kind)

    for nm, src, shape in STAGE_SPECS:
        scratch(nm, shape, BF16)
    scratch("st_wqkv2", (512, 1024), BF16)
    scratch("st_uvw2", (128, 1024), BF16)
    scratch("xn0_bf", (T, D), BF16)
    scratch("xa_bf", (T, D), BF16)
    scratch("x1", (T, D), F32)
    scratch("xn1_bf", (T, D), BF16)
    scratch("xn1_tm", (T, D), F32)
    scratch("xb_bf", (T, D), BF16)
    scratch("uT", (2048, T), BF16)
    scratch("v_tm", (T, 2048), BF16)
    scratch("wext_rev", (1, 4352), F32)
    scratch("dbg0", (16, 128, 512), F32)
    return h


def phase_l0_attn(ctx):
    nc = ctx.nc
    v, sc, g, sy, te = nc.vector, nc.scalar, nc.gpsimd, nc.sync, nc.tensor
    h = ctx.h
    GW = 4352
    qkT = [ctx.sbuf(f"qkT{c}", [128, GW], BF16) for c in range(4)]
    qk2T = [ctx.sbuf(f"qk2T{c}", [128, GW], BF16) for c in range(4)]
    vtm = ctx.sbuf("vtm", [128, 32 * 512], BF16)
    cq = ctx.sbuf("cq", [128, 128], F32)
    sq = ctx.sbuf("sq", [128, 128], F32)
    ck = ctx.sbuf("ck", [128, 384], F32)
    sk = ctx.sbuf("sk", [128, 384], F32)
    msk = ctx.sbuf("msk", [128, 3 * 384], F32)
    identb = ctx.sbuf("identb", [128, 128], BF16)
    gcol = ctx.sbuf("gcol", [128, 8], F32)
    bcol = ctx.sbuf("bcol", [128, 8], F32)
    pan = [ctx.sbuf(f"l0pan{i}", [128, 512], BF16) for i in range(8)]
    wq = ctx.sbuf("wq", [128, 8 * 512], BF16)
    wq2 = ctx.sbuf("wq2", [128, 8 * 512], BF16)
    wv = ctx.sbuf("wv", [128, 8 * 512], BF16)
    wo = ctx.sbuf("wo", [128, 8 * 512], BF16)
    x0t = [ctx.sbuf(f"x0t{i}", [128, D], F32) for i in range(2)]
    x1t = ctx.sbuf("x1t", [128, D], F32)
    tm = ctx.sbuf("tm", [128, 384], F32)
    ssc = ctx.sbuf("ssc", [128, 4], F32)
    attn_bf = ctx.sbuf("attn_bf", [128, 384], BF16)
    e32 = ctx.sbuf("e32", [128, 384], F32)
    attnT_bf = ctx.sbuf("attnT_bf", [128, 384], BF16)
    bq = [ctx.sbuf(f"bq{i}", [128, 128], BF16) for i in range(4)]
    bk = [ctx.sbuf(f"bk{i}", [128, 384], BF16) for i in range(4)]
    rt1 = ctx.sbuf("rt1", [128, 384], F32)
    rt2 = ctx.sbuf("rt2", [128, 384], F32)
    outT_bf = ctx.sbuf("outT_bf", [128, 512], BF16)
    p_sim = [ctx.psum(f"p_sim{i}", [128, 384]) for i in range(2)]
    p_at = ctx.es.enter_context(nc.psum_tensor("p_at", [128, 384], BF16))
    p_outT = ctx.psum("p_outT", [128, 512])
    p_pr = [ctx.psum(f"p_pr{i}", [128, 512]) for i in range(2)]
    dbgt = ctx.sbuf("dbgt", [128, 512], F32)

    def dump(idx, src_ap, width=512):
        if not (ctx.dbg and "attn0" in ctx.dbg):
            return
        v.memset(dbgt[:], 0.0)
        ctx.s_v.inc(v.tensor_copy(dbgt[:, 0:width], src_ap))
        ctx.s_v.wait(g)
        ctx.s_st.inc(g.dma_start(out=h["dbg0"].ap()[idx, :, :], in_=dbgt[:]), 16)
        ctx.s_st.wait(v)

    for dst, src in [(cq, "l0_cq"), (sq, "l0_sq"), (ck, "l0_ck"), (sk, "l0_sk"), (msk, "l0_mask")]:
        ctx.s_ld.inc(sy.dma_start(out=dst[:], in_=h[src].ap()), 16)
    ctx.s_cast.inc(g.dma_start(out=identb[:], in_=h["ident"].ap()), 16)
    for dc in range(8):
        ctx.s_ld.inc(sy.dma_start(out=gcol[:, dc:dc + 1],
                                  in_=bass.AP(h["L0_attn_ln_g"], dc * 128, [[1, 128], [1, 1]])), 16)
        ctx.s_ld.inc(sy.dma_start(out=bcol[:, dc:dc + 1],
                                  in_=bass.AP(h["L0_attn_ln_b"], dc * 128, [[1, 128], [1, 1]])), 16)
    for ic in range(4):
        for dh2 in range(2):
            ctx.s_ld.inc(sy.dma_start_transpose(
                wo[:, (ic * 2 + dh2) * 512:(ic * 2 + dh2 + 1) * 512],
                h["st_wout"].ap()[dh2 * 512:(dh2 + 1) * 512, ic * 128:(ic + 1) * 128]), 16)
    for c in range(4):
        v.memset(qkT[c][:, 0:128], 0.0)
        v.memset(qkT[c][:, GW - 128:GW], 0.0)
        v.memset(qk2T[c][:, 0:128], 0.0)
        v.memset(qk2T[c][:, GW - 128:GW], 0.0)
    ctx.s_v.inc(v.memset(tm[:, 0:1], 0.0))
    ctx.barrier()

    for p in range(8):
        tok0 = p * 512
        ldv = []
        for dc in range(8):
            ldv.append(ctx.s_ld.inc(sy.dma_start_transpose(
                pan[dc][:], h["xa_bf"].ap()[tok0:tok0 + 512, dc * 128:(dc + 1) * 128]), 16))
            ldv.append(ctx.s_ld.inc(sy.dma_start_transpose(
                wq[:, dc * 512:(dc + 1) * 512],
                h["st_wqkv"].ap()[0:512, dc * 128:(dc + 1) * 128]), 16))
            ldv.append(ctx.s_ld.inc(sy.dma_start_transpose(
                wq2[:, dc * 512:(dc + 1) * 512],
                h["st_wqkv2"].ap()[0:512, dc * 128:(dc + 1) * 128]), 16))
            ldv.append(ctx.s_ld.inc(sy.dma_start_transpose(
                wv[:, dc * 512:(dc + 1) * 512],
                h["st_wqkv"].ap()[512:1024, dc * 128:(dc + 1) * 128]), 16))
        v.wait_ge(ctx.s_ld.h, ldv[-1])
        for dc in range(8):
            last = v.tensor_scalar(pan[dc][:], pan[dc][:], gcol[:, dc:dc + 1], bcol[:, dc:dc + 1],
                                   ALU.mult, ALU.add)
        ctx.s_v.inc(last)
        ctx.s_v.wait(te)
        for c in range(4):
            ps = p_pr[c % 2]
            for dc in range(8):
                mm = te.matmul(ps[:], wq[:, dc * 512 + c * 128:dc * 512 + (c + 1) * 128],
                               pan[dc][:], start=(dc == 0), stop=(dc == 7))
            ctx.s_mm.inc(mm)
            ctx.s_mm.wait(v)
            ctx.s_v.inc(v.tensor_copy(qkT[c][:, 128 + tok0:128 + tok0 + 512], ps[:, 0:512]))
            if c >= 1:
                ctx.s_v.wait(te)
        for c in range(4):
            ps = p_pr[c % 2]
            ctx.s_v.wait(te)
            for dc in range(8):
                mm = te.matmul(ps[:], wq2[:, dc * 512 + c * 128:dc * 512 + (c + 1) * 128],
                               pan[dc][:], start=(dc == 0), stop=(dc == 7))
            ctx.s_mm.inc(mm)
            ctx.s_mm.wait(v)
            ctx.s_v.inc(v.tensor_copy(qk2T[c][:, 128 + tok0:128 + tok0 + 512], ps[:, 0:512]))
        for tt in range(4):
            ps = p_pr[tt % 2]
            ctx.s_v.wait(te)
            for dc in range(8):
                mm = te.matmul(ps[:], pan[dc][:, tt * 128:(tt + 1) * 128], wv[:, dc * 512:(dc + 1) * 512],
                               start=(dc == 0), stop=(dc == 7))
            ctx.s_mm.inc(mm)
            ctx.s_mm.wait(v)
            gt = (tok0 // 128) + tt
            ctx.s_v.inc(v.tensor_copy(vtm[:, gt * 512:(gt + 1) * 512], ps[:, 0:512]))
        ctx.s_v.wait(te)
        ctx.s_v.wait(sy)
    ctx.barrier()
    dump(0, qkT[0][:, 128:640])
    dump(1, qk2T[0][:, 128:640])
    dump(2, vtm[:, 0:512])
    ctx.barrier()

    x0ld = []
    for t in range(NT):
        b, w = t // 16, t % 16
        typ = 0 if w == 0 else (2 if w == 15 else 1)
        if t == 0:
            x0ld.append(ctx.s_ldr[0].inc(sy.dma_start(out=x0t[0][:], in_=h["x"].ap()[0:128, :]), 16))
        if t + 1 < NT:
            if t >= 1:
                ctx.s_v.wait(sy)
            x0ld.append(ctx.s_ldr[(t + 1) % 2].inc(
                sy.dma_start(out=x0t[(t + 1) % 2][:], in_=h["x"].ap()[(t + 1) * 128:(t + 2) * 128, :]), 16))
        for hp in range(4):
            qsl = qkT[hp][:, 128 + t * 128:128 + (t + 1) * 128]
            q2sl = qk2T[hp][:, 128 + t * 128:128 + (t + 1) * 128]
            v.tensor_tensor(rt1[:, 0:128], qsl, cq[:], ALU.mult)
            v.tensor_tensor(rt2[:, 0:128], q2sl, sq[:], ALU.mult)
            v.tensor_add(bq[hp][:], rt1[:, 0:128], rt2[:, 0:128])
            ksl = qkT[hp][:, t * 128:t * 128 + 384]
            k2sl = qk2T[hp][:, t * 128:t * 128 + 384]
            v.tensor_tensor(rt1[:], ksl, ck[:], ALU.mult)
            v.tensor_tensor(rt2[:], k2sl, sk[:], ALU.mult)
            last = v.tensor_add(bk[hp][:], rt1[:], rt2[:])
        ctx.s_v.inc(last)
        if t == 0:
            dump(3, bq[0][:], 128)
            dump(4, bk[0][:], 384)
        ctx.s_v.wait(te)
        for hd in range(8):
            hp, half = hd // 2, hd % 2
            ps = p_sim[hd % 2]
            r0 = 64 * half
            ctx.s_mm.inc(te.matmul(ps[:], bq[hp][r0:r0 + 64, :], bk[hp][r0:r0 + 64, :],
                                   start=True, stop=True))
            ctx.s_mm.wait(v)
            if t == 0 and hd == 0:
                dump(5, ps[:, 0:384], 384)
            v.tensor_tensor(tm[:], ps[:], msk[:, typ * 384:(typ + 1) * 384], ALU.add)
            ctx.s_v.inc(v.reduce_max(ssc[:, 0:1], tm[:], AX.X, negate=True))
            ctx.s_v.wait(sc)
            ctx.s_sc.inc(sc.activation(e32[:], tm[:], AF.Exp, bias=ssc[:, 0:1]))
            ctx.s_sc.wait(v)
            ctx.s_v.inc(v.reduce_sum(ssc[:, 1:2], e32[:], AX.X))
            ctx.s_v.wait(v)
            ctx.s_v.inc(v.reciprocal(ssc[:, 2:3], ssc[:, 1:2]))
            ctx.s_v.wait(v)
            ctx.s_v.inc(v.tensor_tensor(attn_bf[:], e32[:],
                                        ssc[:, 2:3].broadcast_to([128, 384]), ALU.mult))
            if t == 0 and hd == 0:
                dump(6, ssc[:, 0:4], 4)
                dump(7, attn_bf[:], 384)
                dump(8, e32[:], 384)
            ctx.s_v.wait(te)
            for kb in range(3):
                tr = te.transpose(p_at[:, kb * 128:(kb + 1) * 128], attn_bf[:, kb * 128:(kb + 1) * 128],
                                  identb[:])
            ctx.s_mm.inc(tr)
            ctx.s_mm.wait(v)
            ctx.s_v.inc(v.tensor_copy(attnT_bf[:], p_at[:]))
            ctx.s_v.wait(te)
            first = True
            for kb in range(3):
                kw = w - 1 + kb
                if kw < 0 or kw >= 16:
                    continue
                kt = b * 16 + kw
                mm = te.matmul(p_outT[r0:r0 + 64, hp * 128:(hp + 1) * 128],
                               vtm[:, kt * 512 + hd * 64:kt * 512 + (hd + 1) * 64],
                               attnT_bf[:, kb * 128:(kb + 1) * 128],
                               start=first, stop=(kb == 2 or (w == 15 and kb == 1)))
                first = False
            ctx.s_mm.inc(mm)
        ctx.s_mm.wait(v)
        ctx.s_v.inc(v.tensor_copy(outT_bf[:], p_outT[:]))
        if t == 0:
            dump(9, outT_bf[:])
        ctx.s_v.wait(te)
        for dh2 in range(2):
            for ic in range(4):
                mm = te.matmul(p_pr[dh2][:], outT_bf[:, ic * 128:(ic + 1) * 128],
                               wo[:, (ic * 2 + dh2) * 512:(ic * 2 + dh2 + 1) * 512],
                               start=(ic == 0), stop=(ic == 3))
        ctx.s_mm.inc(mm)
        v.wait_ge(ctx.s_ldr[t % 2].h, x0ld[t])
        ctx.s_mm.wait(v)
        if t == 0:
            dump(10, p_pr[0][:])
            dump(11, p_pr[1][:])
        v.tensor_add(x1t[:, 0:512], p_pr[0][:], x0t[t % 2][:, 0:512])
        ctx.s_v.inc(v.tensor_add(x1t[:, 512:1024], p_pr[1][:], x0t[t % 2][:, 512:1024]))
        ctx.s_v.wait(g)
        ctx.s_st.inc(g.dma_start(out=h["x1"].ap()[t * 128:(t + 1) * 128, :], in_=x1t[:]), 16)
        ctx.s_st.wait(v)
        ctx.s_v.wait(te)
    nc.gpsimd.wait_ge(ctx.s_st.h, ctx.s_st.n)


def phase_l0_ffn(ctx):
    nc = ctx.nc
    v, sc, g, sy, te = nc.vector, nc.scalar, nc.gpsimd, nc.sync, nc.tensor
    h = ctx.h
    identf = ctx.sbuf("identf", [128, 128], F32)
    pan = [ctx.sbuf(f"f0pan{i}", [128, 512], BF16) for i in range(8)]
    rwbf = ctx.sbuf("rwbf", [128, 8 * 8], BF16)
    rbb = ctx.sbuf("rbb", [128, 8], F32)
    w8 = ctx.sbuf("w8", [128, NT * 8], F32)
    w8T8 = ctx.sbuf("w8T8", [8, T], BF16)
    oh8 = ctx.sbuf("oh8", [8, 8 * 128], BF16)
    lg = ctx.sbuf("lg", [128, 8], F32)
    pe_ = ctx.sbuf("pe_", [128, 8], F32)
    mske = ctx.sbuf("mske", [128, 8], F32)
    mst = ctx.sbuf("mst", [128, 8], F32)
    stt = ctx.sbuf("stt", [128, 4], F32)
    p_r = ctx.psum("p_r", [128, 8])
    p_t8 = ctx.psum("p_t8", [8, 128])
    p_wb = ctx.psum("p_wb", [128, 512])
    p_h = [ctx.psum(f"p_h{i}", [128, 512]) for i in range(2)]
    p_o = [ctx.psum(f"p_o{i}", [128, 512]) for i in range(2)]
    rw1sb = [ctx.sbuf(f"rw1sb{i}", [128, 8 * 2048], BF16) for i in range(2)]
    rw2sb = [ctx.sbuf(f"rw2sb{i}", [128, 16 * 1024], BF16) for i in range(2)]
    h1sb = [ctx.sbuf(f"h1sb{i}", [128, 16 * 512], BF16) for i in range(2)]
    wbsb = [ctx.sbuf(f"wbsb{i}", [128, 512], F32) for i in range(2)]
    g1 = [ctx.sbuf(f"g1_{i}", [128, 512], F32) for i in range(2)]
    rbcol = ctx.sbuf("rbcol", [128, 2 * 16], F32)
    sbcol = ctx.sbuf("sbcol", [128, 16], F32)
    rb2bf = ctx.sbuf("rb2bf", [8, 1024], BF16)
    sb2bf = ctx.sbuf("sb2bf", [1, 1024], BF16)
    onesb = ctx.sbuf("onesb", [1, 128], BF16)
    acc = [ctx.sbuf(f"acc{i}", [128, D], F32) for i in range(3)]
    dbgt = ctx.sbuf("dbgtf", [128, 512], F32) if (ctx.dbg and "ffn0" in ctx.dbg) else None

    def dump(idx, src_ap, width=512, parts=128):
        if not (ctx.dbg and "ffn0" in ctx.dbg):
            return
        v.memset(dbgt[:], 0.0)
        ctx.s_v.inc(v.tensor_copy(dbgt[0:parts, 0:width], src_ap))
        ctx.s_v.wait(g)
        ctx.s_st.inc(g.dma_start(out=h["dbg0"].ap()[idx, :, :], in_=dbgt[:]), 16)
        ctx.s_st.wait(v)

    ctx.s_ld.inc(sy.dma_start(out=identf[:], in_=h["ident"].ap()), 16)
    ctx.s_ld.inc(sy.dma_start(out=rbb[:], in_=bass.AP(h["L0_ffn_router_b"], 0, [[0, 128], [1, 8]])), 16)
    for dc in range(8):
        ctx.s_cast.inc(g.dma_start(out=rwbf[:, dc * 8:(dc + 1) * 8],
                                   in_=bass.AP(h["L0_ffn_router_w"], dc * 128 * 8, [[8, 128], [1, 8]])), 16)
    ctx.s_cast.inc(g.dma_start(out=rb2bf[:], in_=h["L0_ffn_rb2"].ap()), 16)
    ctx.s_cast.inc(g.dma_start(out=sb2bf[:], in_=h["L0_ffn_sb2"].ap()), 16)
    v.memset(onesb[:], 1.0)
    with ExitStack() as ioh_es0:
        ioh = ioh_es0.enter_context(nc.sbuf_tensor("ioh", [8, 8], I32))
        nc.gpsimd.iota(ioh[:], [[1, 8]], base=0, channel_multiplier=-1)
        ctx.barrier()
        v.tensor_single_scalar(oh8.rearrange("p (b j) -> p b j", b=8),
                               ioh[:].unsqueeze(2).broadcast_to([8, 8, 128]), 0, ALU.is_equal)
        ctx.s_v.inc(v.memset(stt[:, 0:1], 0.0))
        ctx.barrier()

    for ch in range(8):
        tok0 = ch * 512
        ctx.s_mm.wait(sy)
        ldv = None
        for dc in range(8):
            ldv = ctx.s_ld.inc(sy.dma_start_transpose(
                pan[dc][:], h["xn0_bf"].ap()[tok0:tok0 + 512, dc * 128:(dc + 1) * 128]), 16)
        te.wait_ge(ctx.s_ld.h, ldv)
        for tt in range(4):
            t = ch * 4 + tt
            ctx.s_v.wait(te)
            for dc in range(8):
                mm = te.matmul(p_r[:], pan[dc][:, tt * 128:(tt + 1) * 128], rwbf[:, dc * 8:(dc + 1) * 8],
                               start=(dc == 0), stop=(dc == 7))
            ctx.s_mm.inc(mm)
            ctx.s_mm.wait(v)

            def vs(instr):
                ctx.s_v.inc(instr)
                ctx.s_v.wait(v)

            vs(v.tensor_tensor(lg[:], p_r[:], rbb[:], ALU.add))
            vs(v.reduce_max(stt[:, 0:1], lg[:], AX.X))
            ctx.s_v.inc(v.reduce_max(stt[:, 1:2], lg[:], AX.X, negate=True))
            ctx.s_v.wait(sc)
            ctx.s_sc.inc(sc.activation(pe_[:], lg[:], AF.Exp, bias=stt[:, 1:2]))
            ctx.s_sc.wait(v)
            if t == 1:
                dump(9, lg[:], 8)
            vs(v.reduce_sum(stt[:, 2:3], pe_[:], AX.X))
            vs(v.reciprocal(stt[:, 2:3], stt[:, 2:3]))
            vs(v.tensor_tensor(mst[:], lg[:], stt[:, 0:1].broadcast_to([128, 8]), ALU.is_ge))
            if t == 1:
                dump(10, stt[:, 0:4], 4)
                dump(12, mst[:], 8)
            vs(v.scalar_tensor_tensor(mske[:], mst[:], -1e30, lg[:], ALU.mult, ALU.add))
            if t == 1:
                dump(14, mske[:], 8)
            vs(v.reduce_max(stt[:, 3:4], mske[:], AX.X))
            vs(v.tensor_tensor(mske[:], lg[:], stt[:, 3:4].broadcast_to([128, 8]), ALU.is_ge))
            vs(v.tensor_mul(mst[:], pe_[:], mske[:]))
            ctx.s_v.inc(v.tensor_tensor(w8[:, t * 8:(t + 1) * 8], mst[:],
                                        stt[:, 2:3].broadcast_to([128, 8]), ALU.mult))
            ctx.s_v.wait(te)
            ctx.s_mm.inc(te.transpose(p_t8[:], w8[:, t * 8:(t + 1) * 8], identf[:]))
            ctx.s_mm.wait(v)
            ctx.s_v.inc(v.tensor_copy(w8T8[:, t * 128:(t + 1) * 128], p_t8[:]))
            if t == 1:
                dump(6, lg[:], 8)
                dump(7, pe_[:], 8)
                dump(8, mske[:], 8)
                dump(11, mst[:], 8)
    ctx.barrier()
    dump(0, w8[:, 0:32], 32)
    dump(1, w8T8[0:8, 0:512], 512, 8)
    ctx.barrier()

    def ffn_pass(pair_idx, experts, hdim, w1_src, w1_rows, b1_src, w2_src, w2_rows, b2=False, wb=True):
        nht = hdim // 128
        stg2_ctr = [0]
        for ei, (slot, _e) in enumerate(experts):
            for dc in range(8):
                ctx.s_cast.inc(g.dma_start(
                    out=rw1sb[slot][:, dc * hdim:(dc + 1) * hdim],
                    in_=bass.AP(w1_src, (w1_rows[ei] + dc * 128) * hdim, [[hdim, 128], [1, hdim]])), 16)
            for hc in range(nht):
                ctx.s_cast.inc(g.dma_start(
                    out=rw2sb[slot][:, hc * 1024:(hc + 1) * 1024],
                    in_=bass.AP(w2_src, (w2_rows[ei] + hc * 128) * 1024, [[1024, 128], [1, 1024]])), 16)
            for hc in range(nht):
                if b1_src is not None:
                    boff = (0 if _e is None else _e) * hdim + hc * 128
                    ctx.s_ld.inc(sy.dma_start(
                        out=rbcol[:, ei * 16 + hc:ei * 16 + hc + 1],
                        in_=bass.AP(b1_src, boff, [[1, 128], [1, 1]])), 16)
        g.wait_ge(ctx.s_cast.h, ctx.s_cast.n)
        sy.wait_ge(ctx.s_ld.h, ctx.s_ld.n)
        ctx.barrier()
        for ch in range(8):
            tok0 = ch * 512
            ctx.s_mm.wait(sy)
            ldv = None
            for dc in range(8):
                ldv = ctx.s_ld.inc(sy.dma_start_transpose(
                    pan[dc][:], h["xn0_bf"].ap()[tok0:tok0 + 512, dc * 128:(dc + 1) * 128]), 16)
            te.wait_ge(ctx.s_ld.h, ldv)
            v.wait_ge(ctx.s_ld.h, ldv)
            ctx.s_mm.wait(v)
            if wb:
                for ei, (slot, e) in enumerate(experts):
                    ctx.s_v.wait(te)
                    ctx.s_mm.inc(te.matmul(p_wb[:], oh8[:, e * 128:(e + 1) * 128],
                                           w8T8[:, tok0:tok0 + 512], start=True, stop=True))
                    ctx.s_mm.wait(v)
                    ctx.s_v.inc(v.tensor_copy(wbsb[slot][:], p_wb[:]))
                    if ch == 0 and pair_idx == 0 and ei == 0:
                        dump(2, wbsb[0][:])
            pslot_val = [0, 0]
            gslot_val = [0, 0]
            ii = 0
            for ei, (slot, e) in enumerate(experts):
                for ht in range(nht):
                    ps = p_h[ii % 2]
                    gbuf = g1[ii % 2]
                    if pslot_val[ii % 2]:
                        ctx.s_sc.wait(te, pslot_val[ii % 2])
                    for dc in range(8):
                        mm = te.matmul(ps[:], rw1sb[slot][:, dc * hdim + ht * 128:dc * hdim + (ht + 1) * 128],
                                       pan[dc][:], start=(dc == 0), stop=(dc == 7))
                    ctx.s_mm.inc(mm)
                    ctx.s_mm.wait(sc)
                    if gslot_val[ii % 2]:
                        ctx.s_v.wait(sc, gslot_val[ii % 2])
                    pslot_val[ii % 2] = ctx.s_sc.inc(
                        sc.activation(gbuf[:], ps[:], AF.Gelu, bias=rbcol[:, ei * 16 + ht:ei * 16 + ht + 1]))
                    ctx.s_sc.wait(v)
                    if wb:
                        gslot_val[ii % 2] = ctx.s_v.inc(
                            v.tensor_tensor(h1sb[slot][:, ht * 512:(ht + 1) * 512], gbuf[:],
                                            wbsb[slot][:], ALU.mult))
                    else:
                        gslot_val[ii % 2] = ctx.s_v.inc(
                            v.tensor_copy(h1sb[slot][:, ht * 512:(ht + 1) * 512], gbuf[:]))
                    ii += 1
            ctx.s_v.wait(te)
            for tt in range(4):
                po0, po1 = p_o[0], p_o[1]
                if tt >= 1:
                    ctx.s_v.wait(te)
                mms = []
                for ei, (slot, e) in enumerate(experts):
                    for hc in range(nht):
                        lh = h1sb[slot][:, hc * 512 + tt * 128:hc * 512 + (tt + 1) * 128]
                        mms.append((0, lh, rw2sb[slot][:, hc * 1024:hc * 1024 + 512]))
                        mms.append((1, lh, rw2sb[slot][:, hc * 1024 + 512:(hc + 1) * 1024]))
                if b2:
                    tcols = slice(tok0 + tt * 128, tok0 + (tt + 1) * 128)
                    mms.append((0, w8T8[:, tcols], rb2bf[:, 0:512]))
                    mms.append((1, w8T8[:, tcols], rb2bf[:, 512:1024]))
                    mms.append((0, onesb[:, 0:128], sb2bf[:, 0:512]))
                    mms.append((1, onesb[:, 0:128], sb2bf[:, 512:1024]))
                lastof = {0: max(i for i, m in enumerate(mms) if m[0] == 0),
                          1: max(i for i, m in enumerate(mms) if m[0] == 1)}
                seen = set()
                for i, (pi, lh, rh) in enumerate(mms):
                    ps = po0 if pi == 0 else po1
                    mm = te.matmul(ps[:], lh, rh, start=(pi not in seen), stop=(i == lastof[pi]))
                    seen.add(pi)
                ctx.s_mm.inc(mm)
                ctx.s_mm.wait(v)
                aslot = stg2_ctr[0] % 3
                ab = acc[aslot]
                ctx.s_accr[aslot].wait(v)
                v.tensor_copy(ab[:, 0:512], po0[:])
                ctx.s_v.inc(v.tensor_copy(ab[:, 512:1024], po1[:]))
                ctx.s_v.wait(g)
                t = ch * 4 + tt
                ctx.s_accr[aslot].inc(
                    g.dma_start(out=h["x1"].ap()[t * 128:(t + 1) * 128, :], in_=ab[:],
                                accum_op=ALU.add), 16)
                stg2_ctr[0] += 1
            ctx.s_v.wait(sy)
        g.wait_ge(ctx.s_st.h, ctx.s_st.n)
        for sr in ctx.s_accr:
            sr.wait(g)
        ctx.barrier()

    for pr in range(4):
        ffn_pass(pr, [(0, 2 * pr), (1, 2 * pr + 1)], 2048,
                 h["L0_ffn_rw1"], [2 * pr * 1024, (2 * pr + 1) * 1024], h["L0_ffn_rb1"],
                 h["L0_ffn_rw2"], [2 * pr * 2048, (2 * pr + 1) * 2048], b2=False, wb=True)
    ffn_pass(4, [(0, None)], 2048, h["L0_ffn_sw1"], [0], h["L0_ffn_sb1"],
             h["L0_ffn_sw2"], [0], b2=True, wb=False)
    nc.gpsimd.wait_ge(ctx.s_st.h, ctx.s_st.n)


def phase_l1_attn(ctx, stage="all"):
    nc = ctx.nc
    v, sc, g, sy, te = nc.vector, nc.scalar, nc.gpsimd, nc.sync, nc.tensor
    h = ctx.h
    SQRT_S = float(np.sqrt(128.0))
    S2 = ctx.sbuf("S2", [128, 4224], F32)
    import contextlib
    with ExitStack() as ses:
        old_es = ctx.es
        ctx.es = ses
        wr = ctx.sbuf("wr", [1, 1023], F32)
        we = ctx.sbuf("we", [1, 4352], F32)
        st0 = ctx.sbuf("stair0", [128, 4352], F32)
        st1 = ctx.sbuf("stair1", [128, 4352], F32)
        bits = ctx.sbuf("bits_sb", [128, 7], I32)
        ctx.s_ld.inc(sy.dma_start(out=wr[:], in_=h["w_rev"].ap()), 16)
        ctx.s_ld.inc(sy.dma_start(out=bits[:], in_=h["bits"].ap()), 16)
        sy.wait_ge(ctx.s_ld.h, ctx.s_ld.n)
        ctx.barrier()
        v.tensor_copy(we[0:1, 0:1536], wr[0:1, 0:1].broadcast_to([1, 1536]))
        v.tensor_copy(we[0:1, 1536:2559], wr[0:1, :])
        v.tensor_copy(we[0:1, 2559:3582], wr[0:1, :])
        ctx.s_v.inc(v.tensor_copy(we[0:1, 3582:4352], wr[0:1, 1022:1023].broadcast_to([1, 770])))
        ctx.s_v.wait(g)
        ctx.s_st.inc(g.dma_start(out=h["wext_rev"].ap(), in_=we[0:1, :]), 16)
        g.wait_ge(ctx.s_st.h, ctx.s_st.n)
        ctx.s_cast.inc(g.dma_start(out=st0[:], in_=bass.AP(h["wext_rev"], 0, [[0, 128], [1, 4352]])), 16)
        g.wait_ge(ctx.s_cast.h, ctx.s_cast.n)
        ctx.barrier()
        N = 4352
        for k in range(7):
            sh = 1 << k
            tin, tout = (st0, st1) if k % 2 == 0 else (st1, st0)
            v.select(tout[:, sh:N], bits[:, k:k + 1].broadcast_to([128, N - sh]),
                     tin[:, 0:N - sh], tin[:, sh:N])
        v.tensor_copy(S2[:], st1[:, 0:4224])
        ctx.barrier()
        ctx.es = old_es

    qT = ctx.sbuf("qTl1", [128, T], BF16)
    kT = ctx.sbuf("kTl1", [128, T], BF16)
    onesb = ctx.sbuf("onesl1", [1, 128], BF16)
    p_a = [ctx.psum(f"p_l1a{i}", [128, 512]) for i in range(2)]
    sub_es = ExitStack()
    old_es2 = ctx.es
    ctx.es = sub_es
    pan8 = [ctx.sbuf(f"xbpan{i}", [128, T], BF16) for i in range(8)]
    cos1 = ctx.sbuf("cos1", [128, SEQ], F32)
    sin1 = ctx.sbuf("sin1", [128, SEQ], F32)
    lwcol = ctx.sbuf("lwcol", [128, 8], F32)
    lbcol = ctx.sbuf("lbcol", [128, 8], F32)
    gbcol = ctx.sbuf("gbcol", [128, 4], F32)
    gbcol2 = ctx.sbuf("gbcol2", [128, 4], F32)
    uvbcol2 = ctx.sbuf("uvbcol2", [128, 1], F32)
    uvbcol = ctx.sbuf("uvbcol", [128, 33], F32)
    wt8 = ctx.sbuf("wt8", [128, 8 * 512], BF16)
    bsil = ctx.sbuf("bsil", [128, 512], F32)
    qpre = ctx.sbuf("qpre", [128, 512], F32)
    rt1 = ctx.sbuf("rt1l1", [128, 512], F32)
    rt2 = ctx.sbuf("rt2l1", [128, 512], F32)
    uvbrow = ctx.sbuf("uvbrow", [1, 2048], BF16)
    evbf = ctx.sbuf("evbf", [128, 512], BF16)

    v.memset(onesb[:], 1.0)
    ctx.s_ld.inc(sy.dma_start(out=cos1[:], in_=h["l1_cos"].ap()), 16)
    ctx.s_ld.inc(sy.dma_start(out=sin1[:], in_=h["l1_sin"].ap()), 16)
    for dc in range(8):
        ctx.s_ld.inc(sy.dma_start(out=lwcol[:, dc:dc + 1],
                                  in_=bass.AP(h["L1_attn_ln_w"], dc * 128, [[1, 128], [1, 1]])), 16)
        ctx.s_ld.inc(sy.dma_start(out=lbcol[:, dc:dc + 1],
                                  in_=bass.AP(h["L1_attn_ln_b"], dc * 128, [[1, 128], [1, 1]])), 16)
    for i in range(2):
        ctx.s_ld.inc(sy.dma_start(out=gbcol[:, 2 * i:2 * i + 1],
                                  in_=bass.AP(h["L1_attn_gamma"], i * 128, [[1, 128], [1, 1]])), 16)
        ctx.s_ld.inc(sy.dma_start(out=gbcol[:, 2 * i + 1:2 * i + 2],
                                  in_=bass.AP(h["L1_attn_beta"], i * 128, [[1, 128], [1, 1]])), 16)
        for half in range(2):
            ctx.s_ld.inc(sy.dma_start(out=gbcol2[half * 64:(half + 1) * 64, 2 * i:2 * i + 1],
                                      in_=bass.AP(h["L1_attn_gamma"], i * 128 + (1 - half) * 64,
                                                  [[1, 64], [1, 1]])), 16)
            ctx.s_ld.inc(sy.dma_start(out=gbcol2[half * 64:(half + 1) * 64, 2 * i + 1:2 * i + 2],
                                      in_=bass.AP(h["L1_attn_beta"], i * 128 + (1 - half) * 64,
                                                  [[1, 64], [1, 1]])), 16)
    for half in range(2):
        ctx.s_ld.inc(sy.dma_start(out=uvbcol2[half * 64:(half + 1) * 64, 0:1],
                                  in_=bass.AP(h["L1_attn_uv_b"], 4096 + (1 - half) * 64,
                                              [[1, 64], [1, 1]])), 16)
    for ft in range(33):
        ctx.s_ld.inc(sy.dma_start(out=uvbcol[:, ft:ft + 1],
                                  in_=bass.AP(h["L1_attn_uv_b"], ft * 128, [[1, 128], [1, 1]])), 16)
    pldv = None
    for dc in range(8):
        for p in range(8):
            pldv = ctx.s_ld.inc(sy.dma_start_transpose(
                pan8[dc][:, p * 512:(p + 1) * 512],
                h["xb_bf"].ap()[p * 512:(p + 1) * 512, dc * 128:(dc + 1) * 128]), 16)
    v.wait_ge(ctx.s_ld.h, ctx.s_ld.n)
    for dc in range(8):
        last = v.tensor_scalar(pan8[dc][:], pan8[dc][:], lwcol[:, dc:dc + 1], lbcol[:, dc:dc + 1],
                               ALU.mult, ALU.add)
    ctx.s_v.inc(last)
    ctx.barrier()

    wtb = ctx.sbuf("wtb", [128, 8 * 128], BF16)
    wtb2 = ctx.sbuf("wtb2", [128, 8 * 128], BF16)
    bsil2 = ctx.sbuf("bsil2", [128, 512], F32)
    qpre2 = ctx.sbuf("qpre2", [128, 512], F32)
    for dc in range(8):
        ctx.s_ld.inc(sy.dma_start_transpose(
            wtb[:, dc * 128:(dc + 1) * 128],
            h["st_uvw"].ap()[4096:4224, dc * 128:(dc + 1) * 128]), 16)
        ctx.s_ld.inc(sy.dma_start_transpose(
            wtb2[:, dc * 128:(dc + 1) * 128],
            h["st_uvw2"].ap()[0:128, dc * 128:(dc + 1) * 128]), 16)
    te.wait_ge(ctx.s_ld.h, ctx.s_ld.n)
    for s in range(2):
        for p in range(4):
            gp = s * 4 + p
            ps, ps2 = p_a[0], p_a[1]
            cs = cos1[:, p * 512:(p + 1) * 512]
            sn = sin1[:, p * 512:(p + 1) * 512]
            ctx.s_v.wait(te)
            for dc in range(8):
                te.matmul(ps[:], wtb[:, dc * 128:(dc + 1) * 128],
                          pan8[dc][:, gp * 512:(gp + 1) * 512], start=(dc == 0), stop=(dc == 7))
            for dc in range(8):
                mm = te.matmul(ps2[:], wtb2[:, dc * 128:(dc + 1) * 128],
                               pan8[dc][:, gp * 512:(gp + 1) * 512], start=(dc == 0), stop=(dc == 7))
            ctx.s_mm.inc(mm)
            ctx.s_mm.wait(sc)
            sc.activation(bsil[:], ps[:], AF.Silu, bias=uvbcol[:, 32:33])
            ctx.s_sc.inc(sc.activation(bsil2[:], ps2[:], AF.Silu, bias=uvbcol2[:, 0:1]))
            ctx.s_sc.wait(v)
            cols = slice(gp * 512, (gp + 1) * 512)
            for qi in range(2):
                v.tensor_scalar(qpre[:], bsil[:], gbcol[:, 2 * qi:2 * qi + 1],
                                gbcol[:, 2 * qi + 1:2 * qi + 2], ALU.mult, ALU.add)
                v.tensor_scalar(qpre2[:], bsil2[:], gbcol2[:, 2 * qi:2 * qi + 1],
                                gbcol2[:, 2 * qi + 1:2 * qi + 2], ALU.mult, ALU.add)
                dst = qT if qi == 0 else kT
                v.tensor_tensor(rt1[:], qpre[:], cs, ALU.mult)
                v.tensor_tensor(rt2[:], qpre2[:], sn, ALU.mult)
                last = v.tensor_add(dst[:, cols], rt1[:], rt2[:])
            ctx.s_v.inc(last)
    ctx.barrier()

    for fg in range(4):
        ctx.s_mm.wait(sy)
        ldv = None
        for dc in range(8):
            ldv = ctx.s_ld.inc(sy.dma_start_transpose(
                wt8[:, dc * 512:(dc + 1) * 512],
                h["st_uvw"].ap()[fg * 512:(fg + 1) * 512, dc * 128:(dc + 1) * 128]), 16)
        te.wait_ge(ctx.s_ld.h, ldv)
        for ft in range(4):
            for gp in range(8):
                ps = p_a[gp % 2]
                ctx.s_sc.wait(te)
                for dc in range(8):
                    mm = te.matmul(ps[:], wt8[:, dc * 512 + ft * 128:dc * 512 + (ft + 1) * 128],
                                   pan8[dc][:, gp * 512:(gp + 1) * 512], start=(dc == 0), stop=(dc == 7))
                ctx.s_mm.inc(mm)
                ctx.s_mm.wait(sc)
                ctx.s_sc.inc(sc.activation(evbf[:], ps[:], AF.Silu, bias=uvbcol[:, fg * 4 + ft:fg * 4 + ft + 1]))
                ctx.s_sc.wait(g)
                ctx.s_st.inc(g.dma_start(
                    out=h["uT"].ap()[(fg * 4 + ft) * 128:(fg * 4 + ft + 1) * 128, gp * 512:(gp + 1) * 512],
                    in_=evbf[:]), 16)
                ctx.s_st.wait(sc)
    for vb in range(4):
        ctx.s_mm.wait(sy)
        ldv = None
        for dc in range(8):
            ldv = ctx.s_ld.inc(sy.dma_start_transpose(
                wt8[:, dc * 512:(dc + 1) * 512],
                h["st_uvw"].ap()[2048 + vb * 512:2048 + (vb + 1) * 512, dc * 128:(dc + 1) * 128]), 16)
        ctx.s_cast.inc(g.dma_start(out=uvbrow[0:1, 0:512],
                                   in_=bass.AP(h["L1_attn_uv_b"], 2048 + vb * 512, [[1, 1], [1, 512]])), 16)
        te.wait_ge(ctx.s_ld.h, ldv)
        te.wait_ge(ctx.s_cast.h, ctx.s_cast.n)
        for tt in range(NT):
            ps = p_a[tt % 2]
            ctx.s_sc.wait(te)
            for dc in range(8):
                te.matmul(ps[:], pan8[dc][:, tt * 128:(tt + 1) * 128], wt8[:, dc * 512:(dc + 1) * 512],
                          start=(dc == 0), stop=False)
            mm = te.matmul(ps[:], onesb[:, 0:128], uvbrow[0:1, 0:512], start=False, stop=True)
            ctx.s_mm.inc(mm)
            ctx.s_mm.wait(sc)
            ctx.s_sc.inc(sc.activation(evbf[:], ps[:], AF.Silu))
            ctx.s_sc.wait(g)
            ctx.s_st.inc(g.dma_start(
                out=h["v_tm"].ap()[tt * 128:(tt + 1) * 128, vb * 512:(vb + 1) * 512], in_=evbf[:]), 16)
            ctx.s_st.wait(sc)
    nc.gpsimd.wait_ge(ctx.s_st.h, ctx.s_st.n)
    ctx.barrier()
    sub_es.close()
    ctx.es = old_es2
    if stage == "s2":
        sc.copy(dbg_s2a := None, None) if False else None
        kfa = ctx.sbuf("kfa", [128, 512], F32)
        for blk in range(6):
            ctx.s_v.wait(g)
            ctx.s_v.inc(v.tensor_copy(kfa[:], S2[:, 1024 + blk * 512:1024 + (blk + 1) * 512]))
            ctx.s_v.wait(g)
            ctx.s_st.inc(g.dma_start(out=h["out"].ap()[blk * 128:(blk + 1) * 128, 0:512], in_=kfa[:]), 16)
            ctx.s_st.wait(v)
        nc.gpsimd.wait_ge(ctx.s_st.h, ctx.s_st.n)
        return
    if stage == "uv":
        return

    vtm_s = ctx.sbuf("vtm_s", [128, 16 * 2048], BF16)
    owT = ctx.sbuf("owT", [128, 16 * 1024], BF16)
    obrow = ctx.sbuf("obrow", [1, 1024], BF16)
    uts = ctx.sbuf("uts", [128, 16 * 512], BF16)
    kernT = ctx.sbuf("kernT", [128, 16 * 512], BF16)
    out2T = ctx.sbuf("out2T", [128, 16 * 512], BF16)
    kf = [ctx.sbuf(f"kf_{i}", [128, 512], F32) for i in range(2)]
    kf2 = [ctx.sbuf(f"kf2_{i}", [128, 512], F32) for i in range(2)]
    x1t2 = [ctx.sbuf(f"x1t2_{i}", [128, D], F32) for i in range(2)]
    xn1t2 = [ctx.sbuf(f"xn1t2_{i}", [128, D], F32) for i in range(2)]
    accv = ctx.sbuf("accv", [128, D], F32)
    p_pr = [ctx.psum(f"p_l1pr{i}", [128, 512]) for i in range(2)]

    for es in range(16):
        for dh2 in range(2):
            ctx.s_ld.inc(sy.dma_start_transpose(
                owT[:, es * 1024 + dh2 * 512:es * 1024 + (dh2 + 1) * 512],
                h["st_ow"].ap()[dh2 * 512:(dh2 + 1) * 512, es * 128:(es + 1) * 128]), 16)
    ctx.s_cast.inc(g.dma_start(out=obrow[:], in_=h["L1_attn_o_b"].ap()), 16)

    nseq = 1 if stage in ("sim", "att") else 2
    for s in range(nseq):
        ctx.s_mm.wait(sy)
        ldvt = None
        for kt in range(16):
            ldvt = ctx.s_ld.inc(sy.dma_start(
                out=vtm_s[:, kt * 2048:(kt + 1) * 2048],
                in_=h["v_tm"].ap()[s * 2048 + kt * 128:s * 2048 + (kt + 1) * 128, :]), 16)
        for qp in range(1 if stage in ("sim", "att") else 4):
            ldu = None
            for es in range(16):
                ldu = ctx.s_ld.inc(sy.dma_start(
                    out=uts[:, es * 512:(es + 1) * 512],
                    in_=h["uT"].ap()[es * 128:(es + 1) * 128, s * 2048 + qp * 512:s * 2048 + (qp + 1) * 512]), 16)
            te.wait_ge(ctx.s_ld.h, ldvt)
            te.wait_ge(ctx.s_ld.h, ldu)
            kslot_val = [0, 0]
            ksq_val = [0, 0]
            for kt in range(16):
                sl = kt % 2
                ps = p_a[sl]
                if kslot_val[sl]:
                    ctx.s_v.wait(te, kslot_val[sl])
                ctx.s_mm.inc(te.matmul(ps[:], kT[:, s * 2048 + kt * 128:s * 2048 + (kt + 1) * 128],
                                       qT[:, s * 2048 + qp * 512:s * 2048 + (qp + 1) * 512],
                                       start=True, stop=True))
                ctx.s_mm.wait(v)
                m0 = 2047 - kt * 128 + qp * 512
                ctx.s_sc.wait(v, ksq_val[sl]) if ksq_val[sl] else None
                v.scalar_tensor_tensor(kf[sl][:], ps[:], 1.0 / SQRT_S, S2[:, m0:m0 + 512],
                                       ALU.mult, ALU.add)
                kslot_val[sl] = ctx.s_v.inc(v.tensor_scalar_max(kf2[sl][:], kf[sl][:], 0.0))
                ctx.s_v.wait(sc)
                ksq_val[sl] = ctx.s_sc.inc(
                    sc.activation(kernT[:, kt * 512:(kt + 1) * 512], kf2[sl][:], AF.Square))
            ctx.s_sc.wait(te)
            if stage == "sim":
                ctx.s_sc.wait(v)
                ctx.s_v.inc(v.tensor_copy(kf[:], kernT[:, 0:512]))
                ctx.s_v.inc(v.tensor_copy(kf2[:], kernT[:, 512:1024]))
                ctx.s_v.wait(g)
                ctx.s_st.inc(g.dma_start(out=h["out"].ap()[0:128, 0:512], in_=kf[:]), 16)
                ctx.s_st.inc(g.dma_start(out=h["out"].ap()[0:128, 512:1024], in_=kf2[:]), 16)
                nc.gpsimd.wait_ge(ctx.s_st.h, ctx.s_st.n)
                return
            aslot_val = [0, 0]
            for es in range(16):
                sl = es % 2
                ps = p_a[sl]
                if es < 2:
                    ctx.s_v.wait(te)
                elif aslot_val[sl]:
                    ctx.s_v.wait(te, aslot_val[sl])
                for kt in range(16):
                    mm = te.matmul(ps[:], vtm_s[:, kt * 2048 + es * 128:kt * 2048 + (es + 1) * 128],
                                   kernT[:, kt * 512:(kt + 1) * 512], start=(kt == 0), stop=(kt == 15))
                ctx.s_mm.inc(mm)
                ctx.s_mm.wait(v)
                aslot_val[sl] = ctx.s_v.inc(v.tensor_tensor(out2T[:, es * 512:(es + 1) * 512], ps[:],
                                                            uts[:, es * 512:(es + 1) * 512], ALU.mult))
            ctx.s_v.wait(te)
            if stage == "att":
                ctx.s_v.wait(g)
                for i4 in range(2):
                    ctx.s_st.inc(g.dma_start(out=h["out"].ap()[128 + i4 * 128:256 + i4 * 128, 0:1024],
                                             in_=out2T[:, i4 * 1024:(i4 + 1) * 1024]), 16)
                nc.gpsimd.wait_ge(ctx.s_st.h, ctx.s_st.n)
                return
            for qs in range(4):
                tglob = s * 16 + qp * 4 + qs
                buf = qs % 2
                ctx.s_v.wait(sy)
                l1 = ctx.s_ld.inc(sy.dma_start(out=x1t2[buf][:],
                                               in_=h["x1"].ap()[tglob * 128:(tglob + 1) * 128, :]), 16)
                l2 = ctx.s_ld.inc(sy.dma_start(out=xn1t2[buf][:],
                                               in_=h["xn1_tm"].ap()[tglob * 128:(tglob + 1) * 128, :]), 16)
                mms = []
                for es in range(16):
                    lh = out2T[:, es * 512 + qs * 128:es * 512 + (qs + 1) * 128]
                    mms.append((0, lh, owT[:, es * 1024:es * 1024 + 512]))
                    mms.append((1, lh, owT[:, es * 1024 + 512:(es + 1) * 1024]))
                mms.append((0, onesb[:, 0:128], obrow[0:1, 0:512]))
                mms.append((1, onesb[:, 0:128], obrow[0:1, 512:1024]))
                te.wait_ge(ctx.s_cast.h, ctx.s_cast.n)
                ctx.s_v.wait(te)
                seen = set()
                lastof = {0: max(i for i, m in enumerate(mms) if m[0] == 0),
                          1: max(i for i, m in enumerate(mms) if m[0] == 1)}
                for i, (pi, lh, rh) in enumerate(mms):
                    ps = p_pr[pi]
                    mm = te.matmul(ps[:], lh, rh, start=(pi not in seen), stop=(i == lastof[pi]))
                    seen.add(pi)
                ctx.s_mm.inc(mm)
                v.wait_ge(ctx.s_ld.h, l2)
                ctx.s_mm.wait(v)
                v.tensor_add(accv[:], x1t2[buf][:], xn1t2[buf][:])
                v.tensor_add(accv[:, 0:512], accv[:, 0:512], p_pr[0][:])
                ctx.s_v.inc(v.tensor_add(accv[:, 512:1024], accv[:, 512:1024], p_pr[1][:]))
                ctx.s_v.wait(g)
                ctx.s_st.inc(g.dma_start(out=h["out"].ap()[tglob * 128:(tglob + 1) * 128, :],
                                         in_=accv[:]), 16)
                ctx.s_st.wait(v)
            ctx.s_v.wait(sy)
    nc.gpsimd.wait_ge(ctx.s_st.h, ctx.s_st.n)


def phase_l1_ffn(ctx):
    nc = ctx.nc
    v, sc, g, sy, te = nc.vector, nc.scalar, nc.gpsimd, nc.sync, nc.tensor
    h = ctx.h
    identf = ctx.sbuf("identf1", [128, 128], F32)
    pan = [ctx.sbuf(f"f1pan{i}", [128, 1024], BF16) for i in range(8)]
    gwT = ctx.sbuf("gwT", [128, 8 * 32], BF16)
    gbb = ctx.sbuf("gbb", [128, 32], F32)
    lgch = ctx.sbuf("lgch", [128, 128], F32)
    sc_ = ctx.sbuf("sc_", [128, 128], F32)
    scb = ctx.sbuf("scb", [128, 128], F32)
    mwrk = ctx.sbuf("mwrk", [128, 128], F32)
    tcur = ctx.sbuf("tcur", [128, 128], F32)
    gstat = ctx.sbuf("gstat", [128, 64], F32)
    gstat2 = ctx.sbuf("gstat2", [128, 16], F32)
    mx8 = ctx.sbuf("mx8", [128, 32], F32)
    w32 = ctx.sbuf("w32ch", [128, 128], F32)
    w32T = ctx.sbuf("w32T", [32, T], BF16)
    oh32 = ctx.sbuf("oh32", [32, 32 * 128], BF16)
    onesb = ctx.sbuf("onesf1", [1, 128], BF16)
    p_wb = ctx.psum("p_wbl1", [128, 512])
    p_g = p_wb
    p_t32 = p_wb
    p_h = [ctx.psum(f"p_l1h{i}", [128, 512]) for i in range(2)]
    p_u = [ctx.psum(f"p_l1u{i}", [128, 512]) for i in range(2)]
    p_o = [ctx.psum(f"p_l1o{i}", [128, 512]) for i in range(2)]

    v.memset(onesb[:], 1.0)
    ctx.s_ld.inc(sy.dma_start(out=identf[:], in_=h["ident"].ap()), 16)
    ctx.s_ld.inc(sy.dma_start(out=gbb[:], in_=bass.AP(h["L1_ffn_gate_bias"], 0, [[0, 128], [1, 32]])), 16)
    for dc in range(8):
        ctx.s_ld.inc(sy.dma_start_transpose(
            gwT[:, dc * 32:(dc + 1) * 32],
            h["st_gw"].ap()[0:32, dc * 128:(dc + 1) * 128]), 16)
    with ExitStack() as ioh_es:
        ioh = ioh_es.enter_context(nc.sbuf_tensor("ioh32", [32, 32], I32))
        nc.gpsimd.iota(ioh[:], [[1, 32]], base=0, channel_multiplier=-1)
        ctx.barrier()
        ctx.s_v.inc(v.tensor_single_scalar(oh32.rearrange("p (b j) -> p b j", b=32),
                                           ioh[:].unsqueeze(2).broadcast_to([32, 32, 128]),
                                           0, ALU.is_equal))
        ctx.barrier()

    inv28 = 1.0
    BIGN = 1e30
    for ch in range(8):
        tok0 = ch * 512
        ctx.s_mm.wait(sy)
        ldv = None
        for dc in range(8):
            ldv = ctx.s_ld.inc(sy.dma_start_transpose(
                pan[dc][:, 0:512], h["xn1_bf"].ap()[tok0:tok0 + 512, dc * 128:(dc + 1) * 128]), 16)
        te.wait_ge(ctx.s_ld.h, ldv)
        for tt in range(4):
            ctx.s_v.wait(te)
            for dc in range(8):
                mm = te.matmul(p_g[:, 0:32], pan[dc][:, tt * 128:(tt + 1) * 128], gwT[:, dc * 32:(dc + 1) * 32],
                               start=(dc == 0), stop=(dc == 7))
            ctx.s_mm.inc(mm)
            ctx.s_mm.wait(v)
            ctx.s_v.inc(v.tensor_copy(lgch[:, tt * 32:(tt + 1) * 32], p_g[:, 0:32]))
        ctx.s_v.wait(sc)
        ctx.s_sc.inc(sc.activation(sc_[:], lgch[:], AF.Sigmoid))
        ctx.s_sc.wait(v)

        def vs(instr):
            ctx.s_v.inc(instr)
            ctx.s_v.wait(v)

        gbbv = bass.AP(gbb.handle if hasattr(gbb, "handle") else gbb, 0, [[32, 128], [0, 4], [1, 32]])
        vs(v.tensor_tensor(scb.rearrange("p (t e) -> p t e", t=4), sc_.rearrange("p (t e) -> p t e", t=4),
                           gbbv, ALU.add))
        vw4 = scb.rearrange("p (t g e) -> p t g e", t=4, g=4)
        vs(v.reduce_max(gstat[:, 0:16].rearrange("p (t g) -> p t g", t=4), vw4, AX.X))
        m1bc = gstat[:, 0:16].rearrange("p (t g) -> p t g", t=4).unsqueeze(3).broadcast_to([128, 4, 4, 8])
        vs(v.tensor_tensor(tcur.rearrange("p (t g e) -> p t g e", t=4, g=4), vw4, m1bc, ALU.is_ge))
        vs(v.scalar_tensor_tensor(mwrk[:], tcur[:], -BIGN, scb[:], ALU.mult, ALU.add))
        vs(v.reduce_max(gstat[:, 16:32].rearrange("p (t g) -> p t g", t=4),
                        mwrk.rearrange("p (t g e) -> p t g e", t=4, g=4), AX.X))
        vs(v.tensor_add(gstat[:, 32:48], gstat[:, 0:16], gstat[:, 16:32]))
        gsv = gstat[:, 32:48].rearrange("p (t g) -> p t g", t=4)
        vs(v.reduce_max(gstat2[:, 0:4].rearrange("p (t x) -> p t x", t=4), gsv, AX.X))
        m1Gbc = gstat2[:, 0:4].rearrange("p (t x) -> p t x", t=4).broadcast_to([128, 4, 4])
        vs(v.tensor_tensor(gstat[:, 0:16].rearrange("p (t g) -> p t g", t=4), gsv, m1Gbc, ALU.is_ge))
        vs(v.scalar_tensor_tensor(gstat[:, 48:64], gstat[:, 0:16], -BIGN, gstat[:, 32:48],
                                  ALU.mult, ALU.add))
        vs(v.reduce_max(gstat2[:, 4:8].rearrange("p (t x) -> p t x", t=4),
                        gstat[:, 48:64].rearrange("p (t g) -> p t g", t=4), AX.X))
        m2Gbc = gstat2[:, 4:8].rearrange("p (t x) -> p t x", t=4).broadcast_to([128, 4, 4])
        vs(v.tensor_tensor(gstat[:, 0:16].rearrange("p (t g) -> p t g", t=4), gsv, m2Gbc, ALU.is_ge))
        gmaskbc = gstat[:, 0:16].rearrange("p (t g) -> p t g", t=4).unsqueeze(3).broadcast_to([128, 4, 4, 8])
        vs(v.tensor_tensor(mwrk.rearrange("p (t g e) -> p t g e", t=4, g=4), vw4, gmaskbc, ALU.mult))
        vs(v.tensor_copy(tcur[:], mwrk[:]))
        for it in range(8):
            vs(v.reduce_max(mx8[:, it * 4:(it + 1) * 4].rearrange("p (t x) -> p t x", t=4),
                            tcur.rearrange("p (t e) -> p t e", t=4), AX.X))
            if it < 7:
                mxbc = mx8[:, it * 4:(it + 1) * 4].rearrange("p (t x) -> p t x", t=4).broadcast_to([128, 4, 32])
                vs(v.tensor_tensor(w32.rearrange("p (t e) -> p t e", t=4),
                                   tcur.rearrange("p (t e) -> p t e", t=4), mxbc, ALU.is_ge))
                vs(v.scalar_tensor_tensor(scb[:], w32[:], -BIGN, tcur[:], ALU.mult, ALU.add))
                vs(v.tensor_copy(tcur[:], scb[:]))
        t8bc = mx8[:, 28:32].rearrange("p (t x) -> p t x", t=4).broadcast_to([128, 4, 32])
        vs(v.tensor_tensor(w32.rearrange("p (t e) -> p t e", t=4),
                           mwrk.rearrange("p (t e) -> p t e", t=4), t8bc, ALU.is_ge))
        vs(v.tensor_mul(tcur[:], w32[:], sc_[:]))
        vs(v.reduce_sum(gstat2[:, 8:12].rearrange("p (t x) -> p t x", t=4),
                        tcur.rearrange("p (t e) -> p t e", t=4), AX.X))
        vs(v.tensor_scalar_add(gstat2[:, 0:4], gstat2[:, 8:12], 1e-20))
        vs(v.reciprocal(gstat2[:, 12:16], gstat2[:, 0:4]))
        rbc = gstat2[:, 12:16].rearrange("p (t x) -> p t x", t=4).broadcast_to([128, 4, 32])
        ctx.s_v.inc(v.scalar_tensor_tensor(w32.rearrange("p (t e) -> p t e", t=4),
                                           tcur.rearrange("p (t e) -> p t e", t=4), 2.5, rbc,
                                           ALU.mult, ALU.mult))
        for tt in range(4):
            t = ch * 4 + tt
            ctx.s_v.wait(te)
            ctx.s_mm.inc(te.transpose(p_t32[0:32, 0:128], w32[:, tt * 32:(tt + 1) * 32], identf[:]))
            ctx.s_mm.wait(v)
            ctx.s_v.inc(v.tensor_copy(w32T[:, t * 128:(t + 1) * 128], p_t32[0:32, 0:128]))
    ctx.barrier()

    egT = [ctx.sbuf(f"egT{i}", [128, 4096], BF16) for i in range(4)]
    euT = [ctx.sbuf(f"euT{i}", [128, 4096], BF16) for i in range(4)]
    edT = [ctx.sbuf(f"edT{i}", [128, 4096], BF16) for i in range(4)]
    hsb = [ctx.sbuf(f"l1h{i}", [128, 4096], BF16) for i in range(4)]

    def _sl(bufs, slot, idx, width):
        t = bufs[slot + idx // 4096]
        c = idx % 4096
        return t[:, c:c + width]
    wbsb = ctx.sbuf("wbsbl1", [128, 512], F32)
    sg1 = [ctx.sbuf(f"sg1_{i}", [128, 512], F32) for i in range(2)]
    ug1 = [ctx.sbuf(f"ug1_{i}", [128, 512], F32) for i in range(2)]
    acc1 = [ctx.sbuf(f"accl1{i}", [128, D], F32) for i in range(4)]

    def grp_pass(experts, I, w1g, w1u, w2, wb=True):
        nit = I // 128
        stg2c = [0]
        for ei, (slot, e) in enumerate(experts):
            r1 = (e or 0) * I
            for dc in range(8):
                ctx.s_ld.inc(sy.dma_start_transpose(
                    _sl(egT, slot, dc * I, I),
                    w1g.ap()[r1:r1 + I, dc * 128:(dc + 1) * 128]), 16)
                ctx.s_ld.inc(sy.dma_start_transpose(
                    _sl(euT, slot, dc * I, I),
                    w1u.ap()[r1:r1 + I, dc * 128:(dc + 1) * 128]), 16)
            r2 = (e or 0) * 1024
            for ic in range(nit):
                for dh2 in range(2):
                    ctx.s_ld.inc(sy.dma_start_transpose(
                        _sl(edT, slot, ic * 1024 + dh2 * 512, 512),
                        w2.ap()[r2 + dh2 * 512:r2 + (dh2 + 1) * 512, ic * 128:(ic + 1) * 128]), 16)
        sy.wait_ge(ctx.s_ld.h, ctx.s_ld.n)
        ctx.barrier()
        for ch in range(4):
            tok0 = ch * 1024
            ctx.s_mm.wait(sy)
            ldv = None
            for dc in range(8):
                ldv = ctx.s_ld.inc(sy.dma_start_transpose(
                    pan[dc][:], h["xn1_bf"].ap()[tok0:tok0 + 1024, dc * 128:(dc + 1) * 128]), 16)
            te.wait_ge(ctx.s_ld.h, ldv)
            v.wait_ge(ctx.s_ld.h, ldv)
            ctx.s_mm.wait(v)
            ctx.s_v.wait(te)
            pslot_val = [0, 0]
            ii = 0
            for ei, (slot, e) in enumerate(experts):
                for half in range(2):
                    if wb:
                        ctx.s_v.wait(te)
                        ctx.s_mm.inc(te.matmul(p_wb[:], oh32[:, e * 128:(e + 1) * 128],
                                               w32T[:, tok0 + half * 512:tok0 + (half + 1) * 512],
                                               start=True, stop=True))
                        ctx.s_mm.wait(v)
                        ctx.s_v.inc(v.tensor_copy(wbsb[:], p_wb[:]))
                    for it in range(nit):
                        sl = ii % 2
                        psg, psu = p_h[sl], p_u[sl]
                        if pslot_val[sl]:
                            ctx.s_v.wait(te, pslot_val[sl])
                        for dc in range(8):
                            te.matmul(psg[:], _sl(egT, slot, dc * I + it * 128, 128),
                                      pan[dc][:, half * 512:(half + 1) * 512],
                                      start=(dc == 0), stop=(dc == 7))
                        for dc in range(8):
                            mm = te.matmul(psu[:], _sl(euT, slot, dc * I + it * 128, 128),
                                           pan[dc][:, half * 512:(half + 1) * 512],
                                           start=(dc == 0), stop=(dc == 7))
                        ctx.s_mm.inc(mm)
                        ctx.s_mm.wait(sc)
                        ctx.s_mm.wait(v)
                        ctx.s_sc.inc(sc.activation(sg1[sl][:], psg[:], AF.Silu))
                        ctx.s_sc.wait(v)
                        if wb:
                            v.tensor_tensor(ug1[sl][:], psu[:], wbsb[:], ALU.mult)
                        else:
                            v.tensor_copy(ug1[sl][:], psu[:])
                        pslot_val[sl] = ctx.s_v.inc(v.tensor_tensor(
                            _sl(hsb, slot, it * 1024 + half * 512, 512),
                            sg1[sl][:], ug1[sl][:], ALU.mult))
                        ii += 1
            ctx.s_v.wait(te)
            for tt in range(8):
                pslot = tt % 2
                po0, po1 = p_o[pslot], p_h[pslot]
                if tt >= 2:
                    ctx.s_v.wait(te)
                mms = []
                for ei, (slot, e) in enumerate(experts):
                    for ic in range(nit):
                        lh = _sl(hsb, slot, ic * 1024 + tt * 128, 128)
                        mms.append((0, lh, _sl(edT, slot, ic * 1024, 512)))
                        mms.append((1, lh, _sl(edT, slot, ic * 1024 + 512, 512)))
                seen = set()
                lastof = {0: max(i for i, m in enumerate(mms) if m[0] == 0),
                          1: max(i for i, m in enumerate(mms) if m[0] == 1)}
                for i, (pi, lh, rh) in enumerate(mms):
                    ps = po0 if pi == 0 else po1
                    mm = te.matmul(ps[:], lh, rh, start=(pi not in seen), stop=(i == lastof[pi]))
                    seen.add(pi)
                ctx.s_mm.inc(mm)
                ctx.s_mm.wait(v)
                aslot = stg2c[0] % 4
                ab = acc1[aslot]
                ctx.s_accr[aslot].wait(v)
                v.tensor_copy(ab[:, 0:512], po0[:])
                ctx.s_v.inc(v.tensor_copy(ab[:, 512:1024], po1[:]))
                ctx.s_v.wait(g)
                tg = ch * 8 + tt
                ctx.s_accr[aslot].inc(
                    g.dma_start(out=h["out"].ap()[tg * 128:(tg + 1) * 128, :], in_=ab[:],
                                accum_op=ALU.add), 16)
                stg2c[0] += 1
            ctx.s_v.wait(sy)
        g.wait_ge(ctx.s_st.h, ctx.s_st.n)
        for sr in ctx.s_accr:
            sr.wait(g)
        ctx.barrier()

    for gi in range(8):
        grp_pass([(i, gi * 4 + i) for i in range(4)], 512,
                 h["st_eg"], h["st_eu"], h["st_ed"], wb=True)
    grp_pass([(0, None)], 1024, h["st_shg"], h["st_shu"], h["st_shd"], wb=False)
    nc.gpsimd.wait_ge(ctx.s_st.h, ctx.s_st.n)


def build(dbg_taps=(), upto=None, dbg=False):
    nc = bass.Bass("TRN2", target_bir_lowering=False)
    with ExitStack() as es:
        ctx = Ctx(nc, es)
        ctx.dbg = dbg if isinstance(dbg, (set, frozenset)) else ({"attn0", "ffn0"} if dbg else None)
        declare_tensors(nc, ctx, dbg_taps)

        def phase(fn, *a, **k):
            with ExitStack() as pes:
                old = ctx.es
                ctx.es = pes
                fn(ctx, *a, **k)
                ctx.es = old
            ctx.barrier()

        phase(phase_stage)
        phase(phase_ln, ctx.h["x"].ap(), ctx.h["xn0_bf"], None, ctx.h["xa_bf"], tag="0")
        if upto == "ln0":
            return nc
        phase(phase_l0_attn)
        if upto == "l0attn":
            return nc
        phase(phase_l0_ffn)
        if upto == "l0":
            return nc
        phase(phase_ln, ctx.h["x1"].ap(), ctx.h["xn1_bf"], ctx.h["xn1_tm"], ctx.h["xb_bf"], tag="1")
        if upto == "ln1":
            return nc
        if upto == "l1uv":
            phase(phase_l1_attn, stage="uv")
            return nc
        if upto in ("l1sim", "l1att", "l1s2"):
            phase(phase_l1_attn, stage=upto[2:])
            return nc
        phase(phase_l1_attn)
        if upto == "l1attn":
            return nc
        phase(phase_l1_ffn)
    return nc


def make_in_map(core, inputs):
    c = _host_consts()
    m = {}
    x = np.asarray(inputs["L0_attn_x"])
    m["x"] = np.ascontiguousarray(x[2 * core:2 * core + 2].reshape(T, D), dtype=np.float32)
    for nm in WEIGHT_NAMES:
        a = np.asarray(inputs[nm], dtype=np.float32)
        shp = INPUT_SHAPES[nm]
        if len(shp) == 3:
            a = a.reshape(shp[0] * shp[1], shp[2])
        elif len(shp) == 1:
            a = a.reshape(1, shp[0])
        m[nm] = np.ascontiguousarray(a)
    m["w_rev"] = np.ascontiguousarray(np.asarray(inputs["L1_attn_w"], dtype=np.float32)[::-1].reshape(1, 1023))
    for nm in CONST_SHAPES:
        m[nm] = c[nm]
    m["bits"] = c["bits"]
    return m


_NC_CACHE = {}


def kernel(**inputs):
    key = "full"
    if key not in _NC_CACHE:
        _NC_CACHE[key] = build()
    nc = _NC_CACHE[key]
    in_maps = [make_in_map(cc, inputs) for cc in range(8)]
    res = run_bass_kernel_spmd(nc, in_maps, core_ids=list(range(8)))
    out = np.stack([r["out"] for r in res.results])
    return out.reshape(16, SEQ, D).astype(np.float32)


# version 13
# speedup vs baseline: 1.3516x; 1.0085x over previous
import math
from contextlib import ExitStack

import numpy as np
import ml_dtypes

import concourse.bass as bass
import concourse.mybir as mybir
from concourse.bass_utils import run_bass_kernel_spmd

F32 = mybir.dt.float32
BF16 = mybir.dt.bfloat16
I32 = mybir.dt.int32
AF = mybir.ActivationFunctionType
ALU = mybir.AluOpType
AX = mybir.AxisListType
BF = ml_dtypes.bfloat16

T = 4096
SEQ = 2048
D = 1024
NEG = -3.4028234663852886e+38
NT = T // 128

_CONST_CACHE = None

def _host_consts():
    global _CONST_CACHE
    if _CONST_CACHE is not None:
        return _CONST_CACHE
    c = {}
    dh, ws, jlen, SCALE_BASE = 64, 128, 384, 64
    inv_freq = 1.0 / 10000.0 ** (np.arange(0, dh, 2, dtype=np.float64) / dh)
    t = np.arange(jlen, dtype=np.float64)
    half = t[:, None] * inv_freq[None, :]
    freqs = np.concatenate([half, half], -1)
    power = (t - jlen // 2) / SCALE_BASE
    sbase = (np.arange(0, dh, 2, dtype=np.float64) + 0.4 * dh) / (1.4 * dh)
    scale = sbase[None, :] ** power[:, None]
    scale = np.concatenate([scale, scale], -1)
    qf, qs = freqs[-ws:], scale[-ws:]
    sgn = np.where(np.arange(dh) < dh // 2, -1.0, 1.0)
    s05 = dh ** -0.5
    CQ = (np.cos(qf) * qs * s05).T
    SQ = (np.sin(qf) * qs * s05).T * sgn[:, None]
    inv_s = 1.0 / scale
    CK = (np.cos(freqs) * inv_s).T
    SK = (np.sin(freqs) * inv_s).T * sgn[:, None]
    c["l0_cq"] = np.concatenate([CQ, CQ], 0).astype(np.float32)
    c["l0_sq"] = np.concatenate([SQ, SQ], 0).astype(np.float32)
    c["l0_ck"] = np.concatenate([CK, CK], 0).astype(np.float32)
    c["l0_sk"] = np.concatenate([SK, SK], 0).astype(np.float32)
    i = np.arange(128)[:, None]
    j = np.arange(384)[None, :]
    band = (j >= i) & (j <= i + 256)
    m = []
    for typ in range(3):
        ok = band.copy()
        if typ == 0:
            ok &= j >= 128
        if typ == 2:
            ok &= j < 256
        m.append(np.where(ok, 0.0, NEG).astype(np.float32))
    c["l0_mask"] = np.concatenate(m, axis=1)
    try:
        import jax
        import jax.numpy as jnp
        with jax.default_device(jax.devices("cpu")[0]):
            pos = jnp.arange(SEQ, dtype=jnp.float32)
            invf = 10000.0 ** (jnp.arange(64, dtype=jnp.float32) / 64)
            sinu = pos[:, None] * invf[None, :]
            sin1 = np.asarray(jnp.sin(sinu), dtype=np.float32)
            cos1 = np.asarray(jnp.cos(sinu), dtype=np.float32)
    except Exception:
        pos = np.arange(SEQ, dtype=np.float32)
        invf = (np.float32(10000.0) ** (np.arange(64, dtype=np.float32) / np.float32(64))).astype(np.float32)
        sinu = (pos[:, None] * invf[None, :]).astype(np.float32)
        sin1 = np.sin(sinu).astype(np.float32)
        cos1 = np.cos(sinu).astype(np.float32)
    sinT = sin1.T
    cosT = cos1.T
    c["l1_sin"] = np.ascontiguousarray(np.concatenate([-sinT, sinT], 0))
    c["l1_cos"] = np.ascontiguousarray(np.concatenate([cosT, cosT], 0))
    c["bits"] = ((np.arange(128)[:, None] >> np.arange(7)[None, :]) & 1).astype(np.int32)
    c["ident"] = np.eye(128, dtype=np.float32)
    _CONST_CACHE = c
    return c

class Sem:
    def __init__(self, ctx, name):
        self.h = ctx.es.enter_context(ctx.nc.semaphore(name))
        self.n = 0

    def inc(self, instr, k=1):
        instr.then_inc(self.h, k)
        self.n += k
        return self.n

    def wait(self, eng, val=None):
        eng.wait_ge(self.h, self.n if val is None else val)


class Ctx:
    def __init__(self, nc, es, dbg=False):
        self.nc = nc
        self.es = es
        self.dbg = dbg
        self.h = {}
        self.sb = {}
        self.s_ld = Sem(self, "s_ld")
        self.s_cast = Sem(self, "s_cast")
        self.s_st = Sem(self, "s_st")
        self.s_mm = Sem(self, "s_mm")
        self.s_v = Sem(self, "s_v")
        self.s_sc = Sem(self, "s_sc")
        self.s_g = Sem(self, "s_g")
        self.s_ldr = [Sem(self, "s_ldr0"), Sem(self, "s_ldr1")]
        self.s_accr = [Sem(self, f"s_accr{i}") for i in range(4)]

    def sbuf(self, name, shape, dtype):
        return self.es.enter_context(self.nc.sbuf_tensor(name, shape, dtype))

    def psum(self, name, shape, dtype=F32):
        return self.es.enter_context(self.nc.psum_tensor(name, shape, dtype))

    def barrier(self):
        self.nc.all_engine_barrier()


def _dram(nc, name, shape, dtype, kind=None):
    if kind is None:
        return nc.dram_tensor(name, shape, dtype)
    return nc.dram_tensor(name, shape, dtype, kind=kind)


STAGE_SPECS = [
    ("st_wqkv", "L0_attn_w_qkv", (1024, 1024)),
    ("st_wout", "L0_attn_w_out", (1024, 512)),
    ("st_uvw", "L1_attn_uv_w", (4224, 1024)),
    ("st_ow", "L1_attn_o_w", (1024, 2048)),
    ("st_gw", "L1_ffn_gate_w", (32, 1024)),
    ("st_eg", "L1_ffn_expert_gate", (32 * 512, 1024)),
    ("st_eu", "L1_ffn_expert_up", (32 * 512, 1024)),
    ("st_ed", "L1_ffn_expert_down", (32 * 1024, 512)),
    ("st_shg", "L1_ffn_shared_gate", (1024, 1024)),
    ("st_shu", "L1_ffn_shared_up", (1024, 1024)),
    ("st_shd", "L1_ffn_shared_down", (1024, 1024)),
]


def phase_stage(ctx):
    nc = ctx.nc
    for dst, src, shape in STAGE_SPECS:
        d = ctx.h[dst]
        s = ctx.h[src]
        rows, cols = shape
        rchunk = max(1, min(rows, (2 << 20) // cols))
        r = 0
        while r < rows:
            n = min(rchunk, rows - r)
            ctx.s_cast.inc(nc.gpsimd.dma_start(
                out=bass.AP(d, r * cols, [[cols, n], [1, cols]]),
                in_=bass.AP(s, r * cols, [[cols, n], [1, cols]])), 16)
            r += n
    d = ctx.h["st_wqkv2"]
    s = ctx.h["L0_attn_w_qkv"]
    ctx.s_cast.inc(nc.gpsimd.dma_start(
        out=bass.AP(d, 0, [[64 * 1024, 8], [1024, 32], [1, 1024]]),
        in_=bass.AP(s, 32 * 1024, [[64 * 1024, 8], [1024, 32], [1, 1024]])), 16)
    ctx.s_cast.inc(nc.gpsimd.dma_start(
        out=bass.AP(d, 32 * 1024, [[64 * 1024, 8], [1024, 32], [1, 1024]]),
        in_=bass.AP(s, 0, [[64 * 1024, 8], [1024, 32], [1, 1024]])), 16)
    d2 = ctx.h["st_uvw2"]
    s2 = ctx.h["L1_attn_uv_w"]
    ctx.s_cast.inc(nc.gpsimd.dma_start(
        out=bass.AP(d2, 0, [[1024, 64], [1, 1024]]),
        in_=bass.AP(s2, (4096 + 64) * 1024, [[1024, 64], [1, 1024]])), 16)
    ctx.s_cast.inc(nc.gpsimd.dma_start(
        out=bass.AP(d2, 64 * 1024, [[1024, 64], [1, 1024]]),
        in_=bass.AP(s2, 4096 * 1024, [[1024, 64], [1, 1024]])), 16)
    nc.gpsimd.wait_ge(ctx.s_cast.h, ctx.s_cast.n)


def phase_ln(ctx, src_ap, dst_bf, dst_f32=None, inner_bf=None, tag=""):
    nc = ctx.nc
    v, sc, g, sy = nc.vector, nc.scalar, nc.gpsimd, nc.sync
    sx = [ctx.sbuf(f"ln_x{tag}{i}", [128, D], F32) for i in range(2)]
    xn32 = ctx.sbuf(f"ln_xn32{tag}", [128, D], F32)
    xnbf = ctx.sbuf(f"ln_xnbf{tag}", [128, D], BF16)
    xabf = ctx.sbuf(f"ln_xabf{tag}", [128, D], BF16)
    scr = ctx.sbuf(f"ln_scr{tag}", [128, D], F32)
    st = ctx.sbuf(f"ln_st{tag}", [128, 8], F32)
    v.memset(st[:, 7:8], 1e-5)
    def norm_tile(buf, out_tile):
        v.reduce_sum(st[:, 0:1], buf, AX.X, negate=True)
        ctx.s_v.inc(v.memset(st[:, 1:2], 0.0))
        ctx.s_v.wait(sc)
        sc.activation(scr[:], buf, AF.Square, accum_out=st[:, 1:2])
        ctx.s_sc.inc(sc.mul(st[:, 2:3], st[:, 0:1], 1.0 / D))
        ctx.s_sc.wait(sc)
        ctx.s_sc.inc(sc.activation(st[:, 3:4], st[:, 2:3], AF.Square))
        ctx.s_sc.wait(v)
        ctx.s_v.inc(v.tensor_scalar(st[:, 4:5], st[:, 1:2], 1.0 / D, st[:, 3:4],
                                    ALU.mult, ALU.subtract))
        ctx.s_v.wait(sc)
        ctx.s_sc.inc(sc.activation(st[:, 5:6], st[:, 4:5], AF.Sqrt, bias=st[:, 7:8]))
        ctx.s_sc.wait(v)
        ctx.s_v.inc(v.reciprocal(st[:, 6:7], st[:, 5:6]))
        ctx.s_v.wait(v)
        ctx.s_v.inc(v.scalar_tensor_tensor(out_tile, buf, st[:, 2:3],
                                           st[:, 6:7].broadcast_to([128, D]),
                                           ALU.add, ALU.mult))

    ld_done = {}
    for t in range(NT):
        buf = sx[t % 2]
        if t == 0:
            ld_done[0] = ctx.s_ldr[0].inc(sy.dma_start(out=sx[0][:], in_=src_ap[0:128, :]), 16)
        if t + 1 < NT:
            sl = (t + 1) % 2
            ld_done[t + 1] = ctx.s_ldr[sl].inc(
                sy.dma_start(out=sx[sl][:], in_=src_ap[(t + 1) * 128:(t + 2) * 128, :]), 16)
        v.wait_ge(ctx.s_ldr[t % 2].h, ld_done[t])
        norm_tile(buf[:], xn32[:])
        ctx.s_v.inc(v.tensor_copy(xnbf[:], xn32[:]))
        ctx.s_v.wait(g)
        ctx.s_st.inc(g.dma_start(out=dst_bf.ap()[t * 128:(t + 1) * 128, :], in_=xnbf[:]), 16)
        if dst_f32 is not None:
            ctx.s_st.inc(g.dma_start(out=dst_f32.ap()[t * 128:(t + 1) * 128, :], in_=xn32[:]), 16)
        if inner_bf is not None:
            norm_tile(xn32[:], xabf[:])
            ctx.s_v.wait(g)
            ctx.s_st.inc(g.dma_start(out=inner_bf.ap()[t * 128:(t + 1) * 128, :], in_=xabf[:]), 16)
        ctx.s_st.wait(sy)
    nc.gpsimd.wait_ge(ctx.s_st.h, ctx.s_st.n)


WEIGHT_NAMES = [
    "L0_attn_ln_g", "L0_attn_ln_b", "L0_attn_w_qkv", "L0_attn_w_out",
    "L0_ffn_router_w", "L0_ffn_router_b", "L0_ffn_sw1", "L0_ffn_sb1",
    "L0_ffn_sw2", "L0_ffn_sb2", "L0_ffn_rw1", "L0_ffn_rb1", "L0_ffn_rw2", "L0_ffn_rb2",
    "L1_attn_ln_w", "L1_attn_ln_b", "L1_attn_uv_w", "L1_attn_uv_b",
    "L1_attn_gamma", "L1_attn_beta", "L1_attn_w", "L1_attn_o_w", "L1_attn_o_b",
    "L1_ffn_gate_w", "L1_ffn_gate_bias", "L1_ffn_expert_gate", "L1_ffn_expert_up",
    "L1_ffn_expert_down", "L1_ffn_shared_gate", "L1_ffn_shared_up", "L1_ffn_shared_down",
]
INPUT_SHAPES = {
    "L0_attn_ln_g": (D,), "L0_attn_ln_b": (D,),
    "L0_attn_w_qkv": (1024, 1024), "L0_attn_w_out": (1024, 512),
    "L0_ffn_router_w": (1024, 8), "L0_ffn_router_b": (8,),
    "L0_ffn_sw1": (1024, 2048), "L0_ffn_sb1": (2048,),
    "L0_ffn_sw2": (2048, 1024), "L0_ffn_sb2": (1024,),
    "L0_ffn_rw1": (8, 1024, 2048), "L0_ffn_rb1": (8, 2048),
    "L0_ffn_rw2": (8, 2048, 1024), "L0_ffn_rb2": (8, 1024),
    "L1_attn_ln_w": (D,), "L1_attn_ln_b": (D,),
    "L1_attn_uv_w": (4224, 1024), "L1_attn_uv_b": (4224,),
    "L1_attn_gamma": (2, 128), "L1_attn_beta": (2, 128),
    "L1_attn_w": (1023,), "L1_attn_o_w": (1024, 2048), "L1_attn_o_b": (1024,),
    "L1_ffn_gate_w": (32, 1024), "L1_ffn_gate_bias": (32,),
    "L1_ffn_expert_gate": (32, 512, 1024), "L1_ffn_expert_up": (32, 512, 1024),
    "L1_ffn_expert_down": (32, 1024, 512),
    "L1_ffn_shared_gate": (1024, 1024), "L1_ffn_shared_up": (1024, 1024),
    "L1_ffn_shared_down": (1024, 1024),
}
CONST_SHAPES = {
    "l0_cq": (128, 128), "l0_sq": (128, 128), "l0_ck": (128, 384), "l0_sk": (128, 384),
    "l0_mask": (128, 3 * 384), "l1_sin": (128, SEQ), "l1_cos": (128, SEQ),
    "ident": (128, 128),
}


def declare_tensors(nc, ctx, dbg_taps=()):
    h = ctx.h
    h["x"] = nc.dram_tensor("x", [T, D], F32, kind="ExternalInput")
    h["out"] = nc.dram_tensor("out", [T, D], F32, kind="ExternalOutput")
    for nm in WEIGHT_NAMES:
        shp = list(INPUT_SHAPES[nm])
        if len(shp) == 3:
            shp = [shp[0] * shp[1], shp[2]]
        if len(shp) == 1:
            shp = [1, shp[0]]
        h[nm] = nc.dram_tensor(nm, shp, F32, kind="ExternalInput")
    h["w_rev"] = nc.dram_tensor("w_rev", [1, 1023], F32, kind="ExternalInput")
    for nm, shp in CONST_SHAPES.items():
        h[nm] = nc.dram_tensor(nm, list(shp), F32, kind="ExternalInput")
    h["bits"] = nc.dram_tensor("bits", [128, 7], I32, kind="ExternalInput")

    def scratch(name, shape, dtype):
        kind = "ExternalOutput" if name in dbg_taps else None
        h[name] = _dram(nc, name, list(shape), dtype, kind)

    for nm, src, shape in STAGE_SPECS:
        scratch(nm, shape, BF16)
    scratch("st_wqkv2", (512, 1024), BF16)
    scratch("st_uvw2", (128, 1024), BF16)
    scratch("xn0_bf", (T, D), BF16)
    scratch("xa_bf", (T, D), BF16)
    scratch("x1", (T, D), F32)
    scratch("xn1_bf", (T, D), BF16)
    scratch("xn1_tm", (T, D), F32)
    scratch("xb_bf", (T, D), BF16)
    scratch("uT", (2048, T), BF16)
    scratch("v_tm", (T, 2048), BF16)
    scratch("wext_rev", (1, 4352), F32)
    scratch("dbg0", (16, 128, 512), F32)
    return h


def phase_l0_attn(ctx):
    nc = ctx.nc
    v, sc, g, sy, te = nc.vector, nc.scalar, nc.gpsimd, nc.sync, nc.tensor
    h = ctx.h
    GW = 4352
    qkT = [ctx.sbuf(f"qkT{c}", [128, GW], BF16) for c in range(4)]
    qk2T = [ctx.sbuf(f"qk2T{c}", [128, GW], BF16) for c in range(4)]
    vtm = ctx.sbuf("vtm", [128, 32 * 512], BF16)
    cq = ctx.sbuf("cq", [128, 128], F32)
    sq = ctx.sbuf("sq", [128, 128], F32)
    ck = ctx.sbuf("ck", [128, 384], F32)
    sk = ctx.sbuf("sk", [128, 384], F32)
    msk = ctx.sbuf("msk", [128, 3 * 384], F32)
    identb = ctx.sbuf("identb", [128, 128], BF16)
    gcol = ctx.sbuf("gcol", [128, 8], F32)
    bcol = ctx.sbuf("bcol", [128, 8], F32)
    pan = [ctx.sbuf(f"l0pan{i}", [128, 512], BF16) for i in range(8)]
    wq = ctx.sbuf("wq", [128, 8 * 512], BF16)
    wq2 = ctx.sbuf("wq2", [128, 8 * 512], BF16)
    wv = ctx.sbuf("wv", [128, 8 * 512], BF16)
    wo = ctx.sbuf("wo", [128, 8 * 512], BF16)
    x0t = [ctx.sbuf(f"x0t{i}", [128, D], F32) for i in range(2)]
    x1t = ctx.sbuf("x1t", [128, D], F32)
    tm = ctx.sbuf("tm", [128, 384], F32)
    ssc = ctx.sbuf("ssc", [128, 4], F32)
    attn_bf = ctx.sbuf("attn_bf", [128, 384], BF16)
    e32 = ctx.sbuf("e32", [128, 384], F32)
    attnT_bf = ctx.sbuf("attnT_bf", [128, 384], BF16)
    bq = [ctx.sbuf(f"bq{i}", [128, 128], BF16) for i in range(4)]
    bk = [ctx.sbuf(f"bk{i}", [128, 384], BF16) for i in range(4)]
    rt1 = ctx.sbuf("rt1", [128, 384], F32)
    rt2 = ctx.sbuf("rt2", [128, 384], F32)
    outT_bf = ctx.sbuf("outT_bf", [128, 512], BF16)
    p_sim = [ctx.psum(f"p_sim{i}", [128, 384]) for i in range(2)]
    p_at = ctx.es.enter_context(nc.psum_tensor("p_at", [128, 384], BF16))
    p_outT = ctx.psum("p_outT", [128, 512])
    p_pr = [ctx.psum(f"p_pr{i}", [128, 512]) for i in range(2)]
    dbgt = ctx.sbuf("dbgt", [128, 512], F32)

    def dump(idx, src_ap, width=512):
        if not (ctx.dbg and "attn0" in ctx.dbg):
            return
        v.memset(dbgt[:], 0.0)
        ctx.s_v.inc(v.tensor_copy(dbgt[:, 0:width], src_ap))
        ctx.s_v.wait(g)
        ctx.s_st.inc(g.dma_start(out=h["dbg0"].ap()[idx, :, :], in_=dbgt[:]), 16)
        ctx.s_st.wait(v)

    for dst, src in [(cq, "l0_cq"), (sq, "l0_sq"), (ck, "l0_ck"), (sk, "l0_sk"), (msk, "l0_mask")]:
        ctx.s_ld.inc(sy.dma_start(out=dst[:], in_=h[src].ap()), 16)
    ctx.s_cast.inc(g.dma_start(out=identb[:], in_=h["ident"].ap()), 16)
    for dc in range(8):
        ctx.s_ld.inc(sy.dma_start(out=gcol[:, dc:dc + 1],
                                  in_=bass.AP(h["L0_attn_ln_g"], dc * 128, [[1, 128], [1, 1]])), 16)
        ctx.s_ld.inc(sy.dma_start(out=bcol[:, dc:dc + 1],
                                  in_=bass.AP(h["L0_attn_ln_b"], dc * 128, [[1, 128], [1, 1]])), 16)
    for ic in range(4):
        for dh2 in range(2):
            ctx.s_ld.inc(sy.dma_start_transpose(
                wo[:, (ic * 2 + dh2) * 512:(ic * 2 + dh2 + 1) * 512],
                h["st_wout"].ap()[dh2 * 512:(dh2 + 1) * 512, ic * 128:(ic + 1) * 128]), 16)
    for c in range(4):
        v.memset(qkT[c][:, 0:128], 0.0)
        v.memset(qkT[c][:, GW - 128:GW], 0.0)
        v.memset(qk2T[c][:, 0:128], 0.0)
        v.memset(qk2T[c][:, GW - 128:GW], 0.0)
    ctx.s_v.inc(v.memset(tm[:, 0:1], 0.0))
    ctx.barrier()

    for p in range(8):
        tok0 = p * 512
        ldv = []
        for dc in range(8):
            ldv.append(ctx.s_ld.inc(sy.dma_start_transpose(
                pan[dc][:], h["xa_bf"].ap()[tok0:tok0 + 512, dc * 128:(dc + 1) * 128]), 16))
            ldv.append(ctx.s_ld.inc(sy.dma_start_transpose(
                wq[:, dc * 512:(dc + 1) * 512],
                h["st_wqkv"].ap()[0:512, dc * 128:(dc + 1) * 128]), 16))
            ldv.append(ctx.s_ld.inc(sy.dma_start_transpose(
                wq2[:, dc * 512:(dc + 1) * 512],
                h["st_wqkv2"].ap()[0:512, dc * 128:(dc + 1) * 128]), 16))
            ldv.append(ctx.s_ld.inc(sy.dma_start_transpose(
                wv[:, dc * 512:(dc + 1) * 512],
                h["st_wqkv"].ap()[512:1024, dc * 128:(dc + 1) * 128]), 16))
        v.wait_ge(ctx.s_ld.h, ldv[-1])
        for dc in range(8):
            last = v.tensor_scalar(pan[dc][:], pan[dc][:], gcol[:, dc:dc + 1], bcol[:, dc:dc + 1],
                                   ALU.mult, ALU.add)
        ctx.s_v.inc(last)
        ctx.s_v.wait(te)
        jobs = []
        for c in range(4):
            jobs.append(("qk", c))
        for c in range(4):
            jobs.append(("qk2", c))
        for tt in range(4):
            jobs.append(("v", tt))
        aslotv = [0, 0]
        for ji, (kind, c) in enumerate(jobs):
            sl = ji % 2
            ps = p_pr[sl]
            if aslotv[sl]:
                ctx.s_v.wait(te, aslotv[sl])
            if kind == "qk":
                for dc in range(8):
                    mm = te.matmul(ps[:], wq[:, dc * 512 + c * 128:dc * 512 + (c + 1) * 128],
                                   pan[dc][:], start=(dc == 0), stop=(dc == 7))
            elif kind == "qk2":
                for dc in range(8):
                    mm = te.matmul(ps[:], wq2[:, dc * 512 + c * 128:dc * 512 + (c + 1) * 128],
                                   pan[dc][:], start=(dc == 0), stop=(dc == 7))
            else:
                for dc in range(8):
                    mm = te.matmul(ps[:], pan[dc][:, c * 128:(c + 1) * 128], wv[:, dc * 512:(dc + 1) * 512],
                                   start=(dc == 0), stop=(dc == 7))
            ctx.s_mm.inc(mm)
            ctx.s_mm.wait(v)
            if kind == "qk":
                aslotv[sl] = ctx.s_v.inc(v.tensor_copy(qkT[c][:, 128 + tok0:128 + tok0 + 512], ps[:, 0:512]))
            elif kind == "qk2":
                aslotv[sl] = ctx.s_v.inc(v.tensor_copy(qk2T[c][:, 128 + tok0:128 + tok0 + 512], ps[:, 0:512]))
            else:
                gt = (tok0 // 128) + c
                aslotv[sl] = ctx.s_v.inc(v.tensor_copy(vtm[:, gt * 512:(gt + 1) * 512], ps[:, 0:512]))
        ctx.s_v.wait(te)
        ctx.s_v.wait(sy)
    ctx.barrier()
    dump(0, qkT[0][:, 128:640])
    dump(1, qk2T[0][:, 128:640])
    dump(2, vtm[:, 0:512])
    ctx.barrier()

    x0ld = []
    for t in range(NT):
        b, w = t // 16, t % 16
        typ = 0 if w == 0 else (2 if w == 15 else 1)
        if t == 0:
            x0ld.append(ctx.s_ldr[0].inc(sy.dma_start(out=x0t[0][:], in_=h["x"].ap()[0:128, :]), 16))
        if t + 1 < NT:
            if t >= 1:
                ctx.s_v.wait(sy)
            x0ld.append(ctx.s_ldr[(t + 1) % 2].inc(
                sy.dma_start(out=x0t[(t + 1) % 2][:], in_=h["x"].ap()[(t + 1) * 128:(t + 2) * 128, :]), 16))
        for hp in range(4):
            qsl = qkT[hp][:, 128 + t * 128:128 + (t + 1) * 128]
            q2sl = qk2T[hp][:, 128 + t * 128:128 + (t + 1) * 128]
            v.tensor_tensor(rt1[:, 0:128], qsl, cq[:], ALU.mult)
            v.tensor_tensor(rt2[:, 0:128], q2sl, sq[:], ALU.mult)
            v.tensor_add(bq[hp][:], rt1[:, 0:128], rt2[:, 0:128])
            ksl = qkT[hp][:, t * 128:t * 128 + 384]
            k2sl = qk2T[hp][:, t * 128:t * 128 + 384]
            v.tensor_tensor(rt1[:], ksl, ck[:], ALU.mult)
            v.tensor_tensor(rt2[:], k2sl, sk[:], ALU.mult)
            last = v.tensor_add(bk[hp][:], rt1[:], rt2[:])
        ctx.s_v.inc(last)
        if t == 0:
            dump(3, bq[0][:], 128)
            dump(4, bk[0][:], 384)
        ctx.s_v.wait(te)
        for hd in range(8):
            hp, half = hd // 2, hd % 2
            ps = p_sim[hd % 2]
            r0 = 64 * half
            ctx.s_mm.inc(te.matmul(ps[:], bq[hp][r0:r0 + 64, :], bk[hp][r0:r0 + 64, :],
                                   start=True, stop=True))
            ctx.s_mm.wait(v)
            if t == 0 and hd == 0:
                dump(5, ps[:, 0:384], 384)
            v.tensor_tensor(tm[:], ps[:], msk[:, typ * 384:(typ + 1) * 384], ALU.add)
            ctx.s_v.inc(v.reduce_max(ssc[:, 0:1], tm[:], AX.X, negate=True))
            ctx.s_v.wait(sc)
            ctx.s_sc.inc(sc.activation(e32[:], tm[:], AF.Exp, bias=ssc[:, 0:1]))
            ctx.s_sc.wait(v)
            ctx.s_v.inc(v.reduce_sum(ssc[:, 1:2], e32[:], AX.X))
            ctx.s_v.wait(v)
            ctx.s_v.inc(v.reciprocal(ssc[:, 2:3], ssc[:, 1:2]))
            ctx.s_v.wait(v)
            ctx.s_v.inc(v.tensor_tensor(attn_bf[:], e32[:],
                                        ssc[:, 2:3].broadcast_to([128, 384]), ALU.mult))
            if t == 0 and hd == 0:
                dump(6, ssc[:, 0:4], 4)
                dump(7, attn_bf[:], 384)
                dump(8, e32[:], 384)
            ctx.s_v.wait(te)
            for kb in range(3):
                tr = te.transpose(p_at[:, kb * 128:(kb + 1) * 128], attn_bf[:, kb * 128:(kb + 1) * 128],
                                  identb[:])
            ctx.s_mm.inc(tr)
            ctx.s_mm.wait(v)
            ctx.s_v.inc(v.tensor_copy(attnT_bf[:], p_at[:]))
            ctx.s_v.wait(te)
            first = True
            for kb in range(3):
                kw = w - 1 + kb
                if kw < 0 or kw >= 16:
                    continue
                kt = b * 16 + kw
                mm = te.matmul(p_outT[r0:r0 + 64, hp * 128:(hp + 1) * 128],
                               vtm[:, kt * 512 + hd * 64:kt * 512 + (hd + 1) * 64],
                               attnT_bf[:, kb * 128:(kb + 1) * 128],
                               start=first, stop=(kb == 2 or (w == 15 and kb == 1)))
                first = False
            ctx.s_mm.inc(mm)
        ctx.s_mm.wait(v)
        ctx.s_v.inc(v.tensor_copy(outT_bf[:], p_outT[:]))
        if t == 0:
            dump(9, outT_bf[:])
        ctx.s_v.wait(te)
        for dh2 in range(2):
            for ic in range(4):
                mm = te.matmul(p_pr[dh2][:], outT_bf[:, ic * 128:(ic + 1) * 128],
                               wo[:, (ic * 2 + dh2) * 512:(ic * 2 + dh2 + 1) * 512],
                               start=(ic == 0), stop=(ic == 3))
        ctx.s_mm.inc(mm)
        v.wait_ge(ctx.s_ldr[t % 2].h, x0ld[t])
        ctx.s_mm.wait(v)
        if t == 0:
            dump(10, p_pr[0][:])
            dump(11, p_pr[1][:])
        v.tensor_add(x1t[:, 0:512], p_pr[0][:], x0t[t % 2][:, 0:512])
        ctx.s_v.inc(v.tensor_add(x1t[:, 512:1024], p_pr[1][:], x0t[t % 2][:, 512:1024]))
        ctx.s_v.wait(g)
        ctx.s_st.inc(g.dma_start(out=h["x1"].ap()[t * 128:(t + 1) * 128, :], in_=x1t[:]), 16)
        ctx.s_st.wait(v)
        ctx.s_v.wait(te)
    nc.gpsimd.wait_ge(ctx.s_st.h, ctx.s_st.n)


def phase_l0_ffn(ctx):
    nc = ctx.nc
    v, sc, g, sy, te = nc.vector, nc.scalar, nc.gpsimd, nc.sync, nc.tensor
    h = ctx.h
    identf = ctx.sbuf("identf", [128, 128], F32)
    pan = [ctx.sbuf(f"f0pan{i}", [128, 512], BF16) for i in range(8)]
    rwbf = ctx.sbuf("rwbf", [128, 8 * 8], BF16)
    rbb = ctx.sbuf("rbb", [128, 8], F32)
    w8 = ctx.sbuf("w8", [128, NT * 8], F32)
    w8T8 = ctx.sbuf("w8T8", [8, T], BF16)
    oh8 = ctx.sbuf("oh8", [8, 8 * 128], BF16)
    lg = ctx.sbuf("lg", [128, 8], F32)
    pe_ = ctx.sbuf("pe_", [128, 8], F32)
    mske = ctx.sbuf("mske", [128, 8], F32)
    mst = ctx.sbuf("mst", [128, 8], F32)
    stt = ctx.sbuf("stt", [128, 4], F32)
    p_r = ctx.psum("p_r", [128, 8])
    p_t8 = ctx.psum("p_t8", [8, 128])
    p_wb = ctx.psum("p_wb", [128, 512])
    p_h = [ctx.psum(f"p_h{i}", [128, 512]) for i in range(2)]
    p_o = [ctx.psum(f"p_o{i}", [128, 512]) for i in range(2)]
    rw1sb = [ctx.sbuf(f"rw1sb{i}", [128, 8 * 2048], BF16) for i in range(2)]
    rw2sb = [ctx.sbuf(f"rw2sb{i}", [128, 16 * 1024], BF16) for i in range(2)]
    h1sb = [ctx.sbuf(f"h1sb{i}", [128, 16 * 512], BF16) for i in range(2)]
    wbsb = [ctx.sbuf(f"wbsb{i}", [128, 512], F32) for i in range(2)]
    g1 = [ctx.sbuf(f"g1_{i}", [128, 512], F32) for i in range(2)]
    rbcol = ctx.sbuf("rbcol", [128, 2 * 16], F32)
    sbcol = ctx.sbuf("sbcol", [128, 16], F32)
    rb2bf = ctx.sbuf("rb2bf", [8, 1024], BF16)
    sb2bf = ctx.sbuf("sb2bf", [1, 1024], BF16)
    onesb = ctx.sbuf("onesb", [1, 128], BF16)
    acc = [ctx.sbuf(f"acc{i}", [128, D], F32) for i in range(3)]
    dbgt = ctx.sbuf("dbgtf", [128, 512], F32) if (ctx.dbg and "ffn0" in ctx.dbg) else None

    def dump(idx, src_ap, width=512, parts=128):
        if not (ctx.dbg and "ffn0" in ctx.dbg):
            return
        v.memset(dbgt[:], 0.0)
        ctx.s_v.inc(v.tensor_copy(dbgt[0:parts, 0:width], src_ap))
        ctx.s_v.wait(g)
        ctx.s_st.inc(g.dma_start(out=h["dbg0"].ap()[idx, :, :], in_=dbgt[:]), 16)
        ctx.s_st.wait(v)

    ctx.s_ld.inc(sy.dma_start(out=identf[:], in_=h["ident"].ap()), 16)
    ctx.s_ld.inc(sy.dma_start(out=rbb[:], in_=bass.AP(h["L0_ffn_router_b"], 0, [[0, 128], [1, 8]])), 16)
    for dc in range(8):
        ctx.s_cast.inc(g.dma_start(out=rwbf[:, dc * 8:(dc + 1) * 8],
                                   in_=bass.AP(h["L0_ffn_router_w"], dc * 128 * 8, [[8, 128], [1, 8]])), 16)
    ctx.s_cast.inc(g.dma_start(out=rb2bf[:], in_=h["L0_ffn_rb2"].ap()), 16)
    ctx.s_cast.inc(g.dma_start(out=sb2bf[:], in_=h["L0_ffn_sb2"].ap()), 16)
    v.memset(onesb[:], 1.0)
    with ExitStack() as ioh_es0:
        ioh = ioh_es0.enter_context(nc.sbuf_tensor("ioh", [8, 8], I32))
        nc.gpsimd.iota(ioh[:], [[1, 8]], base=0, channel_multiplier=-1)
        ctx.barrier()
        v.tensor_single_scalar(oh8.rearrange("p (b j) -> p b j", b=8),
                               ioh[:].unsqueeze(2).broadcast_to([8, 8, 128]), 0, ALU.is_equal)
        ctx.s_v.inc(v.memset(stt[:, 0:1], 0.0))
        ctx.barrier()

    for ch in range(8):
        tok0 = ch * 512
        ctx.s_mm.wait(sy)
        ldv = None
        for dc in range(8):
            ldv = ctx.s_ld.inc(sy.dma_start_transpose(
                pan[dc][:], h["xn0_bf"].ap()[tok0:tok0 + 512, dc * 128:(dc + 1) * 128]), 16)
        te.wait_ge(ctx.s_ld.h, ldv)
        for tt in range(4):
            t = ch * 4 + tt
            ctx.s_v.wait(te)
            for dc in range(8):
                mm = te.matmul(p_r[:], pan[dc][:, tt * 128:(tt + 1) * 128], rwbf[:, dc * 8:(dc + 1) * 8],
                               start=(dc == 0), stop=(dc == 7))
            ctx.s_mm.inc(mm)
            ctx.s_mm.wait(v)

            def vs(instr):
                ctx.s_v.inc(instr)
                ctx.s_v.wait(v)

            vs(v.tensor_tensor(lg[:], p_r[:], rbb[:], ALU.add))
            vs(v.reduce_max(stt[:, 0:1], lg[:], AX.X))
            ctx.s_v.inc(v.reduce_max(stt[:, 1:2], lg[:], AX.X, negate=True))
            ctx.s_v.wait(sc)
            ctx.s_sc.inc(sc.activation(pe_[:], lg[:], AF.Exp, bias=stt[:, 1:2]))
            ctx.s_sc.wait(v)
            if t == 1:
                dump(9, lg[:], 8)
            vs(v.reduce_sum(stt[:, 2:3], pe_[:], AX.X))
            vs(v.reciprocal(stt[:, 2:3], stt[:, 2:3]))
            vs(v.tensor_tensor(mst[:], lg[:], stt[:, 0:1].broadcast_to([128, 8]), ALU.is_ge))
            if t == 1:
                dump(10, stt[:, 0:4], 4)
                dump(12, mst[:], 8)
            vs(v.scalar_tensor_tensor(mske[:], mst[:], -1e30, lg[:], ALU.mult, ALU.add))
            if t == 1:
                dump(14, mske[:], 8)
            vs(v.reduce_max(stt[:, 3:4], mske[:], AX.X))
            vs(v.tensor_tensor(mske[:], lg[:], stt[:, 3:4].broadcast_to([128, 8]), ALU.is_ge))
            vs(v.tensor_mul(mst[:], pe_[:], mske[:]))
            ctx.s_v.inc(v.tensor_tensor(w8[:, t * 8:(t + 1) * 8], mst[:],
                                        stt[:, 2:3].broadcast_to([128, 8]), ALU.mult))
            ctx.s_v.wait(te)
            ctx.s_mm.inc(te.transpose(p_t8[:], w8[:, t * 8:(t + 1) * 8], identf[:]))
            ctx.s_mm.wait(v)
            ctx.s_v.inc(v.tensor_copy(w8T8[:, t * 128:(t + 1) * 128], p_t8[:]))
            if t == 1:
                dump(6, lg[:], 8)
                dump(7, pe_[:], 8)
                dump(8, mske[:], 8)
                dump(11, mst[:], 8)
    ctx.barrier()
    dump(0, w8[:, 0:32], 32)
    dump(1, w8T8[0:8, 0:512], 512, 8)
    ctx.barrier()

    def ffn_pass(pair_idx, experts, hdim, w1_src, w1_rows, b1_src, w2_src, w2_rows, b2=False, wb=True):
        nht = hdim // 128
        stg2_ctr = [0]
        for ei, (slot, _e) in enumerate(experts):
            for dc in range(8):
                ctx.s_cast.inc(g.dma_start(
                    out=rw1sb[slot][:, dc * hdim:(dc + 1) * hdim],
                    in_=bass.AP(w1_src, (w1_rows[ei] + dc * 128) * hdim, [[hdim, 128], [1, hdim]])), 16)
            for hc in range(nht):
                ctx.s_cast.inc(g.dma_start(
                    out=rw2sb[slot][:, hc * 1024:(hc + 1) * 1024],
                    in_=bass.AP(w2_src, (w2_rows[ei] + hc * 128) * 1024, [[1024, 128], [1, 1024]])), 16)
            for hc in range(nht):
                if b1_src is not None:
                    boff = (0 if _e is None else _e) * hdim + hc * 128
                    ctx.s_ld.inc(sy.dma_start(
                        out=rbcol[:, ei * 16 + hc:ei * 16 + hc + 1],
                        in_=bass.AP(b1_src, boff, [[1, 128], [1, 1]])), 16)
        g.wait_ge(ctx.s_cast.h, ctx.s_cast.n)
        sy.wait_ge(ctx.s_ld.h, ctx.s_ld.n)
        ctx.barrier()
        for ch in range(8):
            tok0 = ch * 512
            ctx.s_mm.wait(sy)
            ldv = None
            for dc in range(8):
                ldv = ctx.s_ld.inc(sy.dma_start_transpose(
                    pan[dc][:], h["xn0_bf"].ap()[tok0:tok0 + 512, dc * 128:(dc + 1) * 128]), 16)
            te.wait_ge(ctx.s_ld.h, ldv)
            v.wait_ge(ctx.s_ld.h, ldv)
            ctx.s_mm.wait(v)
            if wb:
                for ei, (slot, e) in enumerate(experts):
                    ctx.s_v.wait(te)
                    ctx.s_mm.inc(te.matmul(p_wb[:], oh8[:, e * 128:(e + 1) * 128],
                                           w8T8[:, tok0:tok0 + 512], start=True, stop=True))
                    ctx.s_mm.wait(v)
                    ctx.s_v.inc(v.tensor_copy(wbsb[slot][:], p_wb[:]))
                    if ch == 0 and pair_idx == 0 and ei == 0:
                        dump(2, wbsb[0][:])
            pslot_val = [0, 0]
            gslot_val = [0, 0]
            ii = 0
            for ei, (slot, e) in enumerate(experts):
                for ht in range(nht):
                    ps = p_h[ii % 2]
                    gbuf = g1[ii % 2]
                    if pslot_val[ii % 2]:
                        ctx.s_sc.wait(te, pslot_val[ii % 2])
                    for dc in range(8):
                        mm = te.matmul(ps[:], rw1sb[slot][:, dc * hdim + ht * 128:dc * hdim + (ht + 1) * 128],
                                       pan[dc][:], start=(dc == 0), stop=(dc == 7))
                    ctx.s_mm.inc(mm)
                    ctx.s_mm.wait(sc)
                    if gslot_val[ii % 2]:
                        ctx.s_v.wait(sc, gslot_val[ii % 2])
                    pslot_val[ii % 2] = ctx.s_sc.inc(
                        sc.activation(gbuf[:], ps[:], AF.Gelu, bias=rbcol[:, ei * 16 + ht:ei * 16 + ht + 1]))
                    ctx.s_sc.wait(v)
                    if wb:
                        gslot_val[ii % 2] = ctx.s_v.inc(
                            v.tensor_tensor(h1sb[slot][:, ht * 512:(ht + 1) * 512], gbuf[:],
                                            wbsb[slot][:], ALU.mult))
                    else:
                        gslot_val[ii % 2] = ctx.s_v.inc(
                            v.tensor_copy(h1sb[slot][:, ht * 512:(ht + 1) * 512], gbuf[:]))
                    ii += 1
            ctx.s_v.wait(te)
            for tt in range(4):
                po0, po1 = p_o[0], p_o[1]
                if tt >= 1:
                    ctx.s_v.wait(te)
                mms = []
                for ei, (slot, e) in enumerate(experts):
                    for hc in range(nht):
                        lh = h1sb[slot][:, hc * 512 + tt * 128:hc * 512 + (tt + 1) * 128]
                        mms.append((0, lh, rw2sb[slot][:, hc * 1024:hc * 1024 + 512]))
                        mms.append((1, lh, rw2sb[slot][:, hc * 1024 + 512:(hc + 1) * 1024]))
                if b2:
                    tcols = slice(tok0 + tt * 128, tok0 + (tt + 1) * 128)
                    mms.append((0, w8T8[:, tcols], rb2bf[:, 0:512]))
                    mms.append((1, w8T8[:, tcols], rb2bf[:, 512:1024]))
                    mms.append((0, onesb[:, 0:128], sb2bf[:, 0:512]))
                    mms.append((1, onesb[:, 0:128], sb2bf[:, 512:1024]))
                lastof = {0: max(i for i, m in enumerate(mms) if m[0] == 0),
                          1: max(i for i, m in enumerate(mms) if m[0] == 1)}
                seen = set()
                for i, (pi, lh, rh) in enumerate(mms):
                    ps = po0 if pi == 0 else po1
                    mm = te.matmul(ps[:], lh, rh, start=(pi not in seen), stop=(i == lastof[pi]))
                    seen.add(pi)
                ctx.s_mm.inc(mm)
                ctx.s_mm.wait(v)
                aslot = stg2_ctr[0] % 3
                ab = acc[aslot]
                ctx.s_accr[aslot].wait(v)
                v.tensor_copy(ab[:, 0:512], po0[:])
                ctx.s_v.inc(v.tensor_copy(ab[:, 512:1024], po1[:]))
                ctx.s_v.wait(g)
                t = ch * 4 + tt
                ctx.s_accr[aslot].inc(
                    g.dma_start(out=h["x1"].ap()[t * 128:(t + 1) * 128, :], in_=ab[:],
                                accum_op=ALU.add), 16)
                stg2_ctr[0] += 1
            ctx.s_v.wait(sy)
        g.wait_ge(ctx.s_st.h, ctx.s_st.n)
        for sr in ctx.s_accr:
            sr.wait(g)
        ctx.barrier()

    for pr in range(4):
        ffn_pass(pr, [(0, 2 * pr), (1, 2 * pr + 1)], 2048,
                 h["L0_ffn_rw1"], [2 * pr * 1024, (2 * pr + 1) * 1024], h["L0_ffn_rb1"],
                 h["L0_ffn_rw2"], [2 * pr * 2048, (2 * pr + 1) * 2048], b2=False, wb=True)
    ffn_pass(4, [(0, None)], 2048, h["L0_ffn_sw1"], [0], h["L0_ffn_sb1"],
             h["L0_ffn_sw2"], [0], b2=True, wb=False)
    nc.gpsimd.wait_ge(ctx.s_st.h, ctx.s_st.n)


def phase_l1_attn(ctx, stage="all"):
    nc = ctx.nc
    v, sc, g, sy, te = nc.vector, nc.scalar, nc.gpsimd, nc.sync, nc.tensor
    h = ctx.h
    SQRT_S = float(np.sqrt(128.0))
    S2 = ctx.sbuf("S2", [128, 4224], F32)
    import contextlib
    with ExitStack() as ses:
        old_es = ctx.es
        ctx.es = ses
        wr = ctx.sbuf("wr", [1, 1023], F32)
        we = ctx.sbuf("we", [1, 4352], F32)
        st0 = ctx.sbuf("stair0", [128, 4352], F32)
        st1 = ctx.sbuf("stair1", [128, 4352], F32)
        bits = ctx.sbuf("bits_sb", [128, 7], I32)
        ctx.s_ld.inc(sy.dma_start(out=wr[:], in_=h["w_rev"].ap()), 16)
        ctx.s_ld.inc(sy.dma_start(out=bits[:], in_=h["bits"].ap()), 16)
        sy.wait_ge(ctx.s_ld.h, ctx.s_ld.n)
        ctx.barrier()
        v.tensor_copy(we[0:1, 0:1536], wr[0:1, 0:1].broadcast_to([1, 1536]))
        v.tensor_copy(we[0:1, 1536:2559], wr[0:1, :])
        v.tensor_copy(we[0:1, 2559:3582], wr[0:1, :])
        ctx.s_v.inc(v.tensor_copy(we[0:1, 3582:4352], wr[0:1, 1022:1023].broadcast_to([1, 770])))
        ctx.s_v.wait(g)
        ctx.s_st.inc(g.dma_start(out=h["wext_rev"].ap(), in_=we[0:1, :]), 16)
        g.wait_ge(ctx.s_st.h, ctx.s_st.n)
        ctx.s_cast.inc(g.dma_start(out=st0[:], in_=bass.AP(h["wext_rev"], 0, [[0, 128], [1, 4352]])), 16)
        g.wait_ge(ctx.s_cast.h, ctx.s_cast.n)
        ctx.barrier()
        N = 4352
        for k in range(7):
            sh = 1 << k
            tin, tout = (st0, st1) if k % 2 == 0 else (st1, st0)
            v.select(tout[:, sh:N], bits[:, k:k + 1].broadcast_to([128, N - sh]),
                     tin[:, 0:N - sh], tin[:, sh:N])
        v.tensor_copy(S2[:], st1[:, 0:4224])
        ctx.barrier()
        ctx.es = old_es

    qT = ctx.sbuf("qTl1", [128, T], BF16)
    kT = ctx.sbuf("kTl1", [128, T], BF16)
    onesb = ctx.sbuf("onesl1", [1, 128], BF16)
    p_a = [ctx.psum(f"p_l1a{i}", [128, 512]) for i in range(2)]
    sub_es = ExitStack()
    old_es2 = ctx.es
    ctx.es = sub_es
    pan8 = [ctx.sbuf(f"xbpan{i}", [128, T], BF16) for i in range(8)]
    cos1 = ctx.sbuf("cos1", [128, SEQ], F32)
    sin1 = ctx.sbuf("sin1", [128, SEQ], F32)
    lwcol = ctx.sbuf("lwcol", [128, 8], F32)
    lbcol = ctx.sbuf("lbcol", [128, 8], F32)
    gbcol = ctx.sbuf("gbcol", [128, 4], F32)
    gbcol2 = ctx.sbuf("gbcol2", [128, 4], F32)
    uvbcol2 = ctx.sbuf("uvbcol2", [128, 1], F32)
    uvbcol = ctx.sbuf("uvbcol", [128, 33], F32)
    wt8 = ctx.sbuf("wt8", [128, 8 * 512], BF16)
    bsil = ctx.sbuf("bsil", [128, 512], F32)
    qpre = ctx.sbuf("qpre", [128, 512], F32)
    rt1 = ctx.sbuf("rt1l1", [128, 512], F32)
    rt2 = ctx.sbuf("rt2l1", [128, 512], F32)
    uvbrow = ctx.sbuf("uvbrow", [1, 2048], BF16)
    evbf = [ctx.sbuf(f"evbf{i}", [128, 512], BF16) for i in range(2)]
    uslotv = [0, 0]

    v.memset(onesb[:], 1.0)
    ctx.s_ld.inc(sy.dma_start(out=cos1[:], in_=h["l1_cos"].ap()), 16)
    ctx.s_ld.inc(sy.dma_start(out=sin1[:], in_=h["l1_sin"].ap()), 16)
    for dc in range(8):
        ctx.s_ld.inc(sy.dma_start(out=lwcol[:, dc:dc + 1],
                                  in_=bass.AP(h["L1_attn_ln_w"], dc * 128, [[1, 128], [1, 1]])), 16)
        ctx.s_ld.inc(sy.dma_start(out=lbcol[:, dc:dc + 1],
                                  in_=bass.AP(h["L1_attn_ln_b"], dc * 128, [[1, 128], [1, 1]])), 16)
    for i in range(2):
        ctx.s_ld.inc(sy.dma_start(out=gbcol[:, 2 * i:2 * i + 1],
                                  in_=bass.AP(h["L1_attn_gamma"], i * 128, [[1, 128], [1, 1]])), 16)
        ctx.s_ld.inc(sy.dma_start(out=gbcol[:, 2 * i + 1:2 * i + 2],
                                  in_=bass.AP(h["L1_attn_beta"], i * 128, [[1, 128], [1, 1]])), 16)
        for half in range(2):
            ctx.s_ld.inc(sy.dma_start(out=gbcol2[half * 64:(half + 1) * 64, 2 * i:2 * i + 1],
                                      in_=bass.AP(h["L1_attn_gamma"], i * 128 + (1 - half) * 64,
                                                  [[1, 64], [1, 1]])), 16)
            ctx.s_ld.inc(sy.dma_start(out=gbcol2[half * 64:(half + 1) * 64, 2 * i + 1:2 * i + 2],
                                      in_=bass.AP(h["L1_attn_beta"], i * 128 + (1 - half) * 64,
                                                  [[1, 64], [1, 1]])), 16)
    for half in range(2):
        ctx.s_ld.inc(sy.dma_start(out=uvbcol2[half * 64:(half + 1) * 64, 0:1],
                                  in_=bass.AP(h["L1_attn_uv_b"], 4096 + (1 - half) * 64,
                                              [[1, 64], [1, 1]])), 16)
    for ft in range(33):
        ctx.s_ld.inc(sy.dma_start(out=uvbcol[:, ft:ft + 1],
                                  in_=bass.AP(h["L1_attn_uv_b"], ft * 128, [[1, 128], [1, 1]])), 16)
    pldv = None
    for dc in range(8):
        for p in range(8):
            pldv = ctx.s_ld.inc(sy.dma_start_transpose(
                pan8[dc][:, p * 512:(p + 1) * 512],
                h["xb_bf"].ap()[p * 512:(p + 1) * 512, dc * 128:(dc + 1) * 128]), 16)
    v.wait_ge(ctx.s_ld.h, ctx.s_ld.n)
    for dc in range(8):
        last = v.tensor_scalar(pan8[dc][:], pan8[dc][:], lwcol[:, dc:dc + 1], lbcol[:, dc:dc + 1],
                               ALU.mult, ALU.add)
    ctx.s_v.inc(last)
    ctx.barrier()

    wtb = ctx.sbuf("wtb", [128, 8 * 128], BF16)
    wtb2 = ctx.sbuf("wtb2", [128, 8 * 128], BF16)
    bsil2 = ctx.sbuf("bsil2", [128, 512], F32)
    qpre2 = ctx.sbuf("qpre2", [128, 512], F32)
    for dc in range(8):
        ctx.s_ld.inc(sy.dma_start_transpose(
            wtb[:, dc * 128:(dc + 1) * 128],
            h["st_uvw"].ap()[4096:4224, dc * 128:(dc + 1) * 128]), 16)
        ctx.s_ld.inc(sy.dma_start_transpose(
            wtb2[:, dc * 128:(dc + 1) * 128],
            h["st_uvw2"].ap()[0:128, dc * 128:(dc + 1) * 128]), 16)
    te.wait_ge(ctx.s_ld.h, ctx.s_ld.n)
    for s in range(2):
        for p in range(4):
            gp = s * 4 + p
            ps, ps2 = p_a[0], p_a[1]
            cs = cos1[:, p * 512:(p + 1) * 512]
            sn = sin1[:, p * 512:(p + 1) * 512]
            ctx.s_v.wait(te)
            for dc in range(8):
                te.matmul(ps[:], wtb[:, dc * 128:(dc + 1) * 128],
                          pan8[dc][:, gp * 512:(gp + 1) * 512], start=(dc == 0), stop=(dc == 7))
            for dc in range(8):
                mm = te.matmul(ps2[:], wtb2[:, dc * 128:(dc + 1) * 128],
                               pan8[dc][:, gp * 512:(gp + 1) * 512], start=(dc == 0), stop=(dc == 7))
            ctx.s_mm.inc(mm)
            ctx.s_mm.wait(sc)
            sc.activation(bsil[:], ps[:], AF.Silu, bias=uvbcol[:, 32:33])
            ctx.s_sc.inc(sc.activation(bsil2[:], ps2[:], AF.Silu, bias=uvbcol2[:, 0:1]))
            ctx.s_sc.wait(v)
            cols = slice(gp * 512, (gp + 1) * 512)
            for qi in range(2):
                v.tensor_scalar(qpre[:], bsil[:], gbcol[:, 2 * qi:2 * qi + 1],
                                gbcol[:, 2 * qi + 1:2 * qi + 2], ALU.mult, ALU.add)
                v.tensor_scalar(qpre2[:], bsil2[:], gbcol2[:, 2 * qi:2 * qi + 1],
                                gbcol2[:, 2 * qi + 1:2 * qi + 2], ALU.mult, ALU.add)
                dst = qT if qi == 0 else kT
                v.tensor_tensor(rt1[:], qpre[:], cs, ALU.mult)
                v.tensor_tensor(rt2[:], qpre2[:], sn, ALU.mult)
                last = v.tensor_add(dst[:, cols], rt1[:], rt2[:])
            ctx.s_v.inc(last)
    ctx.barrier()

    for fg in range(4):
        ctx.s_mm.wait(sy)
        ldv = None
        for dc in range(8):
            ldv = ctx.s_ld.inc(sy.dma_start_transpose(
                wt8[:, dc * 512:(dc + 1) * 512],
                h["st_uvw"].ap()[fg * 512:(fg + 1) * 512, dc * 128:(dc + 1) * 128]), 16)
        te.wait_ge(ctx.s_ld.h, ldv)
        for ft in range(4):
            for gp in range(8):
                ii2 = ft * 8 + gp
                sl = ii2 % 2
                ps = p_a[sl]
                if uslotv[sl]:
                    ctx.s_sc.wait(te, uslotv[sl])
                for dc in range(8):
                    mm = te.matmul(ps[:], wt8[:, dc * 512 + ft * 128:dc * 512 + (ft + 1) * 128],
                                   pan8[dc][:, gp * 512:(gp + 1) * 512], start=(dc == 0), stop=(dc == 7))
                ctx.s_mm.inc(mm)
                ctx.s_mm.wait(sc)
                ctx.s_accr[sl].wait(sc)
                uslotv[sl] = ctx.s_sc.inc(
                    sc.activation(evbf[sl][:], ps[:], AF.Silu, bias=uvbcol[:, fg * 4 + ft:fg * 4 + ft + 1]))
                ctx.s_sc.wait(g)
                ctx.s_accr[sl].inc(g.dma_start(
                    out=h["uT"].ap()[(fg * 4 + ft) * 128:(fg * 4 + ft + 1) * 128, gp * 512:(gp + 1) * 512],
                    in_=evbf[sl][:]), 16)
    for vb in range(4):
        ctx.s_mm.wait(sy)
        ldv = None
        for dc in range(8):
            ldv = ctx.s_ld.inc(sy.dma_start_transpose(
                wt8[:, dc * 512:(dc + 1) * 512],
                h["st_uvw"].ap()[2048 + vb * 512:2048 + (vb + 1) * 512, dc * 128:(dc + 1) * 128]), 16)
        ctx.s_cast.inc(g.dma_start(out=uvbrow[0:1, 0:512],
                                   in_=bass.AP(h["L1_attn_uv_b"], 2048 + vb * 512, [[1, 1], [1, 512]])), 16)
        te.wait_ge(ctx.s_ld.h, ldv)
        te.wait_ge(ctx.s_cast.h, ctx.s_cast.n)
        for tt in range(NT):
            sl = tt % 2
            ps = p_a[sl]
            if uslotv[sl]:
                ctx.s_sc.wait(te, uslotv[sl])
            for dc in range(8):
                te.matmul(ps[:], pan8[dc][:, tt * 128:(tt + 1) * 128], wt8[:, dc * 512:(dc + 1) * 512],
                          start=(dc == 0), stop=False)
            mm = te.matmul(ps[:], onesb[:, 0:128], uvbrow[0:1, 0:512], start=False, stop=True)
            ctx.s_mm.inc(mm)
            ctx.s_mm.wait(sc)
            ctx.s_accr[sl].wait(sc)
            uslotv[sl] = ctx.s_sc.inc(sc.activation(evbf[sl][:], ps[:], AF.Silu))
            ctx.s_sc.wait(g)
            ctx.s_accr[sl].inc(g.dma_start(
                out=h["v_tm"].ap()[tt * 128:(tt + 1) * 128, vb * 512:(vb + 1) * 512], in_=evbf[sl][:]), 16)
    nc.gpsimd.wait_ge(ctx.s_st.h, ctx.s_st.n)
    for sr in ctx.s_accr[:2]:
        sr.wait(g)
    ctx.barrier()
    sub_es.close()
    ctx.es = old_es2
    if stage == "s2":
        sc.copy(dbg_s2a := None, None) if False else None
        kfa = ctx.sbuf("kfa", [128, 512], F32)
        for blk in range(6):
            ctx.s_v.wait(g)
            ctx.s_v.inc(v.tensor_copy(kfa[:], S2[:, 1024 + blk * 512:1024 + (blk + 1) * 512]))
            ctx.s_v.wait(g)
            ctx.s_st.inc(g.dma_start(out=h["out"].ap()[blk * 128:(blk + 1) * 128, 0:512], in_=kfa[:]), 16)
            ctx.s_st.wait(v)
        nc.gpsimd.wait_ge(ctx.s_st.h, ctx.s_st.n)
        return
    if stage == "uv":
        return

    vtm_s = ctx.sbuf("vtm_s", [128, 16 * 2048], BF16)
    owT = ctx.sbuf("owT", [128, 16 * 1024], BF16)
    obrow = ctx.sbuf("obrow", [1, 1024], BF16)
    uts = ctx.sbuf("uts", [128, 16 * 512], BF16)
    kernT = ctx.sbuf("kernT", [128, 16 * 512], BF16)
    out2T = ctx.sbuf("out2T", [128, 16 * 512], BF16)
    kf = [ctx.sbuf(f"kf_{i}", [128, 512], F32) for i in range(2)]
    kf2 = [ctx.sbuf(f"kf2_{i}", [128, 512], F32) for i in range(2)]
    x1t2 = [ctx.sbuf(f"x1t2_{i}", [128, D], F32) for i in range(2)]
    xn1t2 = [ctx.sbuf(f"xn1t2_{i}", [128, D], F32) for i in range(2)]
    accv = ctx.sbuf("accv", [128, D], F32)
    p_pr = [ctx.psum(f"p_l1pr{i}", [128, 512]) for i in range(2)]

    for es in range(16):
        for dh2 in range(2):
            ctx.s_ld.inc(sy.dma_start_transpose(
                owT[:, es * 1024 + dh2 * 512:es * 1024 + (dh2 + 1) * 512],
                h["st_ow"].ap()[dh2 * 512:(dh2 + 1) * 512, es * 128:(es + 1) * 128]), 16)
    ctx.s_cast.inc(g.dma_start(out=obrow[:], in_=h["L1_attn_o_b"].ap()), 16)

    nseq = 1 if stage in ("sim", "att") else 2
    for s in range(nseq):
        ctx.s_mm.wait(sy)
        ldvt = None
        for kt in range(16):
            ldvt = ctx.s_ld.inc(sy.dma_start(
                out=vtm_s[:, kt * 2048:(kt + 1) * 2048],
                in_=h["v_tm"].ap()[s * 2048 + kt * 128:s * 2048 + (kt + 1) * 128, :]), 16)
        for qp in range(1 if stage in ("sim", "att") else 4):
            ldu = None
            for es in range(16):
                ldu = ctx.s_ld.inc(sy.dma_start(
                    out=uts[:, es * 512:(es + 1) * 512],
                    in_=h["uT"].ap()[es * 128:(es + 1) * 128, s * 2048 + qp * 512:s * 2048 + (qp + 1) * 512]), 16)
            te.wait_ge(ctx.s_ld.h, ldvt)
            te.wait_ge(ctx.s_ld.h, ldu)
            kslot_val = [0, 0]
            ksq_val = [0, 0]
            for kt in range(16):
                sl = kt % 2
                ps = p_a[sl]
                if kslot_val[sl]:
                    ctx.s_v.wait(te, kslot_val[sl])
                ctx.s_mm.inc(te.matmul(ps[:], kT[:, s * 2048 + kt * 128:s * 2048 + (kt + 1) * 128],
                                       qT[:, s * 2048 + qp * 512:s * 2048 + (qp + 1) * 512],
                                       start=True, stop=True))
                ctx.s_mm.wait(v)
                m0 = 2047 - kt * 128 + qp * 512
                ctx.s_sc.wait(v, ksq_val[sl]) if ksq_val[sl] else None
                v.scalar_tensor_tensor(kf[sl][:], ps[:], 1.0 / SQRT_S, S2[:, m0:m0 + 512],
                                       ALU.mult, ALU.add)
                kslot_val[sl] = ctx.s_v.inc(v.tensor_scalar_max(kf2[sl][:], kf[sl][:], 0.0))
                ctx.s_v.wait(sc)
                ksq_val[sl] = ctx.s_sc.inc(
                    sc.activation(kernT[:, kt * 512:(kt + 1) * 512], kf2[sl][:], AF.Square))
            ctx.s_sc.wait(te)
            if stage == "sim":
                ctx.s_sc.wait(v)
                ctx.s_v.inc(v.tensor_copy(kf[:], kernT[:, 0:512]))
                ctx.s_v.inc(v.tensor_copy(kf2[:], kernT[:, 512:1024]))
                ctx.s_v.wait(g)
                ctx.s_st.inc(g.dma_start(out=h["out"].ap()[0:128, 0:512], in_=kf[:]), 16)
                ctx.s_st.inc(g.dma_start(out=h["out"].ap()[0:128, 512:1024], in_=kf2[:]), 16)
                nc.gpsimd.wait_ge(ctx.s_st.h, ctx.s_st.n)
                return
            aslot_val = [0, 0]
            for es in range(16):
                sl = es % 2
                ps = p_a[sl]
                if es < 2:
                    ctx.s_v.wait(te)
                elif aslot_val[sl]:
                    ctx.s_v.wait(te, aslot_val[sl])
                for kt in range(16):
                    mm = te.matmul(ps[:], vtm_s[:, kt * 2048 + es * 128:kt * 2048 + (es + 1) * 128],
                                   kernT[:, kt * 512:(kt + 1) * 512], start=(kt == 0), stop=(kt == 15))
                ctx.s_mm.inc(mm)
                ctx.s_mm.wait(v)
                aslot_val[sl] = ctx.s_v.inc(v.tensor_tensor(out2T[:, es * 512:(es + 1) * 512], ps[:],
                                                            uts[:, es * 512:(es + 1) * 512], ALU.mult))
            ctx.s_v.wait(te)
            if stage == "att":
                ctx.s_v.wait(g)
                for i4 in range(2):
                    ctx.s_st.inc(g.dma_start(out=h["out"].ap()[128 + i4 * 128:256 + i4 * 128, 0:1024],
                                             in_=out2T[:, i4 * 1024:(i4 + 1) * 1024]), 16)
                nc.gpsimd.wait_ge(ctx.s_st.h, ctx.s_st.n)
                return
            for qs in range(4):
                tglob = s * 16 + qp * 4 + qs
                buf = qs % 2
                ctx.s_v.wait(sy)
                l1 = ctx.s_ld.inc(sy.dma_start(out=x1t2[buf][:],
                                               in_=h["x1"].ap()[tglob * 128:(tglob + 1) * 128, :]), 16)
                l2 = ctx.s_ld.inc(sy.dma_start(out=xn1t2[buf][:],
                                               in_=h["xn1_tm"].ap()[tglob * 128:(tglob + 1) * 128, :]), 16)
                mms = []
                for es in range(16):
                    lh = out2T[:, es * 512 + qs * 128:es * 512 + (qs + 1) * 128]
                    mms.append((0, lh, owT[:, es * 1024:es * 1024 + 512]))
                    mms.append((1, lh, owT[:, es * 1024 + 512:(es + 1) * 1024]))
                mms.append((0, onesb[:, 0:128], obrow[0:1, 0:512]))
                mms.append((1, onesb[:, 0:128], obrow[0:1, 512:1024]))
                te.wait_ge(ctx.s_cast.h, ctx.s_cast.n)
                ctx.s_v.wait(te)
                seen = set()
                lastof = {0: max(i for i, m in enumerate(mms) if m[0] == 0),
                          1: max(i for i, m in enumerate(mms) if m[0] == 1)}
                for i, (pi, lh, rh) in enumerate(mms):
                    ps = p_pr[pi]
                    mm = te.matmul(ps[:], lh, rh, start=(pi not in seen), stop=(i == lastof[pi]))
                    seen.add(pi)
                ctx.s_mm.inc(mm)
                v.wait_ge(ctx.s_ld.h, l2)
                ctx.s_mm.wait(v)
                v.tensor_add(accv[:], x1t2[buf][:], xn1t2[buf][:])
                v.tensor_add(accv[:, 0:512], accv[:, 0:512], p_pr[0][:])
                ctx.s_v.inc(v.tensor_add(accv[:, 512:1024], accv[:, 512:1024], p_pr[1][:]))
                ctx.s_v.wait(g)
                ctx.s_st.inc(g.dma_start(out=h["out"].ap()[tglob * 128:(tglob + 1) * 128, :],
                                         in_=accv[:]), 16)
                ctx.s_st.wait(v)
            ctx.s_v.wait(sy)
    nc.gpsimd.wait_ge(ctx.s_st.h, ctx.s_st.n)


def phase_l1_ffn(ctx):
    nc = ctx.nc
    v, sc, g, sy, te = nc.vector, nc.scalar, nc.gpsimd, nc.sync, nc.tensor
    h = ctx.h
    identf = ctx.sbuf("identf1", [128, 128], F32)
    pan = [ctx.sbuf(f"f1pan{i}", [128, 1024], BF16) for i in range(8)]
    gwT = ctx.sbuf("gwT", [128, 8 * 32], BF16)
    gbb = ctx.sbuf("gbb", [128, 32], F32)
    lgch = ctx.sbuf("lgch", [128, 128], F32)
    sc_ = ctx.sbuf("sc_", [128, 128], F32)
    scb = ctx.sbuf("scb", [128, 128], F32)
    mwrk = ctx.sbuf("mwrk", [128, 128], F32)
    tcur = ctx.sbuf("tcur", [128, 128], F32)
    gstat = ctx.sbuf("gstat", [128, 64], F32)
    gstat2 = ctx.sbuf("gstat2", [128, 16], F32)
    mx8 = ctx.sbuf("mx8", [128, 32], F32)
    w32 = ctx.sbuf("w32ch", [128, 128], F32)
    w32T = ctx.sbuf("w32T", [32, T], BF16)
    oh32 = ctx.sbuf("oh32", [32, 32 * 128], BF16)
    onesb = ctx.sbuf("onesf1", [1, 128], BF16)
    p_wb = ctx.psum("p_wbl1", [128, 512])
    p_g = p_wb
    p_t32 = p_wb
    p_h = [ctx.psum(f"p_l1h{i}", [128, 512]) for i in range(2)]
    p_u = [ctx.psum(f"p_l1u{i}", [128, 512]) for i in range(2)]
    p_o = [ctx.psum(f"p_l1o{i}", [128, 512]) for i in range(2)]

    v.memset(onesb[:], 1.0)
    ctx.s_ld.inc(sy.dma_start(out=identf[:], in_=h["ident"].ap()), 16)
    ctx.s_ld.inc(sy.dma_start(out=gbb[:], in_=bass.AP(h["L1_ffn_gate_bias"], 0, [[0, 128], [1, 32]])), 16)
    for dc in range(8):
        ctx.s_ld.inc(sy.dma_start_transpose(
            gwT[:, dc * 32:(dc + 1) * 32],
            h["st_gw"].ap()[0:32, dc * 128:(dc + 1) * 128]), 16)
    with ExitStack() as ioh_es:
        ioh = ioh_es.enter_context(nc.sbuf_tensor("ioh32", [32, 32], I32))
        nc.gpsimd.iota(ioh[:], [[1, 32]], base=0, channel_multiplier=-1)
        ctx.barrier()
        ctx.s_v.inc(v.tensor_single_scalar(oh32.rearrange("p (b j) -> p b j", b=32),
                                           ioh[:].unsqueeze(2).broadcast_to([32, 32, 128]),
                                           0, ALU.is_equal))
        ctx.barrier()

    inv28 = 1.0
    BIGN = 1e30
    for ch in range(8):
        tok0 = ch * 512
        ctx.s_mm.wait(sy)
        ldv = None
        for dc in range(8):
            ldv = ctx.s_ld.inc(sy.dma_start_transpose(
                pan[dc][:, 0:512], h["xn1_bf"].ap()[tok0:tok0 + 512, dc * 128:(dc + 1) * 128]), 16)
        te.wait_ge(ctx.s_ld.h, ldv)
        for tt in range(4):
            ctx.s_v.wait(te)
            for dc in range(8):
                mm = te.matmul(p_g[:, 0:32], pan[dc][:, tt * 128:(tt + 1) * 128], gwT[:, dc * 32:(dc + 1) * 32],
                               start=(dc == 0), stop=(dc == 7))
            ctx.s_mm.inc(mm)
            ctx.s_mm.wait(v)
            ctx.s_v.inc(v.tensor_copy(lgch[:, tt * 32:(tt + 1) * 32], p_g[:, 0:32]))
        ctx.s_v.wait(sc)
        ctx.s_sc.inc(sc.activation(sc_[:], lgch[:], AF.Sigmoid))
        ctx.s_sc.wait(v)

        def vs(instr):
            ctx.s_v.inc(instr)
            ctx.s_v.wait(v)

        gbbv = bass.AP(gbb.handle if hasattr(gbb, "handle") else gbb, 0, [[32, 128], [0, 4], [1, 32]])
        vs(v.tensor_tensor(scb.rearrange("p (t e) -> p t e", t=4), sc_.rearrange("p (t e) -> p t e", t=4),
                           gbbv, ALU.add))
        vw4 = scb.rearrange("p (t g e) -> p t g e", t=4, g=4)
        vs(v.reduce_max(gstat[:, 0:16].rearrange("p (t g) -> p t g", t=4), vw4, AX.X))
        m1bc = gstat[:, 0:16].rearrange("p (t g) -> p t g", t=4).unsqueeze(3).broadcast_to([128, 4, 4, 8])
        vs(v.tensor_tensor(tcur.rearrange("p (t g e) -> p t g e", t=4, g=4), vw4, m1bc, ALU.is_ge))
        vs(v.scalar_tensor_tensor(mwrk[:], tcur[:], -BIGN, scb[:], ALU.mult, ALU.add))
        vs(v.reduce_max(gstat[:, 16:32].rearrange("p (t g) -> p t g", t=4),
                        mwrk.rearrange("p (t g e) -> p t g e", t=4, g=4), AX.X))
        vs(v.tensor_add(gstat[:, 32:48], gstat[:, 0:16], gstat[:, 16:32]))
        gsv = gstat[:, 32:48].rearrange("p (t g) -> p t g", t=4)
        vs(v.reduce_max(gstat2[:, 0:4].rearrange("p (t x) -> p t x", t=4), gsv, AX.X))
        m1Gbc = gstat2[:, 0:4].rearrange("p (t x) -> p t x", t=4).broadcast_to([128, 4, 4])
        vs(v.tensor_tensor(gstat[:, 0:16].rearrange("p (t g) -> p t g", t=4), gsv, m1Gbc, ALU.is_ge))
        vs(v.scalar_tensor_tensor(gstat[:, 48:64], gstat[:, 0:16], -BIGN, gstat[:, 32:48],
                                  ALU.mult, ALU.add))
        vs(v.reduce_max(gstat2[:, 4:8].rearrange("p (t x) -> p t x", t=4),
                        gstat[:, 48:64].rearrange("p (t g) -> p t g", t=4), AX.X))
        m2Gbc = gstat2[:, 4:8].rearrange("p (t x) -> p t x", t=4).broadcast_to([128, 4, 4])
        vs(v.tensor_tensor(gstat[:, 0:16].rearrange("p (t g) -> p t g", t=4), gsv, m2Gbc, ALU.is_ge))
        gmaskbc = gstat[:, 0:16].rearrange("p (t g) -> p t g", t=4).unsqueeze(3).broadcast_to([128, 4, 4, 8])
        vs(v.tensor_tensor(mwrk.rearrange("p (t g e) -> p t g e", t=4, g=4), vw4, gmaskbc, ALU.mult))
        vs(v.tensor_copy(tcur[:], mwrk[:]))
        for it in range(8):
            vs(v.reduce_max(mx8[:, it * 4:(it + 1) * 4].rearrange("p (t x) -> p t x", t=4),
                            tcur.rearrange("p (t e) -> p t e", t=4), AX.X))
            if it < 7:
                mxbc = mx8[:, it * 4:(it + 1) * 4].rearrange("p (t x) -> p t x", t=4).broadcast_to([128, 4, 32])
                vs(v.tensor_tensor(w32.rearrange("p (t e) -> p t e", t=4),
                                   tcur.rearrange("p (t e) -> p t e", t=4), mxbc, ALU.is_ge))
                vs(v.scalar_tensor_tensor(scb[:], w32[:], -BIGN, tcur[:], ALU.mult, ALU.add))
                vs(v.tensor_copy(tcur[:], scb[:]))
        t8bc = mx8[:, 28:32].rearrange("p (t x) -> p t x", t=4).broadcast_to([128, 4, 32])
        vs(v.tensor_tensor(w32.rearrange("p (t e) -> p t e", t=4),
                           mwrk.rearrange("p (t e) -> p t e", t=4), t8bc, ALU.is_ge))
        vs(v.tensor_mul(tcur[:], w32[:], sc_[:]))
        vs(v.reduce_sum(gstat2[:, 8:12].rearrange("p (t x) -> p t x", t=4),
                        tcur.rearrange("p (t e) -> p t e", t=4), AX.X))
        vs(v.tensor_scalar_add(gstat2[:, 0:4], gstat2[:, 8:12], 1e-20))
        vs(v.reciprocal(gstat2[:, 12:16], gstat2[:, 0:4]))
        rbc = gstat2[:, 12:16].rearrange("p (t x) -> p t x", t=4).broadcast_to([128, 4, 32])
        ctx.s_v.inc(v.scalar_tensor_tensor(w32.rearrange("p (t e) -> p t e", t=4),
                                           tcur.rearrange("p (t e) -> p t e", t=4), 2.5, rbc,
                                           ALU.mult, ALU.mult))
        for tt in range(4):
            t = ch * 4 + tt
            ctx.s_v.wait(te)
            ctx.s_mm.inc(te.transpose(p_t32[0:32, 0:128], w32[:, tt * 32:(tt + 1) * 32], identf[:]))
            ctx.s_mm.wait(v)
            ctx.s_v.inc(v.tensor_copy(w32T[:, t * 128:(t + 1) * 128], p_t32[0:32, 0:128]))
    ctx.barrier()

    egT = [ctx.sbuf(f"egT{i}", [128, 4096], BF16) for i in range(4)]
    euT = [ctx.sbuf(f"euT{i}", [128, 4096], BF16) for i in range(4)]
    edT = [ctx.sbuf(f"edT{i}", [128, 4096], BF16) for i in range(4)]
    hsb = [ctx.sbuf(f"l1h{i}", [128, 4096], BF16) for i in range(4)]

    def _sl(bufs, slot, idx, width):
        t = bufs[slot + idx // 4096]
        c = idx % 4096
        return t[:, c:c + width]
    wbsb = ctx.sbuf("wbsbl1", [128, 512], F32)
    sg1 = [ctx.sbuf(f"sg1_{i}", [128, 512], F32) for i in range(2)]
    ug1 = [ctx.sbuf(f"ug1_{i}", [128, 512], F32) for i in range(2)]
    acc1 = [ctx.sbuf(f"accl1{i}", [128, D], F32) for i in range(4)]

    def grp_pass(experts, I, w1g, w1u, w2, wb=True):
        nit = I // 128
        stg2c = [0]
        for ei, (slot, e) in enumerate(experts):
            r1 = (e or 0) * I
            for dc in range(8):
                ctx.s_ld.inc(sy.dma_start_transpose(
                    _sl(egT, slot, dc * I, I),
                    w1g.ap()[r1:r1 + I, dc * 128:(dc + 1) * 128]), 16)
                ctx.s_ld.inc(sy.dma_start_transpose(
                    _sl(euT, slot, dc * I, I),
                    w1u.ap()[r1:r1 + I, dc * 128:(dc + 1) * 128]), 16)
            r2 = (e or 0) * 1024
            for ic in range(nit):
                for dh2 in range(2):
                    ctx.s_ld.inc(sy.dma_start_transpose(
                        _sl(edT, slot, ic * 1024 + dh2 * 512, 512),
                        w2.ap()[r2 + dh2 * 512:r2 + (dh2 + 1) * 512, ic * 128:(ic + 1) * 128]), 16)
        sy.wait_ge(ctx.s_ld.h, ctx.s_ld.n)
        ctx.barrier()
        for ch in range(4):
            tok0 = ch * 1024
            ctx.s_mm.wait(sy)
            ldv = None
            for dc in range(8):
                ldv = ctx.s_ld.inc(sy.dma_start_transpose(
                    pan[dc][:], h["xn1_bf"].ap()[tok0:tok0 + 1024, dc * 128:(dc + 1) * 128]), 16)
            te.wait_ge(ctx.s_ld.h, ldv)
            v.wait_ge(ctx.s_ld.h, ldv)
            ctx.s_mm.wait(v)
            ctx.s_v.wait(te)
            pslot_val = [0, 0]
            ii = 0
            for ei, (slot, e) in enumerate(experts):
                for half in range(2):
                    if wb:
                        ctx.s_v.wait(te)
                        ctx.s_mm.inc(te.matmul(p_wb[:], oh32[:, e * 128:(e + 1) * 128],
                                               w32T[:, tok0 + half * 512:tok0 + (half + 1) * 512],
                                               start=True, stop=True))
                        ctx.s_mm.wait(v)
                        ctx.s_v.inc(v.tensor_copy(wbsb[:], p_wb[:]))
                    for it in range(nit):
                        sl = ii % 2
                        psg, psu = p_h[sl], p_u[sl]
                        if pslot_val[sl]:
                            ctx.s_v.wait(te, pslot_val[sl])
                        for dc in range(8):
                            te.matmul(psg[:], _sl(egT, slot, dc * I + it * 128, 128),
                                      pan[dc][:, half * 512:(half + 1) * 512],
                                      start=(dc == 0), stop=(dc == 7))
                        for dc in range(8):
                            mm = te.matmul(psu[:], _sl(euT, slot, dc * I + it * 128, 128),
                                           pan[dc][:, half * 512:(half + 1) * 512],
                                           start=(dc == 0), stop=(dc == 7))
                        ctx.s_mm.inc(mm)
                        ctx.s_mm.wait(sc)
                        ctx.s_mm.wait(v)
                        ctx.s_sc.inc(sc.activation(sg1[sl][:], psg[:], AF.Silu))
                        ctx.s_sc.wait(v)
                        if wb:
                            v.tensor_tensor(ug1[sl][:], psu[:], wbsb[:], ALU.mult)
                        else:
                            v.tensor_copy(ug1[sl][:], psu[:])
                        pslot_val[sl] = ctx.s_v.inc(v.tensor_tensor(
                            _sl(hsb, slot, it * 1024 + half * 512, 512),
                            sg1[sl][:], ug1[sl][:], ALU.mult))
                        ii += 1
            ctx.s_v.wait(te)
            for tt in range(8):
                pslot = tt % 2
                po0, po1 = p_o[pslot], p_h[pslot]
                if tt >= 2:
                    ctx.s_v.wait(te)
                mms = []
                for ei, (slot, e) in enumerate(experts):
                    for ic in range(nit):
                        lh = _sl(hsb, slot, ic * 1024 + tt * 128, 128)
                        mms.append((0, lh, _sl(edT, slot, ic * 1024, 512)))
                        mms.append((1, lh, _sl(edT, slot, ic * 1024 + 512, 512)))
                seen = set()
                lastof = {0: max(i for i, m in enumerate(mms) if m[0] == 0),
                          1: max(i for i, m in enumerate(mms) if m[0] == 1)}
                for i, (pi, lh, rh) in enumerate(mms):
                    ps = po0 if pi == 0 else po1
                    mm = te.matmul(ps[:], lh, rh, start=(pi not in seen), stop=(i == lastof[pi]))
                    seen.add(pi)
                ctx.s_mm.inc(mm)
                ctx.s_mm.wait(v)
                aslot = stg2c[0] % 4
                ab = acc1[aslot]
                ctx.s_accr[aslot].wait(v)
                v.tensor_copy(ab[:, 0:512], po0[:])
                ctx.s_v.inc(v.tensor_copy(ab[:, 512:1024], po1[:]))
                ctx.s_v.wait(g)
                tg = ch * 8 + tt
                ctx.s_accr[aslot].inc(
                    g.dma_start(out=h["out"].ap()[tg * 128:(tg + 1) * 128, :], in_=ab[:],
                                accum_op=ALU.add), 16)
                stg2c[0] += 1
            ctx.s_v.wait(sy)
        g.wait_ge(ctx.s_st.h, ctx.s_st.n)
        for sr in ctx.s_accr:
            sr.wait(g)
        ctx.barrier()

    for gi in range(8):
        grp_pass([(i, gi * 4 + i) for i in range(4)], 512,
                 h["st_eg"], h["st_eu"], h["st_ed"], wb=True)
    grp_pass([(0, None)], 1024, h["st_shg"], h["st_shu"], h["st_shd"], wb=False)
    nc.gpsimd.wait_ge(ctx.s_st.h, ctx.s_st.n)


def build(dbg_taps=(), upto=None, dbg=False):
    nc = bass.Bass("TRN2", target_bir_lowering=False)
    with ExitStack() as es:
        ctx = Ctx(nc, es)
        ctx.dbg = dbg if isinstance(dbg, (set, frozenset)) else ({"attn0", "ffn0"} if dbg else None)
        declare_tensors(nc, ctx, dbg_taps)

        def phase(fn, *a, **k):
            with ExitStack() as pes:
                old = ctx.es
                ctx.es = pes
                fn(ctx, *a, **k)
                ctx.es = old
            ctx.barrier()

        phase(phase_stage)
        phase(phase_ln, ctx.h["x"].ap(), ctx.h["xn0_bf"], None, ctx.h["xa_bf"], tag="0")
        if upto == "ln0":
            return nc
        phase(phase_l0_attn)
        if upto == "l0attn":
            return nc
        phase(phase_l0_ffn)
        if upto == "l0":
            return nc
        phase(phase_ln, ctx.h["x1"].ap(), ctx.h["xn1_bf"], ctx.h["xn1_tm"], ctx.h["xb_bf"], tag="1")
        if upto == "ln1":
            return nc
        if upto == "l1uv":
            phase(phase_l1_attn, stage="uv")
            return nc
        if upto in ("l1sim", "l1att", "l1s2"):
            phase(phase_l1_attn, stage=upto[2:])
            return nc
        phase(phase_l1_attn)
        if upto == "l1attn":
            return nc
        phase(phase_l1_ffn)
    return nc


def make_in_map(core, inputs):
    c = _host_consts()
    m = {}
    x = np.asarray(inputs["L0_attn_x"])
    m["x"] = np.ascontiguousarray(x[2 * core:2 * core + 2].reshape(T, D), dtype=np.float32)
    for nm in WEIGHT_NAMES:
        a = np.asarray(inputs[nm], dtype=np.float32)
        shp = INPUT_SHAPES[nm]
        if len(shp) == 3:
            a = a.reshape(shp[0] * shp[1], shp[2])
        elif len(shp) == 1:
            a = a.reshape(1, shp[0])
        m[nm] = np.ascontiguousarray(a)
    m["w_rev"] = np.ascontiguousarray(np.asarray(inputs["L1_attn_w"], dtype=np.float32)[::-1].reshape(1, 1023))
    for nm in CONST_SHAPES:
        m[nm] = c[nm]
    m["bits"] = c["bits"]
    return m


_NC_CACHE = {}


def kernel(**inputs):
    key = "full"
    if key not in _NC_CACHE:
        _NC_CACHE[key] = build()
    nc = _NC_CACHE[key]
    in_maps = [make_in_map(cc, inputs) for cc in range(8)]
    res = run_bass_kernel_spmd(nc, in_maps, core_ids=list(range(8)))
    out = np.stack([r["out"] for r in res.results])
    return out.reshape(16, SEQ, D).astype(np.float32)
